# version 12
# baseline (speedup 1.0000x reference)
import numpy as np
import concourse.bass as bass
import concourse.mybir as mybir
from concourse.bass_utils import run_bass_kernel_spmd
from concourse.bass import IndirectOffsetOnAxis
from contextlib import ExitStack

F32 = mybir.dt.float32
BF16 = mybir.dt.bfloat16
I32 = mybir.dt.int32
U32 = mybir.dt.uint32
AF = mybir.ActivationFunctionType
ALU = mybir.AluOpType

L = 4096
TILE = 512
GRP = 128
D = 1024
NDC = 8
D3 = 3072
NOC = 24
DI = 2048
NIC = 16
H = 8
NE = 16384
EPS = 1e-5


class Ctr:
    def __init__(self, nc):
        self.nc = nc
        self.sems = {}
        self.cnt = {}
        self.marks = {}

    def sem(self, name):
        s = self.nc.semaphore(name).__enter__()
        self.sems[name] = s
        self.cnt[name] = 0

    def inc(self, instr, name, by=1):
        self.cnt[name] += by
        instr.then_inc(self.sems[name], by)
        return self.cnt[name]

    def wait(self, eng, ename, name, val):
        if val <= 0:
            return
        key = (ename, name)
        if self.marks.get(key, -1) >= val:
            return
        eng.wait_ge(self.sems[name], val)
        self.marks[key] = val


def build_nc(nt=8, debug=False):
    nc = bass.Bass("TRN2", target_bir_lowering=False, debug=False,
                   detect_race_conditions=False)
    NG = (nt * TILE) // GRP

    u_d = nc.dram_tensor("u", [D, L], F32, kind="ExternalInput").ap()
    comb_d = nc.dram_tensor("combt", [NE, 6144], F32, kind="ExternalInput").ap()
    wi_d = nc.dram_tensor("wi", [D, D3], F32, kind="ExternalInput").ap()
    wo_d = nc.dram_tensor("wo", [D, D], F32, kind="ExternalInput").ap()
    wq_d = nc.dram_tensor("wq", [8192, D], F32, kind="ExternalInput").ap()
    kt_d = nc.dram_tensor("kt", [16, 512, 128], F32, kind="ExternalInput").ap()
    wif_d = nc.dram_tensor("wif", [D, DI], F32, kind="ExternalInput").ap()
    wof_d = nc.dram_tensor("wof", [DI, D], F32, kind="ExternalInput").ap()
    bin_d = nc.dram_tensor("binp", [128, NOC], F32, kind="ExternalInput").ap()
    bout_d = nc.dram_tensor("boutp", [128, NDC], F32, kind="ExternalInput").ap()
    sb_d = nc.dram_tensor("sbp", [128, NOC], F32, kind="ExternalInput").ap()
    ib_d = nc.dram_tensor("ibp", [128, NDC], F32, kind="ExternalInput").ap()
    sw_d = nc.dram_tensor("swp", [128, NOC * 3], F32, kind="ExternalInput").ap()
    iw_d = nc.dram_tensor("iwp", [128, NDC * 4], F32, kind="ExternalInput").ap()
    eye_d = nc.inline_tensor(np.eye(128, dtype=np.float32), "eye128").ap()
    comb16 = nc.dram_tensor("comb16sh", [NE, 6144], BF16, kind="Internal",
                            addr_space="Shared").ap()
    shf = nc.dram_tensor("shflag", [2, 128], I32, kind="Internal",
                         addr_space="Shared").ap()

    y_d = nc.dram_tensor("yout", [L, D], F32, kind="ExternalOutput").ap()
    ik = "ExternalOutput" if debug else "Internal"
    r1_d = nc.dram_tensor("r1d", [NDC, 128, L], F32, kind=ik).ap()
    x2_d = nc.dram_tensor("x2d", [NDC, 128, L], BF16, kind=ik).ap()
    if debug:
        idx_d = nc.dram_tensor("idxd", [128, 256], I32, kind="ExternalOutput").ap()
        gate_d = nc.dram_tensor("gated", [128, 256], F32, kind="ExternalOutput").ap()

    ctx = ExitStack()
    e = ctx.enter_context
    C = Ctr(nc)
    for s in ("qC", "qU", "qSPR", "qSPX", "qX", "qR", "qY", "qWA", "qWQ",
              "qWF", "qGI", "qGO", "qWC", "qPF", "sV", "sA", "sP", "sT"):
        C.sem(s)
    SY, V, A, P, T, G = nc.sync, nc.vector, nc.scalar, nc.gpsimd, nc.tensor, nc.gpsimd

    def vop(instr):
        return C.inc(instr, "sV")

    def vwait_self():
        C.wait(V, "V", "sV", C.cnt["sV"])

    ones16 = e(nc.sbuf_tensor("ones16", [128, 1], BF16))
    ones32 = e(nc.sbuf_tensor("ones32", [1, 128], F32))
    eyeF = e(nc.sbuf_tensor("eyeF", [128, 128], F32))
    eye16 = e(nc.sbuf_tensor("eye16", [128, 128], BF16))
    swp = e(nc.sbuf_tensor("swp_s", [128, NOC * 3], F32))
    iwp = e(nc.sbuf_tensor("iwp_s", [128, NDC * 4], F32))
    binp = e(nc.sbuf_tensor("binp_s", [128, NOC], F32))
    boutp = e(nc.sbuf_tensor("boutp_s", [128, NDC], F32))
    sbp = e(nc.sbuf_tensor("sbp_s", [128, NOC], F32))
    ibp = e(nc.sbuf_tensor("ibp_s", [128, NDC], F32))
    cst = e(nc.sbuf_tensor("cst_ln", [128, 2], F32))
    flz = e(nc.sbuf_tensor("flz", [1, 128], I32))
    flone = e(nc.sbuf_tensor("flone", [1, 128], I32))
    flrd = e(nc.sbuf_tensor("flrd", [1, 4], I32))
    idx_all = e(nc.sbuf_tensor("idx_all", [128, 256], I32))
    gate_all = e(nc.sbuf_tensor("gate_all", [128, 256], F32))

    ps_ctx = ExitStack()
    pe = ps_ctx.enter_context
    psA = pe(nc.psum_tensor("psA", [128, 1024], F32))
    psB = pe(nc.psum_tensor("psB", [128, 1024], F32))
    psL = pe(nc.psum_tensor("psL", [128, 1024], F32))
    psS = pe(nc.psum_tensor("psS", [128, 1024], F32))
    slots = [psA, psB]

    p1_ctx = ExitStack()
    p1 = p1_ctx.enter_context
    m0a = p1(nc.sbuf_tensor("m0a", [128, 64], F32))
    m1a = p1(nc.sbuf_tensor("m1a", [128, 64], F32))
    i0a = p1(nc.sbuf_tensor("i0a", [128, 64], U32))
    i1a = p1(nc.sbuf_tensor("i1a", [128, 64], U32))
    sc8 = p1(nc.sbuf_tensor("sc8", [128, 24], F32))
    rowb = p1(nc.sbuf_tensor("rowb", [1, 1024], F32))
    rowt = p1(nc.sbuf_tensor("rowt", [1, 512], F32))
    lnb = p1(nc.sbuf_tensor("lnb", [128, 1024], F32))
    wi16 = [p1(nc.sbuf_tensor(f"wi16_{k}", [128, D3], BF16)) for k in range(NDC)]
    wo16 = [p1(nc.sbuf_tensor(f"wo16_{k}", [128, D], BF16)) for k in range(NDC)]
    keff = [p1(nc.sbuf_tensor(f"keff_{k}", [128, 2048], BF16)) for k in range(NDC)]
    wqb = [p1(nc.sbuf_tensor(f"wqb_{k}", [128, D], BF16)) for k in range(4)]
    ktb = [p1(nc.sbuf_tensor(f"ktb_{k}", [128, 512], BF16)) for k in range(2)]
    dgA = [p1(nc.sbuf_tensor(f"dgA_{k}", [128, 128], BF16)) for k in range(12)]
    dgB = [p1(nc.sbuf_tensor(f"dgB_{k}", [128, 128], BF16)) for k in range(8)]
    ut = p1(nc.sbuf_tensor("ut", [128, 8 * 512], F32))
    u16sq = p1(nc.sbuf_tensor("u16sq", [128, 4096], BF16))
    xa = p1(nc.sbuf_tensor("xa", [128, 8 * 512], BF16))
    x2 = xa
    x1pair = p1(nc.sbuf_tensor("x1pair", [128, 1028], BF16))
    x1pairB = p1(nc.sbuf_tensor("x1pairB", [128, 1028], BF16))
    x1h = p1(nc.sbuf_tensor("x1h", [128, 48], BF16))
    convpair = p1(nc.sbuf_tensor("convpair", [128, 1024], F32))
    kvt = p1(nc.sbuf_tensor("kvt", [128, 515], BF16))
    kvtB = p1(nc.sbuf_tensor("kvtB", [128, 515], BF16))
    kvh = p1(nc.sbuf_tensor("kvh", [128, NDC * 3], BF16))
    vvt = p1(nc.sbuf_tensor("vvt", [128, 8 * 512], BF16))
    y16 = p1(nc.sbuf_tensor("y16", [128, 8 * 512], BF16))
    r1sb = p1(nc.sbuf_tensor("r1sb", [128, 8 * 512], F32))

    for dst, srcd in ((swp, sw_d), (iwp, iw_d), (eyeF, eye_d), (binp, bin_d),
                      (boutp, bout_d), (sbp, sb_d), (ibp, ib_d)):
        C.inc(SY.dma_start(out=dst[:, :], in_=srcd[:, :]), "qC", 16)
    qs_consts = C.cnt["qC"]

    for k in range(NDC):
        C.inc(G.dma_start(out=wi16[k][:, :], in_=wi_d[k * 128:(k + 1) * 128, :]), "qWA", 16)
    for k in range(NDC):
        C.inc(G.dma_start(out=wo16[k][:, :], in_=wo_d[k * 128:(k + 1) * 128, :]), "qWA", 16)
    qg_attnw = C.cnt["qWA"]
    C.inc(G.dma_start(out=ktb[0].ap().rearrange("p (c k) -> p c k", c=4),
                      in_=kt_d[0:1, :, :].rearrange("s (c p) k -> p (s c) k", p=128)), "qWQ", 16)
    for dkc in range(4):
        C.inc(G.dma_start(out=wqb[dkc][:, :],
                          in_=wq_d[dkc * 128:dkc * 128 + 128, :]), "qWQ", 16)

    C.wait(V, "V", "qC", qs_consts)
    vop(V.memset(cst[:, 0:1], EPS))
    vop(V.memset(cst[:, 1:2], -0.5))
    nc.const_aps.register(F32, EPS, cst.ap()[:, 0:1])
    nc.const_aps.register(F32, -0.5, cst.ap()[:, 1:2])
    vop(V.memset(ones16[:, :], 1.0))
    vop(V.memset(ones32[:, :], 1.0))
    vop(V.tensor_copy(eye16[:, :], eyeF[:, :]))
    vop(V.memset(idx_all[:, :], 0))
    vop(V.memset(gate_all[:, :], 0.0))
    vop(V.memset(x1h[:, :], 0.0))
    vop(V.memset(kvt[:, :], 0.0))
    vop(V.memset(kvtB[:, :], 0.0))
    vop(V.memset(kvh[:, :], 0.0))
    vop(V.memset(flz[:, :], 0))
    vop(V.memset(flone[:, :], 1))
    sv_p0 = C.cnt["sV"]

    pid1 = nc.gpsimd.alloc_register("pid1")
    pv = G.partition_id()
    G.reg_alu(pid1, pv, 1, ALU.bitwise_and)
    C.wait(G, "P", "sV", sv_p0)
    with G.If_eq(pid1, 0):
        G.dma_start(out=shf[0:1, :], in_=flz[0:1, :]).then_inc(C.sems["qWC"], 16)
    with G.Else():
        G.dma_start(out=shf[1:2, :], in_=flz[0:1, :]).then_inc(C.sems["qWC"], 16)
    C.cnt["qWC"] += 16
    qwc_zero = C.cnt["qWC"]

    NCV = 16

    def conv_chunk(cc):
        r0e = cc * 512
        r0o = 8192 + cc * 512
        with G.If_eq(pid1, 0):
            G.dma_start(out=comb16[r0e:r0e + 512, :],
                        in_=comb_d[r0e:r0e + 512, :]).then_inc(C.sems["qWC"], 16)
        with G.Else():
            G.dma_start(out=comb16[r0o:r0o + 512, :],
                        in_=comb_d[r0o:r0o + 512, :]).then_inc(C.sems["qWC"], 16)
        C.cnt["qWC"] += 16

    u_rearr = u_d.rearrange("(c p) t -> p c t", p=128)
    ut3 = ut.ap().rearrange("p (c t) -> p c t", c=NDC)
    C.inc(SY.dma_start(out=ut3, in_=u_rearr[:, :, 0:TILE]), "qU", 16)
    qs_u = {0: C.cnt["qU"]}

    gd = dict(
        st_stats_prev=0, sv_rows=0, sa_lnb=0, sv_apply=0,
        sv_argS=0, sv_argL=0,
        st_x2_read=0, st_xa_read=0, qs_x2dma=0, qs_r1dma=0,
        sv_haloout=0, sv_kvh=0,
        st_conv3=0, st_conv4=0, sp_y_last=0,
        sa_vv_last=0, sp_convpair=0, sp_kv=0,
        st_wout_last=0, sv_ln2_apply=0,
        sa_keff=0,
        si=0,
    )
    gd["dgA_mark"] = [0, 0]
    gd["dgA_par"] = 0
    gd["dgB_mark"] = [0, 0]
    gd["dgB_par"] = 0
    gd["slot_sa"] = [("sA", 0), ("sA", 0)]
    gd["st_conv3p"] = [0, 0]
    gd["st_conv4p"] = [0, 0]
    gd["st_stats_par"] = [0, 0]
    gd["kv_aux"] = {}
    keff_done = [False]

    def next_slot():
        s = gd["si"] % 2
        gd["si"] += 1
        return s

    xa3 = xa.ap().rearrange("p (c t) -> p c t", c=NDC)
    tmp3 = u16sq.ap().rearrange("p (c t) -> p c t", c=NDC)
    x1v = x1pair.ap().rearrange("p (w c) -> p w c", w=2)
    x1vB = x1pairB.ap().rearrange("p (w c) -> p w c", w=2)
    x1vs = [x1v, x1vB]
    x1pairs = [x1pair, x1pairB]
    kvts = [kvt, kvtB]

    def ln_stats_piece(src, piece, s_st, first, last_):
        lo = piece * 1024
        par = piece % 2
        base = par * 2048
        u16 = u16sq.ap()[:, base:base + 1024]
        sq16 = u16sq.ap()[:, base + 1024:base + 2048]
        C.wait(V, "V", "sT", gd["st_stats_par"][par])
        vwait_self()
        vop(V.tensor_copy(u16, src[:, lo:lo + 1024]))
        vwait_self()
        vop(V.tensor_mul(sq16, u16, u16))
        sv_c = C.cnt["sV"]
        ps = psL if s_st is None else slots[s_st]
        C.wait(T, "T", "sV", sv_c)
        if first and s_st is not None:
            C.wait(T, "T", gd["slot_sa"][s_st][0], gd["slot_sa"][s_st][1])
        last = None
        for q2 in range(2):
            T.matmul(ps[0:1, 0:512], ones16[:, :], u16[:, q2 * 512:(q2 + 1) * 512],
                     start=(first and q2 == 0), stop=(last_ and q2 == 1))
            last = T.matmul(ps[0:1, 512:1024], ones16[:, :],
                            sq16[:, q2 * 512:(q2 + 1) * 512],
                            start=(first and q2 == 0), stop=(last_ and q2 == 1))
        st = C.inc(last, "sT")
        gd["st_stats_prev"] = st
        gd["st_stats_par"][par] = st
        return st

    def ln_rows_bcast(s_st, s_bc):
        ps = psL if s_st is None else slots[s_st]
        C.wait(V, "V", "sT", gd["st_stats_prev"])
        vwait_self()
        vop(V.tensor_scalar(rowb[0:1, 0:1024], ps[0:1, 0:1024], 1.0 / D, None, op0=ALU.mult))
        vwait_self()
        vop(V.tensor_mul(rowt[0:1, :], rowb[0:1, 0:512], rowb[0:1, 0:512]))
        vwait_self()
        vop(V.tensor_sub(rowt[0:1, :], rowb[0:1, 512:1024], rowt[0:1, :]))
        sv_rows = C.cnt["sV"]
        if s_st is not None:
            gd["slot_sa"][s_st] = ("sV", sv_rows)
        C.wait(A, "A", "sV", sv_rows)
        C.inc(A.activation(rowt[0:1, :], rowt[0:1, :], AF.Ln, bias=EPS), "sA")
        C.wait(A, "A", "sA", C.cnt["sA"])
        C.inc(A.activation(rowt[0:1, :], rowt[0:1, :], AF.Exp, scale=-0.5), "sA")
        sa_rstd = C.cnt["sA"]
        C.wait(V, "V", "sA", sa_rstd)
        vwait_self()
        vop(V.scalar_tensor_tensor(rowb[0:1, 512:1024], rowb[0:1, 0:512], -1.0,
                                   rowt[0:1, :], op0=ALU.mult, op1=ALU.mult))
        sv_rw = C.cnt["sV"]
        ps2 = slots[s_bc]
        C.wait(T, "T", "sV", sv_rw)
        C.wait(T, "T", gd["slot_sa"][s_bc][0], gd["slot_sa"][s_bc][1])
        T.matmul(ps2[:, 0:512], ones32[:, :], rowt[0:1, :], start=True, stop=True)
        st_b = C.inc(T.matmul(ps2[:, 512:1024], ones32[:, :], rowb[0:1, 512:1024],
                              start=True, stop=True), "sT")
        C.wait(A, "A", "sT", st_b)
        C.wait(A, "A", "sV", gd["sv_apply"])
        gd["sa_lnb"] = C.inc(A.copy(lnb[:, :], ps2[:, :]), "sA")
        gd["slot_sa"][s_bc] = ("sA", gd["sa_lnb"])
        return gd["sa_lnb"]

    def ln_apply(src_sb, extra_T_wait=0):
        src3 = src_sb.ap().rearrange("p (c t) -> p c t", c=NDC)
        rstd_b = lnb.ap()[:, 0:512].rearrange("p (o t) -> p o t", o=1).broadcast_to([128, 1, 512])
        mrstd_b = lnb.ap()[:, 512:1024].rearrange("p (o t) -> p o t", o=1).broadcast_to([128, 1, 512])
        C.wait(V, "V", "sA", gd["sa_lnb"])
        C.wait(V, "V", "sT", gd["st_stats_prev"])
        C.wait(V, "V", "sT", max(gd["st_x2_read"], gd["st_xa_read"], extra_T_wait))
        C.wait(V, "V", "qSPX", gd["qs_x2dma"])
        marks = []
        for dc in range(NDC):
            vwait_self()
            vop(V.tensor_mul(tmp3[:, dc:dc + 1, :], src3[:, dc:dc + 1, :], rstd_b))
            vwait_self()
            vop(V.tensor_tensor(xa3[:, dc:dc + 1, :], tmp3[:, dc:dc + 1, :],
                                mrstd_b, op=ALU.add))
            marks.append(C.cnt["sV"])
        gd["sv_apply"] = C.cnt["sV"]
        gd["sv_apply_dc"] = marks
        return gd["sv_apply"]

    def win_pair(ocA, ocB, sv_src, wtab, src_sb):
        s = next_slot()
        ps = slots[s]
        C.wait(T, "T", "qWA", qg_attnw)
        chase = isinstance(sv_src, list)
        if not chase:
            C.wait(T, "T", "sV", sv_src)
        C.wait(T, "T", gd["slot_sa"][s][0], gd["slot_sa"][s][1])
        last = None
        for k in range(NDC):
            if chase:
                C.wait(T, "T", "sV", sv_src[k])
            T.matmul(ps[:, 0:512], wtab[k][:, ocA * 128:(ocA + 1) * 128],
                     src_sb[:, k * 512:(k + 1) * 512],
                     start=(k == 0), stop=(k == NDC - 1))
            last = T.matmul(ps[:, 512:1024], wtab[k][:, ocB * 128:(ocB + 1) * 128],
                            src_sb[:, k * 512:(k + 1) * 512],
                            start=(k == 0), stop=(k == NDC - 1))
        return s, C.inc(last, "sT")

    def evac_to_x1(s, st_win, hslot, ebA, ebB, jpar):
        xv = x1vs[jpar]
        C.wait(A, "A", "sT", st_win)
        C.wait(A, "A", "sT", gd["st_conv3p"][jpar])
        C.wait(A, "A", "sV", gd["sv_haloout"])
        A.activation(xv[:, 0:1, 2:514], slots[s][:, 0:512], AF.Identity,
                     bias=binp[:, ebA:ebA + 1])
        sa = C.inc(A.activation(xv[:, 1:2, 2:514], slots[s][:, 512:1024], AF.Identity,
                                bias=binp[:, ebB:ebB + 1]), "sA")
        C.wait(V, "V", "sA", sa)
        C.wait(V, "V", "sT", gd["st_conv3p"][jpar])
        vwait_self()
        vop(V.tensor_copy(xv[:, :, 0:2],
                          x1h.ap()[:, hslot * 4:hslot * 4 + 4].rearrange(
                              "p (w c) -> p w c", w=2)))
        sv_hin = C.cnt["sV"]
        vop(V.tensor_copy(x1h.ap()[:, hslot * 4:hslot * 4 + 4].rearrange(
            "p (w c) -> p w c", w=2), xv[:, :, 512:514]))
        gd["sv_haloout"] = C.cnt["sV"]
        return sa, sv_hin

    def conv3_pair(s, ocA, ocB, sa_x1, sv_hin, jpar):
        par = gd["dgA_par"] = 1 - gd["dgA_par"]
        C.wait(V, "V", "sT", gd["dgA_mark"][par])
        vwait_self()
        for half, oc in ((0, ocA), (1, ocB)):
            for k in range(3):
                vop(V.tensor_scalar(dgA[par * 6 + half * 3 + k][:, :], eyeF[:, :],
                                    swp[:, oc * 3 + k:oc * 3 + k + 1], None, op0=ALU.mult))
        sv_dg = C.cnt["sV"]
        ps = slots[s]
        C.wait(T, "T", "sA", sa_x1)
        C.wait(T, "T", "sV", max(sv_dg, gd["sv_haloout"], sv_hin, sv_p0))
        last = None
        for half, oc in ((0, ocA), (1, ocB)):
            for k in range(3):
                last = T.matmul(ps[:, half * 512:(half + 1) * 512],
                                dgA[par * 6 + half * 3 + k][:, :],
                                x1pairs[jpar][:, half * 514 + k:half * 514 + k + 512],
                                start=(k == 0), stop=(k == 2))
        st = C.inc(last, "sT")
        gd["st_conv3p"][jpar] = st
        gd["dgA_mark"][par] = st
        return st

    def evac_to_convpair(s, st_c3, btab=None, ebA=0, ebB=0):
        C.wait(A, "A", "sT", st_c3)
        C.wait(A, "A", "sP", gd["sp_convpair"])
        if btab is None:
            sa = C.inc(A.copy(convpair[:, :], slots[s][:, :]), "sA")
        else:
            A.activation(convpair[:, 0:512], slots[s][:, 0:512], AF.Identity,
                         bias=btab[:, ebA:ebA + 1])
            sa = C.inc(A.activation(convpair[:, 512:1024], slots[s][:, 512:1024],
                                    AF.Identity, bias=btab[:, ebB:ebB + 1]), "sA")
        gd["slot_sa"][s] = ("sA", sa)
        return sa

    sv_ln1_pending = [None]

    for t in range(nt):
        t0 = t * TILE
        if t == 0:
            C.wait(V, "V", "qU", qs_u[0])
            s_st = next_slot()
            for piece in range(4):
                ln_stats_piece(ut.ap(), piece, s_st, piece == 0, piece == 3)
            s_bc = next_slot()
            ln_rows_bcast(s_st, s_bc)
            sv_ln1 = ln_apply(ut)
        else:
            sv_ln1 = sv_ln1_pending[0]

        ln1_dc = gd["sv_apply_dc"]
        jobs = [("kv", i) for i in range(NDC)] + [("q", j) for j in range(4)]

        def job_win(idx, job):
            kind, i = job
            if kind == "kv":
                ocA, ocB, hslot = 8 + i, 16 + i, i
            else:
                ocA, ocB, hslot = 2 * i, 2 * i + 1, 8 + i
            s, st_w = win_pair(ocA, ocB, ln1_dc if idx == 0 else sv_ln1, wi16, xa)
            gd["st_xa_read"] = st_w
            return [s, ocA, ocB, st_w, hslot, idx % 2]

        def job_evac(hd):
            s, ocA, ocB, st_w, hslot, jpar = hd
            sa_x1, sv_hin = evac_to_x1(s, st_w, hslot, ocA, ocB, jpar)
            hd[3] = (sa_x1, sv_hin)

        def job_tail1(job, hd):
            kind, i = job
            s, ocA, ocB, (sa_x1, sv_hin), hslot, jpar = hd
            st_c3 = conv3_pair(s, ocA, ocB, sa_x1, sv_hin, jpar)
            sa_cp = evac_to_convpair(s, st_c3, sbp, ocA, ocB)
            if kind == "kv":
                par = i % 2
                kv = kvts[par]
                C.wait(P, "P", "sA", sa_cp)
                C.wait(P, "P", "sT", gd["st_conv4p"][par])
                C.wait(P, "P", "sV", gd["sv_kvh"])
                sp_kv = C.inc(P.tensor_mul(kv[:, 3:515], convpair[:, 0:512],
                                           convpair[:, 512:1024]), "sP")
                gd["sp_convpair"] = sp_kv
                C.wait(V, "V", "sP", sp_kv)
                C.wait(V, "V", "sT", gd["st_conv4p"][par])
                vwait_self()
                vop(V.tensor_copy(kv[:, 0:3], kvh[:, i * 3:i * 3 + 3]))
                vop(V.tensor_copy(kvh[:, i * 3:i * 3 + 3], kv[:, 512:515]))
                gd["sv_kvh"] = C.cnt["sV"]
                dpar = gd["dgB_par"] = 1 - gd["dgB_par"]
                C.wait(V, "V", "sT", gd["dgB_mark"][dpar])
                vwait_self()
                for k in range(4):
                    vop(V.tensor_scalar(dgB[dpar * 4 + k][:, :], eyeF[:, :],
                                        iwp[:, i * 4 + k:i * 4 + k + 1], None,
                                        op0=ALU.mult))
                gd["kv_aux"][i] = (par, dpar, C.cnt["sV"], sp_kv)
            else:
                C.wait(P, "P", "sA", sa_cp)
                C.wait(P, "P", "sA", gd["sa_vv_last"])
                C.wait(P, "P", "sT", gd["st_wout_last"])
                P.tensor_mul(y16[:, ocA * 512:(ocA + 1) * 512], convpair[:, 0:512],
                             vvt[:, ocA * 512:(ocA + 1) * 512])
                spy = C.inc(P.tensor_mul(y16[:, ocB * 512:(ocB + 1) * 512],
                                         convpair[:, 512:1024],
                                         vvt[:, ocB * 512:(ocB + 1) * 512]), "sP")
                gd["sp_convpair"] = spy
                gd["sp_y_last"] = spy

        def job_tail2(job):
            kind, i = job
            par, dpar, sv_dg4, sp_kv = gd["kv_aux"][i]
            kv = kvts[par]
            C.wait(T, "T", "sP", sp_kv)
            C.wait(T, "T", "sV", max(sv_dg4, gd["sv_kvh"]))
            C.wait(T, "T", "sA", gd["sa_vv_last"])
            C.wait(T, "T", "sV", gd["sv_argS"])
            last = None
            for k in range(4):
                last = T.matmul(psS[:, 0:512], dgB[dpar * 4 + k][:, :], kv[:, k:k + 512],
                                start=(k == 0), stop=(k == 3))
            st_c4 = C.inc(last, "sT")
            gd["st_conv4p"][par] = st_c4
            gd["dgB_mark"][dpar] = st_c4
            C.wait(A, "A", "sT", st_c4)
            gd["sa_vv_last"] = C.inc(
                A.activation(vvt[:, i * 512:(i + 1) * 512], psS[:, 0:512],
                             AF.Identity, bias=ibp[:, i:i + 1]), "sA")

        heads = {}
        for idx, job in enumerate(jobs):
            heads[idx] = job_win(idx, job)
            if idx >= 1:
                job_tail1(jobs[idx - 1], heads[idx - 1])
            if idx >= 2 and jobs[idx - 2][0] == "kv":
                job_tail2(jobs[idx - 2])
            job_evac(heads[idx])
        job_tail1(jobs[-1], heads[len(jobs) - 1])
        sp_y = gd["sp_y_last"]

        sp_r1 = 0
        st_wo = 0
        for j in range(4):
            ocA, ocB = 2 * j, 2 * j + 1
            s = next_slot()
            ps = slots[s]
            C.wait(T, "T", "sP", sp_y)
            C.wait(T, "T", gd["slot_sa"][s][0], gd["slot_sa"][s][1])
            last = None
            for k in range(NDC):
                T.matmul(ps[:, 0:512], wo16[k][:, ocA * 128:(ocA + 1) * 128],
                         y16[:, k * 512:(k + 1) * 512], start=(k == 0), stop=(k == NDC - 1))
                last = T.matmul(ps[:, 512:1024], wo16[k][:, ocB * 128:(ocB + 1) * 128],
                                y16[:, k * 512:(k + 1) * 512], start=(k == 0), stop=(k == NDC - 1))
            st_wo = C.inc(last, "sT")
            C.wait(A, "A", "sT", st_wo)
            C.wait(A, "A", "sP", gd["sp_convpair"])
            A.activation(convpair[:, 0:512], ps[:, 0:512], AF.Identity,
                         bias=boutp[:, ocA:ocA + 1])
            sa_o = C.inc(A.activation(convpair[:, 512:1024], ps[:, 512:1024],
                                      AF.Identity, bias=boutp[:, ocB:ocB + 1]), "sA")
            gd["slot_sa"][s] = ("sA", sa_o)
            C.wait(P, "P", "sA", sa_o)
            C.wait(P, "P", "qSPR", gd["qs_r1dma"])
            C.wait(P, "P", "sV", gd["sv_ln2_apply"])
            sp_r1 = C.inc(P.tensor_add(r1sb[:, ocA * 512:(ocB + 1) * 512],
                                       convpair[:, :],
                                       ut[:, ocA * 512:(ocB + 1) * 512]), "sP")
            gd["sp_convpair"] = sp_r1
            C.wait(SY, "S", "sP", sp_r1)
            C.inc(SY.dma_start(
                out=r1_d[ocA:ocB + 1, :, t0:t0 + TILE].rearrange("c p t -> p c t"),
                in_=r1sb.ap()[:, ocA * 512:(ocB + 1) * 512].rearrange(
                    "p (c t) -> p c t", c=2)), "qSPR", 16)
            C.wait(V, "V", "sP", sp_r1)
            ln_stats_piece(r1sb.ap(), j, None, j == 0, j == 3)
        gd["st_wout_last"] = st_wo
        gd["qs_r1dma"] = C.cnt["qSPR"]
        sp_r1_done = sp_r1

        if t + 1 < nt:
            C.wait(SY, "S", "sP", sp_r1_done)
            C.inc(SY.dma_start(out=ut3, in_=u_rearr[:, :, (t + 1) * TILE:(t + 2) * TILE]),
                  "qU", 16)
            qs_u[t + 1] = C.cnt["qU"]

        if not keff_done[0]:
            keff_done[0] = True
            keff_mark = 0
            s = next_slot()
            ps = slots[s]
            first = True
            for ph in range(16):
                C.wait(T, "T", "qWQ", C.cnt["qWQ"])
                for dc in range(NDC):
                    if first:
                        C.wait(T, "T", gd["slot_sa"][s][0], gd["slot_sa"][s][1])
                    else:
                        C.wait(T, "T", "sA", gd["sa_keff"])
                    first = False
                    last = None
                    for dkc in range(4):
                        last = T.matmul(ps[:, 0:128],
                                        wqb[dkc][:, dc * 128:(dc + 1) * 128],
                                        ktb[ph % 2][:, dkc * 128:(dkc + 1) * 128],
                                        start=(dkc == 0), stop=(dkc == 3))
                    stk = C.inc(last, "sT")
                    if dc == NDC - 1:
                        keff_mark = stk
                    C.wait(A, "A", "sT", stk)
                    gd["sa_keff"] = C.inc(
                        A.copy(keff[dc][:, ph * 128:(ph + 1) * 128], ps[:, 0:128]), "sA")
                if ph + 1 < 16:
                    C.wait(G, "P", "sT", keff_mark)
                    for dkc in range(4):
                        r0 = (ph + 1) * 512 + dkc * 128
                        C.inc(G.dma_start(out=wqb[dkc][:, :],
                                          in_=wq_d[r0:r0 + 128, :]), "qWQ", 16)
                    C.inc(G.dma_start(
                        out=ktb[(ph + 1) % 2].ap().rearrange("p (c k) -> p c k", c=4),
                        in_=kt_d[ph + 1:ph + 2, :, :].rearrange(
                            "s (c p) k -> p (s c) k", p=128)), "qWQ", 16)
            gd["slot_sa"][s] = ("sA", gd["sa_keff"])

        s_bc2 = next_slot()
        ln_rows_bcast(None, s_bc2)
        sv_ln2 = ln_apply(r1sb)
        ln2_dc = gd["sv_apply_dc"]
        gd["sv_ln2_apply"] = sv_ln2
        for j in range(4):
            ocA = 2 * j
            C.wait(SY, "S", "sV", sv_ln2)
            C.inc(SY.dma_start(
                out=x2_d[ocA:ocA + 2, :, t0:t0 + TILE].rearrange("c p t -> p c t"),
                in_=x2.ap()[:, ocA * 512:(ocA + 2) * 512].rearrange(
                    "p (c t) -> p c t", c=2)), "qSPX", 16)
        gd["qs_x2dma"] = C.cnt["qSPX"]

        for gi in range(4):
            g = t * 4 + gi
            go = gi * 128
            st_sim = {}
            for p in range(2):
                dstp = psS if p == 0 else psL
                if not (gi == 0 and p == 0):
                    C.wait(T, "T", "sV", sv_ln2)
                if p == 0:
                    C.wait(T, "T", "sV", gd["sv_argS"])
                    C.wait(T, "T", "sA", gd["sa_vv_last"])
                else:
                    C.wait(T, "T", "sV", gd["sv_argL"])
                last = None
                for half in range(2):
                    for dc in range(NDC):
                        if gi == 0 and p == 0 and half == 0:
                            C.wait(T, "T", "sV", ln2_dc[dc])
                        last = T.matmul(dstp[:, half * 512:(half + 1) * 512],
                                        x2[:, dc * 512 + go:dc * 512 + go + 128],
                                        keff[dc][:, p * 1024 + half * 512:p * 1024 + (half + 1) * 512],
                                        start=(dc == 0), stop=(dc == NDC - 1))
                st_sim[p] = C.inc(last, "sT")
            gd["st_x2_read"] = st_sim[1]
            for p in range(2):
                dstp = psS if p == 0 else psL
                ma = m0a if p == 0 else m1a
                ia = i0a if p == 0 else i1a
                C.wait(V, "V", "sT", st_sim[p])
                vwait_self()
                for hh in range(H):
                    vop(V.max(ma[:, hh * 8:hh * 8 + 8], dstp[:, hh * 128:(hh + 1) * 128]))
                vwait_self()
                for hh in range(H):
                    vop(V.max_index(ia[:, hh * 8:hh * 8 + 8], ma[:, hh * 8:hh * 8 + 8],
                                    dstp[:, hh * 128:(hh + 1) * 128]))
                if p == 0:
                    gd["sv_argS"] = C.cnt["sV"]
                else:
                    gd["sv_argL"] = C.cnt["sV"]
            vwait_self()
            m0s = m0a.ap().rearrange("p (h e) -> p h e", e=8)[:, :, 0:1]
            m1s = m1a.ap().rearrange("p (h e) -> p h e", e=8)[:, :, 0:1]
            i0s = i0a.ap().rearrange("p (h e) -> p h e", e=8)[:, :, 0:1]
            i1s = i1a.ap().rearrange("p (h e) -> p h e", e=8)[:, :, 0:1]
            vop(V.tensor_add(sc8[:, 16:24], m0s, m1s))
            vwait_self()
            vop(V.tensor_scalar_max(gate_all[:, g * 8:(g + 1) * 8], sc8[:, 16:24], 0.0))
            vop(V.tensor_copy(sc8[:, 0:8], i0s))
            vop(V.tensor_copy(sc8[:, 8:16], i1s))
            vwait_self()
            vop(V.scalar_tensor_tensor(sc8[:, 16:24], sc8[:, 0:8], 128.0,
                                       sc8[:, 8:16], op0=ALU.mult, op1=ALU.add))
            vwait_self()
            vop(V.tensor_copy(idx_all[:, g * 8:(g + 1) * 8], sc8[:, 16:24]))

        if t + 1 < nt:
            C.wait(V, "V", "qU", qs_u[t + 1])
            s_st = next_slot()
            for piece in range(4):
                ln_stats_piece(ut.ap(), piece, s_st, piece == 0, piece == 3)
            s_bc = next_slot()
            ln_rows_bcast(s_st, s_bc)
            sv_ln1_pending[0] = ln_apply(ut, extra_T_wait=gd["st_x2_read"])

        for cc in (2 * t, 2 * t + 1):
            if cc < NCV:
                conv_chunk(cc)

    for cc in range(2 * nt, NCV):
        conv_chunk(cc)

    C.wait(G, "P", "qWC", C.cnt["qWC"])
    with G.If_eq(pid1, 0):
        G.dma_start(out=shf[0:1, :], in_=flone[0:1, :]).then_inc(C.sems["qWC"], 16)
    with G.Else():
        G.dma_start(out=shf[1:2, :], in_=flone[0:1, :]).then_inc(C.sems["qWC"], 16)
    C.cnt["qWC"] += 16
    C.wait(G, "P", "qWC", C.cnt["qWC"])
    with nc.gpsimd.register("pcnt") as pcnt, nc.gpsimd.register("pcond") as pcond:
        G.reg_mov(pcnt, 0)
        G.reg_mov(pcond, 1)
        with G.While(pcond):
            G.reg_add(pcnt, pcnt, 16)
            with G.If_eq(pid1, 0):
                G.dma_start(out=flrd[0:1, 0:1], in_=shf[1:2, 0:1]).then_inc(C.sems["qPF"], 16)
            with G.Else():
                G.dma_start(out=flrd[0:1, 0:1], in_=shf[0:1, 0:1]).then_inc(C.sems["qPF"], 16)
            G.wait_ge(C.sems["qPF"], pcnt)
            G.reg_load(pcond, flrd[0:1, 0:1])
            G.reg_alu(pcond, pcond, 1, ALU.bitwise_xor)

    endT, endV, endA, endP = C.cnt["sT"], C.cnt["sV"], C.cnt["sA"], C.cnt["sP"]

    p1_ctx.close()
    ps_ctx.close()

    b2_ctx = ExitStack()
    b2 = b2_ctx.enter_context
    wif16 = [b2(nc.sbuf_tensor(f"wif16_{k}", [128, DI], BF16)) for k in range(NDC)]
    wof16 = [b2(nc.sbuf_tensor(f"wof16_{k}", [128, D], BF16)) for k in range(NIC)]
    ainb = [b2(nc.sbuf_tensor(f"ainb_{h}", [128, 1024], BF16)) for h in range(H)]
    binb = [b2(nc.sbuf_tensor(f"binb_{h}", [128, 2048], BF16)) for h in range(H)]
    aoutb = [b2(nc.sbuf_tensor(f"aoutb_{h}", [128, 2048], BF16)) for h in range(H)]
    boutb = [b2(nc.sbuf_tensor(f"boutb_{h}", [128, 1024], BF16)) for h in range(H)]
    x2g2 = [b2(nc.sbuf_tensor(f"x2g_{k}", [128, 1024], BF16)) for k in range(2)]
    r1g = b2(nc.sbuf_tensor("r1g", [128, 1024], F32))
    xtok = b2(nc.sbuf_tensor("xtok", [128, 1024], BF16))
    htok = b2(nc.sbuf_tensor("htok", [128, 2048], BF16))
    hfm = b2(nc.sbuf_tensor("hfm", [128, 2048], BF16))
    pj4 = b2(nc.sbuf_tensor("pj4", [128, 4096], BF16))
    pjin = b2(nc.sbuf_tensor("pjin", [128, 2048], BF16))
    linr = b2(nc.sbuf_tensor("linr", [128, 16], F32))
    ling = b2(nc.sbuf_tensor("ling", [128, 16], F32))
    dgb = [b2(nc.sbuf_tensor(f"dgb_{h}", [128, 128], BF16)) for h in range(2 * H)]
    outg = b2(nc.sbuf_tensor("outg", [128, 1024], F32))

    psH = b2(nc.psum_tensor("psH", [128, 2048], F32))
    psO = b2(nc.psum_tensor("psO", [128, 1024], F32))
    psT2 = b2(nc.psum_tensor("psT2", [128, 512], F32))
    psT3 = b2(nc.psum_tensor("psT3", [128, 512], F32))

    C.wait(G, "P", "sT", endT)
    C.wait(G, "P", "sV", endV)
    C.wait(G, "P", "sA", endA)
    C.wait(G, "P", "sP", endP)
    for k in range(NDC):
        C.inc(G.dma_start(out=wif16[k][:, :], in_=wif_d[k * 128:(k + 1) * 128, :]), "qWF", 16)
    for k in range(NIC):
        C.inc(G.dma_start(out=wof16[k][:, :], in_=wof_d[k * 128:(k + 1) * 128, :]), "qWF", 16)
    qg_ffnw = C.cnt["qWF"]
    C.wait(SY, "S", "sT", endT)
    C.wait(SY, "S", "sA", endA)
    C.wait(SY, "S", "sV", endV)
    C.wait(SY, "S", "sP", endP)
    C.wait(T, "T", "sV", endV)
    C.wait(T, "T", "sA", endA)
    C.wait(A, "A", "sT", endT)
    C.wait(A, "A", "sV", endV)
    C.wait(V, "V", "sT", endT)
    C.wait(V, "V", "sA", endA)
    C.wait(P, "P", "sT", endT)

    bg = dict(st_bin=0, st_bout=0, sa_gelu=0, sa_out=0, qs_y=0,
              sa_xtok=0, sa_hfm=[0, 0, 0, 0], sv_amul_in=0, sv_amul_out=0,
              sv_ling_in=0, sv_ling_out=0, sa_red_in=[0] * H, sa_red_out=[0] * H)

    GOFF = {0: (0, 1024, ainb, "qAI"), 1: (1024, 2048, binb, "qBI"),
            2: (3072, 2048, aoutb, "qAO"), 3: (5120, 1024, boutb, "qBO")}
    for qn in ("qAI", "qBI", "qAO", "qBO"):
        C.sem(qn)

    def emit_gathers(g, which, gate_sem, gate_val):
        off, width, bufs, qn = GOFF[which]
        C.wait(G, "P", gate_sem, gate_val)
        for h in range(H):
            C.inc(G.indirect_dma_start(
                bufs[h][:, 0:width], None,
                comb16[:, :],
                IndirectOffsetOnAxis(idx_all.ap()[:, g * 8 + h:g * 8 + h + 1], 0),
                element_offset=off), qn, 16)
        return C.cnt[qn]

    qai = emit_gathers(0, 0, "sT", 0)
    qbi = emit_gathers(0, 1, "sT", 0)
    qao = emit_gathers(0, 2, "sT", 0)
    qbo = emit_gathers(0, 3, "sT", 0)
    x2g3 = [x2g2[k].ap().rearrange("p (c t) -> p c t", c=NDC) for k in range(2)]
    r1g3 = r1g.ap().rearrange("p (c t) -> p c t", c=NDC)
    C.inc(SY.dma_start(out=x2g3[0], in_=x2_d[:, :, 0:128].rearrange("c p t -> p c t")),
          "qX", 16)
    qx = C.cnt["qX"]
    C.inc(SY.dma_start(out=r1g3, in_=r1_d[:, :, 0:128].rearrange("c p t -> p c t")),
          "qR", 16)
    qr = C.cnt["qR"]

    for g in range(NG):
        par = g % 2
        x2g = x2g2[par]
        C.wait(T, "T", "qX", qx)
        for half in range(2):
            pst = psT2 if half == 0 else psT3
            C.wait(T, "T", "sA", max(bg["sa_xtok"], bg["sa_hfm"][half + 2]))
            last = None
            for c in range(4):
                cc = half * 4 + c
                last = T.matmul(pst[:, c * 128:(c + 1) * 128],
                                x2g[:, cc * 128:(cc + 1) * 128], eye16[:, :],
                                start=True, stop=True)
            st = C.inc(last, "sT")
            C.wait(A, "A", "sT", st)
            C.wait(A, "A", "sV", bg["sv_amul_in"])
            bg["sa_xtok"] = C.inc(
                A.copy(xtok[:, half * 512:(half + 1) * 512], pst[:, :]), "sA")

        C.wait(V, "V", "qAI", qai)
        C.wait(V, "V", "sA", bg["sa_xtok"])
        for h in range(H):
            hp = (h % 2) * 1024
            vwait_self()
            vop(V.scalar_tensor_tensor(pjin[:, hp:hp + 1024], ainb[h][:, :], 1.0,
                                       xtok[:, :], op0=ALU.mult, op1=ALU.mult,
                                       accum_out=linr[:, h:h + 1]))
        bg["sv_amul_in"] = C.cnt["sV"]
        if g + 1 < NG:
            qai = emit_gathers(g + 1, 0, "sV", bg["sv_amul_in"])
        vwait_self()
        vop(V.tensor_mul(ling[:, 0:8], linr[:, 0:8], gate_all[:, g * 8:(g + 1) * 8]))
        bg["sv_ling_in"] = C.cnt["sV"]
        vwait_self()
        for h in range(H):
            C.wait(V, "V", "sT", bg["st_bin"])
            vop(V.tensor_scalar(dgb[h][:, :], eyeF[:, :], ling[:, h:h + 1], None,
                                op0=ALU.mult))
        sv_dgin = C.cnt["sV"]

        C.wait(T, "T", "qWF", qg_ffnw)
        C.wait(T, "T", "qBI", qbi)
        C.wait(T, "T", "sV", sv_dgin)
        C.wait(T, "T", "sA", bg["sa_gelu"])
        n_contrib = NDC + H
        ci = 0
        last1 = None
        for dc in range(NDC):
            lhs = x2g[:, dc * 128:(dc + 1) * 128]
            for qtr in range(4):
                last1 = T.matmul(psH[:, qtr * 512:(qtr + 1) * 512], lhs,
                                 wif16[dc][:, qtr * 512:(qtr + 1) * 512],
                                 start=(ci == 0), stop=(ci == n_contrib - 1))
            ci += 1
        for h in range(H):
            for qtr in range(4):
                last1 = T.matmul(psH[:, qtr * 512:(qtr + 1) * 512], dgb[h][:, :],
                                 binb[h][:, qtr * 512:(qtr + 1) * 512],
                                 start=(ci == 0), stop=(ci == n_contrib - 1))
            ci += 1
        bg["st_bin"] = C.inc(last1, "sT")
        if g + 1 < NG:
            qbi = emit_gathers(g + 1, 1, "sT", bg["st_bin"])
            gsl = slice((g + 1) * 128, (g + 2) * 128)
            C.wait(SY, "S", "sT", bg["st_bin_prev"] if "st_bin_prev" in bg else 0)
            C.inc(SY.dma_start(out=x2g3[(g + 1) % 2],
                               in_=x2_d[:, :, gsl].rearrange("c p t -> p c t")), "qX", 16)
            qx = C.cnt["qX"]
        bg["st_bin_prev"] = bg["st_bin"]

        C.wait(A, "A", "sT", bg["st_bin"])
        C.wait(A, "A", "sV", bg["sv_amul_out"])
        bg["sa_gelu"] = C.inc(A.activation(htok[:, :], psH[:, :], AF.Gelu), "sA")

        C.wait(T, "T", "sA", bg["sa_gelu"])
        for half in range(4):
            pst = psT2 if half % 2 == 0 else psT3
            C.wait(T, "T", "sA",
                   bg["sa_xtok"] if half < 2 else bg["sa_hfm"][half - 2])
            last = None
            for c in range(4):
                cc = half * 4 + c
                last = T.matmul(pst[:, c * 128:(c + 1) * 128],
                                htok[:, cc * 128:(cc + 1) * 128], eye16[:, :],
                                start=True, stop=True)
            st = C.inc(last, "sT")
            C.wait(A, "A", "sT", st)
            C.wait(A, "A", "sT", bg["st_bout"])
            bg["sa_hfm"][half] = C.inc(
                A.copy(hfm[:, half * 512:(half + 1) * 512], pst[:, :]), "sA")

        C.wait(V, "V", "qAO", qao)
        C.wait(V, "V", "sA", bg["sa_gelu"])
        for h in range(H):
            hp = (h % 2) * 2048
            vwait_self()
            vop(V.scalar_tensor_tensor(pj4[:, hp:hp + 2048], aoutb[h][:, :], 1.0,
                                       htok[:, :], op0=ALU.mult, op1=ALU.mult,
                                       accum_out=linr[:, 8 + h:9 + h]))
        bg["sv_amul_out"] = C.cnt["sV"]
        if g + 1 < NG:
            qao = emit_gathers(g + 1, 2, "sV", bg["sv_amul_out"])
        vwait_self()
        vop(V.tensor_mul(ling[:, 8:16], linr[:, 8:16], gate_all[:, g * 8:(g + 1) * 8]))
        bg["sv_ling_out"] = C.cnt["sV"]
        vwait_self()
        for h in range(H):
            C.wait(V, "V", "sT", bg["st_bout"])
            vop(V.tensor_scalar(dgb[8 + h][:, :], eyeF[:, :], ling[:, 8 + h:9 + h], None,
                                op0=ALU.mult))
        sv_dgout = C.cnt["sV"]

        C.wait(T, "T", "sV", sv_dgout)
        C.wait(T, "T", "qBO", qbo)
        C.wait(T, "T", "qR", qr)
        C.wait(T, "T", "sA", bg["sa_out"])
        last = None
        for c in range(NIC):
            for hf in range(2):
                last = T.matmul(psO[:, hf * 512:(hf + 1) * 512],
                                hfm[:, c * 128:(c + 1) * 128],
                                wof16[c][:, hf * 512:(hf + 1) * 512],
                                start=(c == 0), stop=False)
        for h in range(H):
            for hf in range(2):
                last = T.matmul(psO[:, hf * 512:(hf + 1) * 512], dgb[8 + h][:, :],
                                boutb[h][:, hf * 512:(hf + 1) * 512],
                                start=False, stop=False)
        for dc in range(NDC):
            last = T.matmul(psO[:, dc * 128:(dc + 1) * 128],
                            r1g[:, dc * 128:(dc + 1) * 128], eyeF[:, :],
                            start=False, stop=(dc == 3 or dc == NDC - 1))
        bg["st_bout"] = C.inc(last, "sT")
        if g + 1 < NG:
            qbo = emit_gathers(g + 1, 3, "sT", bg["st_bout"])
            gsl = slice((g + 1) * 128, (g + 2) * 128)
            C.wait(SY, "S", "sT", bg["st_bout"])
            C.inc(SY.dma_start(out=r1g3,
                               in_=r1_d[:, :, gsl].rearrange("c p t -> p c t")), "qR", 16)
            qr = C.cnt["qR"]

        C.wait(A, "A", "sT", bg["st_bout"])
        C.wait(A, "A", "qY", bg["qs_y"])
        bg["sa_out"] = C.inc(A.copy(outg[:, :], psO[:, :]), "sA")
        C.wait(SY, "S", "sA", bg["sa_out"])
        C.inc(SY.dma_start(out=y_d[g * 128:(g + 1) * 128, :], in_=outg[:, :]), "qY", 16)
        bg["qs_y"] = C.cnt["qY"]

    if debug:
        C.wait(SY, "S", "sV", C.cnt["sV"])
        C.inc(SY.dma_start(out=idx_d[:, :], in_=idx_all[:, :]), "qY", 16)
        C.inc(SY.dma_start(out=gate_d[:, :], in_=gate_all[:, :]), "qY", 16)

    for s in ("qC", "qU", "qSPR", "qSPX", "qX", "qR", "qY"):
        C.wait(SY, "S", s, C.cnt[s])
    for s in ("qWA", "qWQ", "qWF", "qAI", "qBI", "qAO", "qBO", "qWC"):
        C.wait(G, "P", s, C.cnt[s])

    b2_ctx.close()
    ctx.close()
    return nc


def _host_prep(inputs):
    u = inputs["L0_attn_u"]
    comb = np.concatenate([inputs["L0_ffn_in_a"], inputs["L0_ffn_in_b"],
                           inputs["L0_ffn_out_a"], inputs["L0_ffn_out_b"]], axis=1)
    comb = np.ascontiguousarray(comb)
    keys = inputs["L0_ffn_keys"]
    kt = np.ascontiguousarray(keys.transpose(2, 0, 3, 1).reshape(16, 512, 128))
    base = {
        "combt": comb,
        "wi": np.ascontiguousarray(inputs["L0_attn_W_in"]),
        "wo": np.ascontiguousarray(inputs["L0_attn_W_out"]),
        "wq": np.ascontiguousarray(inputs["L0_ffn_Wq"]),
        "kt": kt,
        "wif": np.ascontiguousarray(inputs["L0_ffn_W_in"].T),
        "wof": np.ascontiguousarray(inputs["L0_ffn_W_out"].T),
        "binp": np.ascontiguousarray(inputs["L0_attn_b_in"].reshape(NOC, 128).T),
        "boutp": np.ascontiguousarray(inputs["L0_attn_b_out"].reshape(NDC, 128).T),
        "sbp": np.ascontiguousarray(inputs["L0_attn_short_b"].reshape(NOC, 128).T),
        "ibp": np.ascontiguousarray(inputs["L0_attn_inner_b"].reshape(NDC, 128).T),
        "swp": np.ascontiguousarray(
            inputs["L0_attn_short_w"].reshape(NOC, 128, 3).transpose(1, 0, 2).reshape(128, NOC * 3)),
        "iwp": np.ascontiguousarray(
            inputs["L0_attn_inner_w"].reshape(NDC, 128, 4).transpose(1, 0, 2).reshape(128, NDC * 4)),
    }
    u_fm = np.ascontiguousarray(u.transpose(0, 2, 1))
    return base, u_fm


def kernel(**inputs):
    base, u_fm = _host_prep(inputs)
    nc = build_nc(nt=8, debug=False)
    in_maps = [dict(base, u=u_fm[c]) for c in range(8)]
    res = run_bass_kernel_spmd(nc, in_maps, core_ids=list(range(8)))
    out = np.stack([np.asarray(r["yout"]) for r in res.results], axis=0)
    return out.astype(np.float32, copy=False)


# version 13
# speedup vs baseline: 1.1861x; 1.1861x over previous
import numpy as np
import concourse.bass as bass
import concourse.mybir as mybir
from concourse.bass_utils import run_bass_kernel_spmd
from concourse.bass import IndirectOffsetOnAxis
from contextlib import ExitStack

F32 = mybir.dt.float32
BF16 = mybir.dt.bfloat16
I32 = mybir.dt.int32
U32 = mybir.dt.uint32
AF = mybir.ActivationFunctionType
ALU = mybir.AluOpType

L = 4096
TILE = 512
GRP = 128
D = 1024
NDC = 8
D3 = 3072
NOC = 24
DI = 2048
NIC = 16
H = 8
NE = 16384
EPS = 1e-5


class Ctr:
    def __init__(self, nc):
        self.nc = nc
        self.sems = {}
        self.cnt = {}
        self.marks = {}

    def sem(self, name):
        s = self.nc.semaphore(name).__enter__()
        self.sems[name] = s
        self.cnt[name] = 0

    def inc(self, instr, name, by=1):
        self.cnt[name] += by
        instr.then_inc(self.sems[name], by)
        return self.cnt[name]

    def wait(self, eng, ename, name, val):
        if val <= 0:
            return
        key = (ename, name)
        if self.marks.get(key, -1) >= val:
            return
        eng.wait_ge(self.sems[name], val)
        self.marks[key] = val


def build_nc(nt=8, debug=False):
    nc = bass.Bass("TRN2", target_bir_lowering=False, debug=False,
                   detect_race_conditions=False)
    NG = (nt * TILE) // GRP

    u_d = nc.dram_tensor("u", [D, L], F32, kind="ExternalInput").ap()
    comb_d = nc.dram_tensor("combt", [NE, 6144], F32, kind="ExternalInput").ap()
    wi_d = nc.dram_tensor("wi", [D, D3], F32, kind="ExternalInput").ap()
    wo_d = nc.dram_tensor("wo", [D, D], F32, kind="ExternalInput").ap()
    wq_d = nc.dram_tensor("wq", [8192, D], F32, kind="ExternalInput").ap()
    kt_d = nc.dram_tensor("kt", [16, 512, 128], F32, kind="ExternalInput").ap()
    wif_d = nc.dram_tensor("wif", [D, DI], F32, kind="ExternalInput").ap()
    wof_d = nc.dram_tensor("wof", [DI, D], F32, kind="ExternalInput").ap()
    bin_d = nc.dram_tensor("binp", [128, NOC], F32, kind="ExternalInput").ap()
    bout_d = nc.dram_tensor("boutp", [128, NDC], F32, kind="ExternalInput").ap()
    sb_d = nc.dram_tensor("sbp", [128, NOC], F32, kind="ExternalInput").ap()
    ib_d = nc.dram_tensor("ibp", [128, NDC], F32, kind="ExternalInput").ap()
    sw_d = nc.dram_tensor("swp", [128, NOC * 3], F32, kind="ExternalInput").ap()
    iw_d = nc.dram_tensor("iwp", [128, NDC * 4], F32, kind="ExternalInput").ap()
    eye_d = nc.inline_tensor(np.eye(128, dtype=np.float32), "eye128").ap()
    comb16 = nc.dram_tensor("comb16sh", [NE, 6144], BF16, kind="Internal",
                            addr_space="Shared").ap()
    shf = nc.dram_tensor("shflag", [2, 128], I32, kind="Internal",
                         addr_space="Shared").ap()

    y_d = nc.dram_tensor("yout", [L, D], F32, kind="ExternalOutput").ap()
    ik = "ExternalOutput" if debug else "Internal"
    r1_d = nc.dram_tensor("r1d", [NDC, 128, L], F32, kind=ik).ap()
    x2_d = nc.dram_tensor("x2d", [NDC, 128, L], BF16, kind=ik).ap()
    if debug:
        idx_d = nc.dram_tensor("idxd", [128, 256], I32, kind="ExternalOutput").ap()
        gate_d = nc.dram_tensor("gated", [128, 256], F32, kind="ExternalOutput").ap()

    ctx = ExitStack()
    e = ctx.enter_context
    C = Ctr(nc)
    for s in ("qC", "qU", "qSPR", "qSPX", "qX", "qR", "qY", "qWA", "qWQ",
              "qWF", "qGI", "qGO", "qWC", "qPF", "sV", "sA", "sP", "sT"):
        C.sem(s)
    SY, V, A, P, T, G = nc.sync, nc.vector, nc.scalar, nc.gpsimd, nc.tensor, nc.gpsimd

    def vop(instr):
        return C.inc(instr, "sV")

    def vwait_self():
        C.wait(V, "V", "sV", C.cnt["sV"])

    ones16 = e(nc.sbuf_tensor("ones16", [128, 1], BF16))
    ones32 = e(nc.sbuf_tensor("ones32", [1, 128], F32))
    eyeF = e(nc.sbuf_tensor("eyeF", [128, 128], F32))
    eye16 = e(nc.sbuf_tensor("eye16", [128, 128], BF16))
    swp = e(nc.sbuf_tensor("swp_s", [128, NOC * 3], F32))
    iwp = e(nc.sbuf_tensor("iwp_s", [128, NDC * 4], F32))
    binp = e(nc.sbuf_tensor("binp_s", [128, NOC], F32))
    boutp = e(nc.sbuf_tensor("boutp_s", [128, NDC], F32))
    sbp = e(nc.sbuf_tensor("sbp_s", [128, NOC], F32))
    ibp = e(nc.sbuf_tensor("ibp_s", [128, NDC], F32))
    cst = e(nc.sbuf_tensor("cst_ln", [128, 2], F32))
    flz = e(nc.sbuf_tensor("flz", [1, 128], I32))
    flone = e(nc.sbuf_tensor("flone", [1, 128], I32))
    flrd = e(nc.sbuf_tensor("flrd", [1, 4], I32))
    idx_all = e(nc.sbuf_tensor("idx_all", [128, 256], I32))
    gate_all = e(nc.sbuf_tensor("gate_all", [128, 256], F32))

    ps_ctx = ExitStack()
    pe = ps_ctx.enter_context
    psA = pe(nc.psum_tensor("psA", [128, 1024], F32))
    psB = pe(nc.psum_tensor("psB", [128, 1024], F32))
    psL = pe(nc.psum_tensor("psL", [128, 1024], F32))
    psS = pe(nc.psum_tensor("psS", [128, 1024], F32))
    slots = [psA, psB]

    p1_ctx = ExitStack()
    p1 = p1_ctx.enter_context
    m0a = p1(nc.sbuf_tensor("m0a", [128, 64], F32))
    m1a = p1(nc.sbuf_tensor("m1a", [128, 64], F32))
    i0a = p1(nc.sbuf_tensor("i0a", [128, 64], U32))
    i1a = p1(nc.sbuf_tensor("i1a", [128, 64], U32))
    sc8 = p1(nc.sbuf_tensor("sc8", [128, 24], F32))
    rowb = p1(nc.sbuf_tensor("rowb", [1, 1024], F32))
    rowt = p1(nc.sbuf_tensor("rowt", [1, 512], F32))
    lnb = p1(nc.sbuf_tensor("lnb", [128, 1024], F32))
    wi16 = [p1(nc.sbuf_tensor(f"wi16_{k}", [128, D3], BF16)) for k in range(NDC)]
    wo16 = [p1(nc.sbuf_tensor(f"wo16_{k}", [128, D], BF16)) for k in range(NDC)]
    keff = [p1(nc.sbuf_tensor(f"keff_{k}", [128, 2048], BF16)) for k in range(NDC)]
    wqb = [p1(nc.sbuf_tensor(f"wqb_{k}", [128, D], BF16)) for k in range(4)]
    ktb = [p1(nc.sbuf_tensor(f"ktb_{k}", [128, 512], BF16)) for k in range(2)]
    dgA = [p1(nc.sbuf_tensor(f"dgA_{k}", [128, 128], BF16)) for k in range(12)]
    dgB = [p1(nc.sbuf_tensor(f"dgB_{k}", [128, 128], BF16)) for k in range(8)]
    ut = p1(nc.sbuf_tensor("ut", [128, 8 * 512], F32))
    u16sq = p1(nc.sbuf_tensor("u16sq", [128, 4096], BF16))
    xa = p1(nc.sbuf_tensor("xa", [128, 8 * 512], BF16))
    x2 = xa
    x1pair = p1(nc.sbuf_tensor("x1pair", [128, 1028], BF16))
    x1pairB = p1(nc.sbuf_tensor("x1pairB", [128, 1028], BF16))
    x1h = p1(nc.sbuf_tensor("x1h", [128, 48], BF16))
    convpair = p1(nc.sbuf_tensor("convpair", [128, 1024], F32))
    kvt = p1(nc.sbuf_tensor("kvt", [128, 515], BF16))
    kvtB = p1(nc.sbuf_tensor("kvtB", [128, 515], BF16))
    kvh = p1(nc.sbuf_tensor("kvh", [128, NDC * 3], BF16))
    vvt = p1(nc.sbuf_tensor("vvt", [128, 8 * 512], BF16))
    y16 = p1(nc.sbuf_tensor("y16", [128, 8 * 512], BF16))
    r1sb = p1(nc.sbuf_tensor("r1sb", [128, 8 * 512], F32))

    for dst, srcd in ((swp, sw_d), (iwp, iw_d), (eyeF, eye_d), (binp, bin_d),
                      (boutp, bout_d), (sbp, sb_d), (ibp, ib_d)):
        C.inc(SY.dma_start(out=dst[:, :], in_=srcd[:, :]), "qC", 16)
    qs_consts = C.cnt["qC"]

    for k in range(NDC):
        C.inc(G.dma_start(out=wi16[k][:, :], in_=wi_d[k * 128:(k + 1) * 128, :]), "qWA", 16)
    for k in range(NDC):
        C.inc(G.dma_start(out=wo16[k][:, :], in_=wo_d[k * 128:(k + 1) * 128, :]), "qWA", 16)
    qg_attnw = C.cnt["qWA"]
    C.inc(G.dma_start(out=ktb[0].ap().rearrange("p (c k) -> p c k", c=4),
                      in_=kt_d[0:1, :, :].rearrange("s (c p) k -> p (s c) k", p=128)), "qWQ", 16)
    for dkc in range(4):
        C.inc(G.dma_start(out=wqb[dkc][:, :],
                          in_=wq_d[dkc * 128:dkc * 128 + 128, :]), "qWQ", 16)

    C.wait(V, "V", "qC", qs_consts)
    vop(V.memset(cst[:, 0:1], EPS))
    vop(V.memset(cst[:, 1:2], -0.5))
    nc.const_aps.register(F32, EPS, cst.ap()[:, 0:1])
    nc.const_aps.register(F32, -0.5, cst.ap()[:, 1:2])
    vop(V.memset(ones16[:, :], 1.0))
    vop(V.memset(ones32[:, :], 1.0))
    vop(V.tensor_copy(eye16[:, :], eyeF[:, :]))
    vop(V.memset(idx_all[:, :], 0))
    vop(V.memset(gate_all[:, :], 0.0))
    vop(V.memset(x1h[:, :], 0.0))
    vop(V.memset(kvt[:, :], 0.0))
    vop(V.memset(kvtB[:, :], 0.0))
    vop(V.memset(kvh[:, :], 0.0))
    vop(V.memset(flz[:, :], 0))
    vop(V.memset(flone[:, :], 1))
    sv_p0 = C.cnt["sV"]

    pid1 = nc.gpsimd.alloc_register("pid1")
    pv = G.partition_id()
    G.reg_alu(pid1, pv, 1, ALU.bitwise_and)
    C.wait(G, "P", "sV", sv_p0)
    with G.If_eq(pid1, 0):
        G.dma_start(out=shf[0:1, :], in_=flz[0:1, :]).then_inc(C.sems["qWC"], 16)
    with G.Else():
        G.dma_start(out=shf[1:2, :], in_=flz[0:1, :]).then_inc(C.sems["qWC"], 16)
    C.cnt["qWC"] += 16
    qwc_zero = C.cnt["qWC"]

    NCV = 16

    def conv_chunk(cc):
        r0e = cc * 512
        r0o = 8192 + cc * 512
        with G.If_eq(pid1, 0):
            G.dma_start(out=comb16[r0e:r0e + 512, :],
                        in_=comb_d[r0e:r0e + 512, :]).then_inc(C.sems["qWC"], 16)
        with G.Else():
            G.dma_start(out=comb16[r0o:r0o + 512, :],
                        in_=comb_d[r0o:r0o + 512, :]).then_inc(C.sems["qWC"], 16)
        C.cnt["qWC"] += 16

    u_rearr = u_d.rearrange("(c p) t -> p c t", p=128)
    ut3 = ut.ap().rearrange("p (c t) -> p c t", c=NDC)
    C.inc(SY.dma_start(out=ut3, in_=u_rearr[:, :, 0:TILE]), "qU", 16)
    qs_u = {0: C.cnt["qU"]}

    gd = dict(
        st_stats_prev=0, sv_rows=0, sa_lnb=0, sv_apply=0,
        sv_argS=0, sv_argL=0,
        st_x2_read=0, st_xa_read=0, qs_x2dma=0, qs_r1dma=0,
        sv_haloout=0, sv_kvh=0,
        st_conv3=0, st_conv4=0, sp_y_last=0,
        sa_vv_last=0, sp_convpair=0, sp_kv=0,
        st_wout_last=0, sv_ln2_apply=0,
        sa_keff=0,
        si=0,
    )
    gd["dgA_mark"] = [0, 0]
    gd["dgA_par"] = 0
    gd["dgB_mark"] = [0, 0]
    gd["dgB_par"] = 0
    gd["slot_sa"] = [("sA", 0), ("sA", 0)]
    gd["st_conv3p"] = [0, 0]
    gd["st_conv4p"] = [0, 0]
    gd["st_stats_par"] = [0, 0]
    gd["kv_aux"] = {}
    keff_done = [False]

    def next_slot():
        s = gd["si"] % 2
        gd["si"] += 1
        return s

    xa3 = xa.ap().rearrange("p (c t) -> p c t", c=NDC)
    tmp3 = u16sq.ap().rearrange("p (c t) -> p c t", c=NDC)
    x1v = x1pair.ap().rearrange("p (w c) -> p w c", w=2)
    x1vB = x1pairB.ap().rearrange("p (w c) -> p w c", w=2)
    x1vs = [x1v, x1vB]
    x1pairs = [x1pair, x1pairB]
    kvts = [kvt, kvtB]

    def ln_stats_piece(src, piece, s_st, first, last_):
        lo = piece * 1024
        par = piece % 2
        base = par * 2048
        u16 = u16sq.ap()[:, base:base + 1024]
        sq16 = u16sq.ap()[:, base + 1024:base + 2048]
        C.wait(V, "V", "sT", gd["st_stats_par"][par])
        vwait_self()
        vop(V.tensor_copy(u16, src[:, lo:lo + 1024]))
        vwait_self()
        vop(V.tensor_mul(sq16, u16, u16))
        sv_c = C.cnt["sV"]
        ps = psL if s_st is None else slots[s_st]
        C.wait(T, "T", "sV", sv_c)
        if first and s_st is not None:
            C.wait(T, "T", gd["slot_sa"][s_st][0], gd["slot_sa"][s_st][1])
        last = None
        for q2 in range(2):
            T.matmul(ps[0:1, 0:512], ones16[:, :], u16[:, q2 * 512:(q2 + 1) * 512],
                     start=(first and q2 == 0), stop=(last_ and q2 == 1))
            last = T.matmul(ps[0:1, 512:1024], ones16[:, :],
                            sq16[:, q2 * 512:(q2 + 1) * 512],
                            start=(first and q2 == 0), stop=(last_ and q2 == 1))
        st = C.inc(last, "sT")
        gd["st_stats_prev"] = st
        gd["st_stats_par"][par] = st
        return st

    def ln_rows_bcast(s_st, s_bc):
        ps = psL if s_st is None else slots[s_st]
        C.wait(V, "V", "sT", gd["st_stats_prev"])
        vwait_self()
        vop(V.tensor_scalar(rowb[0:1, 0:1024], ps[0:1, 0:1024], 1.0 / D, None, op0=ALU.mult))
        vwait_self()
        vop(V.tensor_mul(rowt[0:1, :], rowb[0:1, 0:512], rowb[0:1, 0:512]))
        vwait_self()
        vop(V.tensor_sub(rowt[0:1, :], rowb[0:1, 512:1024], rowt[0:1, :]))
        sv_rows = C.cnt["sV"]
        if s_st is not None:
            gd["slot_sa"][s_st] = ("sV", sv_rows)
        C.wait(A, "A", "sV", sv_rows)
        C.inc(A.activation(rowt[0:1, :], rowt[0:1, :], AF.Ln, bias=EPS), "sA")
        C.wait(A, "A", "sA", C.cnt["sA"])
        C.inc(A.activation(rowt[0:1, :], rowt[0:1, :], AF.Exp, scale=-0.5), "sA")
        sa_rstd = C.cnt["sA"]
        C.wait(V, "V", "sA", sa_rstd)
        vwait_self()
        vop(V.scalar_tensor_tensor(rowb[0:1, 512:1024], rowb[0:1, 0:512], -1.0,
                                   rowt[0:1, :], op0=ALU.mult, op1=ALU.mult))
        sv_rw = C.cnt["sV"]
        ps2 = slots[s_bc]
        C.wait(T, "T", "sV", sv_rw)
        C.wait(T, "T", gd["slot_sa"][s_bc][0], gd["slot_sa"][s_bc][1])
        T.matmul(ps2[:, 0:512], ones32[:, :], rowt[0:1, :], start=True, stop=True)
        st_b = C.inc(T.matmul(ps2[:, 512:1024], ones32[:, :], rowb[0:1, 512:1024],
                              start=True, stop=True), "sT")
        C.wait(A, "A", "sT", st_b)
        C.wait(A, "A", "sV", gd["sv_apply"])
        gd["sa_lnb"] = C.inc(A.copy(lnb[:, :], ps2[:, :]), "sA")
        gd["slot_sa"][s_bc] = ("sA", gd["sa_lnb"])
        return gd["sa_lnb"]

    def ln_apply(src_sb, extra_T_wait=0):
        src3 = src_sb.ap().rearrange("p (c t) -> p c t", c=NDC)
        rstd_b = lnb.ap()[:, 0:512].rearrange("p (o t) -> p o t", o=1).broadcast_to([128, 1, 512])
        mrstd_b = lnb.ap()[:, 512:1024].rearrange("p (o t) -> p o t", o=1).broadcast_to([128, 1, 512])
        C.wait(V, "V", "sA", gd["sa_lnb"])
        C.wait(V, "V", "sT", gd["st_stats_prev"])
        C.wait(V, "V", "sT", max(gd["st_x2_read"], gd["st_xa_read"], extra_T_wait))
        C.wait(V, "V", "qSPX", gd["qs_x2dma"])
        marks = []
        for dc in range(NDC):
            vwait_self()
            vop(V.tensor_mul(tmp3[:, dc:dc + 1, :], src3[:, dc:dc + 1, :], rstd_b))
            vwait_self()
            vop(V.tensor_tensor(xa3[:, dc:dc + 1, :], tmp3[:, dc:dc + 1, :],
                                mrstd_b, op=ALU.add))
            marks.append(C.cnt["sV"])
        gd["sv_apply"] = C.cnt["sV"]
        gd["sv_apply_dc"] = marks
        return gd["sv_apply"]

    def win_pair(ocA, ocB, sv_src, wtab, src_sb):
        s = next_slot()
        ps = slots[s]
        C.wait(T, "T", "qWA", qg_attnw)
        chase = isinstance(sv_src, list)
        if not chase:
            C.wait(T, "T", "sV", sv_src)
        C.wait(T, "T", gd["slot_sa"][s][0], gd["slot_sa"][s][1])
        last = None
        for k in range(NDC):
            if chase:
                C.wait(T, "T", "sV", sv_src[k])
            T.matmul(ps[:, 0:512], wtab[k][:, ocA * 128:(ocA + 1) * 128],
                     src_sb[:, k * 512:(k + 1) * 512],
                     start=(k == 0), stop=(k == NDC - 1))
            last = T.matmul(ps[:, 512:1024], wtab[k][:, ocB * 128:(ocB + 1) * 128],
                            src_sb[:, k * 512:(k + 1) * 512],
                            start=(k == 0), stop=(k == NDC - 1))
        return s, C.inc(last, "sT")

    def evac_to_x1(s, st_win, hslot, ebA, ebB, jpar):
        xv = x1vs[jpar]
        C.wait(A, "A", "sT", st_win)
        C.wait(A, "A", "sT", gd["st_conv3p"][jpar])
        C.wait(A, "A", "sV", gd["sv_haloout"])
        A.activation(xv[:, 0:1, 2:514], slots[s][:, 0:512], AF.Identity,
                     bias=binp[:, ebA:ebA + 1])
        sa = C.inc(A.activation(xv[:, 1:2, 2:514], slots[s][:, 512:1024], AF.Identity,
                                bias=binp[:, ebB:ebB + 1]), "sA")
        C.wait(V, "V", "sA", sa)
        C.wait(V, "V", "sT", gd["st_conv3p"][jpar])
        vwait_self()
        vop(V.tensor_copy(xv[:, :, 0:2],
                          x1h.ap()[:, hslot * 4:hslot * 4 + 4].rearrange(
                              "p (w c) -> p w c", w=2)))
        sv_hin = C.cnt["sV"]
        vop(V.tensor_copy(x1h.ap()[:, hslot * 4:hslot * 4 + 4].rearrange(
            "p (w c) -> p w c", w=2), xv[:, :, 512:514]))
        gd["sv_haloout"] = C.cnt["sV"]
        return sa, sv_hin

    def conv3_pair(s, ocA, ocB, sa_x1, sv_hin, jpar):
        par = gd["dgA_par"] = 1 - gd["dgA_par"]
        C.wait(V, "V", "sT", gd["dgA_mark"][par])
        vwait_self()
        for half, oc in ((0, ocA), (1, ocB)):
            for k in range(3):
                vop(V.tensor_scalar(dgA[par * 6 + half * 3 + k][:, :], eyeF[:, :],
                                    swp[:, oc * 3 + k:oc * 3 + k + 1], None, op0=ALU.mult))
        sv_dg = C.cnt["sV"]
        ps = slots[s]
        C.wait(T, "T", "sA", sa_x1)
        C.wait(T, "T", "sV", max(sv_dg, gd["sv_haloout"], sv_hin, sv_p0))
        last = None
        for half, oc in ((0, ocA), (1, ocB)):
            for k in range(3):
                last = T.matmul(ps[:, half * 512:(half + 1) * 512],
                                dgA[par * 6 + half * 3 + k][:, :],
                                x1pairs[jpar][:, half * 514 + k:half * 514 + k + 512],
                                start=(k == 0), stop=(k == 2))
        st = C.inc(last, "sT")
        gd["st_conv3p"][jpar] = st
        gd["dgA_mark"][par] = st
        return st

    def evac_to_convpair(s, st_c3, btab=None, ebA=0, ebB=0):
        C.wait(A, "A", "sT", st_c3)
        C.wait(A, "A", "sP", gd["sp_convpair"])
        if btab is None:
            sa = C.inc(A.copy(convpair[:, :], slots[s][:, :]), "sA")
        else:
            A.activation(convpair[:, 0:512], slots[s][:, 0:512], AF.Identity,
                         bias=btab[:, ebA:ebA + 1])
            sa = C.inc(A.activation(convpair[:, 512:1024], slots[s][:, 512:1024],
                                    AF.Identity, bias=btab[:, ebB:ebB + 1]), "sA")
        gd["slot_sa"][s] = ("sA", sa)
        return sa

    sv_ln1_pending = [None]

    for t in range(nt):
        t0 = t * TILE
        if t == 0:
            C.wait(V, "V", "qU", qs_u[0])
            s_st = next_slot()
            for piece in range(4):
                ln_stats_piece(ut.ap(), piece, s_st, piece == 0, piece == 3)
            s_bc = next_slot()
            ln_rows_bcast(s_st, s_bc)
            sv_ln1 = ln_apply(ut)
        else:
            sv_ln1 = sv_ln1_pending[0]

        ln1_dc = gd["sv_apply_dc"]
        jobs = [("kv", i) for i in range(NDC)] + [("q", j) for j in range(4)]

        def job_win(idx, job):
            kind, i = job
            if kind == "kv":
                ocA, ocB, hslot = 8 + i, 16 + i, i
            else:
                ocA, ocB, hslot = 2 * i, 2 * i + 1, 8 + i
            s, st_w = win_pair(ocA, ocB, ln1_dc if idx == 0 else sv_ln1, wi16, xa)
            gd["st_xa_read"] = st_w
            return [s, ocA, ocB, st_w, hslot, idx % 2]

        def job_evac(hd):
            s, ocA, ocB, st_w, hslot, jpar = hd
            sa_x1, sv_hin = evac_to_x1(s, st_w, hslot, ocA, ocB, jpar)
            hd[3] = (sa_x1, sv_hin)

        def job_tail1(job, hd):
            kind, i = job
            s, ocA, ocB, (sa_x1, sv_hin), hslot, jpar = hd
            st_c3 = conv3_pair(s, ocA, ocB, sa_x1, sv_hin, jpar)
            sa_cp = evac_to_convpair(s, st_c3, sbp, ocA, ocB)
            if kind == "kv":
                par = i % 2
                kv = kvts[par]
                C.wait(P, "P", "sA", sa_cp)
                C.wait(P, "P", "sT", gd["st_conv4p"][par])
                C.wait(P, "P", "sV", gd["sv_kvh"])
                sp_kv = C.inc(P.tensor_mul(kv[:, 3:515], convpair[:, 0:512],
                                           convpair[:, 512:1024]), "sP")
                gd["sp_convpair"] = sp_kv
                C.wait(V, "V", "sP", sp_kv)
                C.wait(V, "V", "sT", gd["st_conv4p"][par])
                vwait_self()
                vop(V.tensor_copy(kv[:, 0:3], kvh[:, i * 3:i * 3 + 3]))
                vop(V.tensor_copy(kvh[:, i * 3:i * 3 + 3], kv[:, 512:515]))
                gd["sv_kvh"] = C.cnt["sV"]
                dpar = gd["dgB_par"] = 1 - gd["dgB_par"]
                C.wait(V, "V", "sT", gd["dgB_mark"][dpar])
                vwait_self()
                for k in range(4):
                    vop(V.tensor_scalar(dgB[dpar * 4 + k][:, :], eyeF[:, :],
                                        iwp[:, i * 4 + k:i * 4 + k + 1], None,
                                        op0=ALU.mult))
                gd["kv_aux"][i] = (par, dpar, C.cnt["sV"], sp_kv)
            else:
                C.wait(P, "P", "sA", sa_cp)
                C.wait(P, "P", "sA", gd["sa_vv_last"])
                C.wait(P, "P", "sT", gd["st_wout_last"])
                P.tensor_mul(y16[:, ocA * 512:(ocA + 1) * 512], convpair[:, 0:512],
                             vvt[:, ocA * 512:(ocA + 1) * 512])
                spy = C.inc(P.tensor_mul(y16[:, ocB * 512:(ocB + 1) * 512],
                                         convpair[:, 512:1024],
                                         vvt[:, ocB * 512:(ocB + 1) * 512]), "sP")
                gd["sp_convpair"] = spy
                gd["sp_y_last"] = spy

        def job_tail2(job):
            kind, i = job
            par, dpar, sv_dg4, sp_kv = gd["kv_aux"][i]
            kv = kvts[par]
            C.wait(T, "T", "sP", sp_kv)
            C.wait(T, "T", "sV", max(sv_dg4, gd["sv_kvh"]))
            C.wait(T, "T", "sA", gd["sa_vv_last"])
            C.wait(T, "T", "sV", gd["sv_argS"])
            last = None
            for k in range(4):
                last = T.matmul(psS[:, 0:512], dgB[dpar * 4 + k][:, :], kv[:, k:k + 512],
                                start=(k == 0), stop=(k == 3))
            st_c4 = C.inc(last, "sT")
            gd["st_conv4p"][par] = st_c4
            gd["dgB_mark"][dpar] = st_c4
            C.wait(A, "A", "sT", st_c4)
            gd["sa_vv_last"] = C.inc(
                A.activation(vvt[:, i * 512:(i + 1) * 512], psS[:, 0:512],
                             AF.Identity, bias=ibp[:, i:i + 1]), "sA")

        heads = {}
        for idx, job in enumerate(jobs):
            heads[idx] = job_win(idx, job)
            if idx >= 1:
                job_tail1(jobs[idx - 1], heads[idx - 1])
            if idx >= 2 and jobs[idx - 2][0] == "kv":
                job_tail2(jobs[idx - 2])
            job_evac(heads[idx])
        job_tail1(jobs[-1], heads[len(jobs) - 1])
        sp_y = gd["sp_y_last"]

        sp_r1 = 0
        st_wo = 0
        for j in range(4):
            ocA, ocB = 2 * j, 2 * j + 1
            s = next_slot()
            ps = slots[s]
            C.wait(T, "T", "sP", sp_y)
            C.wait(T, "T", gd["slot_sa"][s][0], gd["slot_sa"][s][1])
            last = None
            for k in range(NDC):
                T.matmul(ps[:, 0:512], wo16[k][:, ocA * 128:(ocA + 1) * 128],
                         y16[:, k * 512:(k + 1) * 512], start=(k == 0), stop=(k == NDC - 1))
                last = T.matmul(ps[:, 512:1024], wo16[k][:, ocB * 128:(ocB + 1) * 128],
                                y16[:, k * 512:(k + 1) * 512], start=(k == 0), stop=(k == NDC - 1))
            st_wo = C.inc(last, "sT")
            C.wait(A, "A", "sT", st_wo)
            C.wait(A, "A", "sP", gd["sp_convpair"])
            A.activation(convpair[:, 0:512], ps[:, 0:512], AF.Identity,
                         bias=boutp[:, ocA:ocA + 1])
            sa_o = C.inc(A.activation(convpair[:, 512:1024], ps[:, 512:1024],
                                      AF.Identity, bias=boutp[:, ocB:ocB + 1]), "sA")
            gd["slot_sa"][s] = ("sA", sa_o)
            C.wait(P, "P", "sA", sa_o)
            C.wait(P, "P", "qSPR", gd["qs_r1dma"])
            C.wait(P, "P", "sV", gd["sv_ln2_apply"])
            sp_r1 = C.inc(P.tensor_add(r1sb[:, ocA * 512:(ocB + 1) * 512],
                                       convpair[:, :],
                                       ut[:, ocA * 512:(ocB + 1) * 512]), "sP")
            gd["sp_convpair"] = sp_r1
            C.wait(SY, "S", "sP", sp_r1)
            C.inc(SY.dma_start(
                out=r1_d[ocA:ocB + 1, :, t0:t0 + TILE].rearrange("c p t -> p c t"),
                in_=r1sb.ap()[:, ocA * 512:(ocB + 1) * 512].rearrange(
                    "p (c t) -> p c t", c=2)), "qSPR", 16)
            C.wait(V, "V", "sP", sp_r1)
            ln_stats_piece(r1sb.ap(), j, None, j == 0, j == 3)
        gd["st_wout_last"] = st_wo
        gd["qs_r1dma"] = C.cnt["qSPR"]
        sp_r1_done = sp_r1

        if t + 1 < nt:
            C.wait(SY, "S", "sP", sp_r1_done)
            C.inc(SY.dma_start(out=ut3, in_=u_rearr[:, :, (t + 1) * TILE:(t + 2) * TILE]),
                  "qU", 16)
            qs_u[t + 1] = C.cnt["qU"]

        if not keff_done[0]:
            keff_done[0] = True
            keff_mark = 0
            s = next_slot()
            ps = slots[s]
            first = True
            for ph in range(16):
                C.wait(T, "T", "qWQ", C.cnt["qWQ"])
                for dc in range(NDC):
                    if first:
                        C.wait(T, "T", gd["slot_sa"][s][0], gd["slot_sa"][s][1])
                    else:
                        C.wait(T, "T", "sA", gd["sa_keff"])
                    first = False
                    last = None
                    for dkc in range(4):
                        last = T.matmul(ps[:, 0:128],
                                        wqb[dkc][:, dc * 128:(dc + 1) * 128],
                                        ktb[ph % 2][:, dkc * 128:(dkc + 1) * 128],
                                        start=(dkc == 0), stop=(dkc == 3))
                    stk = C.inc(last, "sT")
                    if dc == NDC - 1:
                        keff_mark = stk
                    C.wait(A, "A", "sT", stk)
                    gd["sa_keff"] = C.inc(
                        A.copy(keff[dc][:, ph * 128:(ph + 1) * 128], ps[:, 0:128]), "sA")
                if ph + 1 < 16:
                    C.wait(G, "P", "sT", keff_mark)
                    for dkc in range(4):
                        r0 = (ph + 1) * 512 + dkc * 128
                        C.inc(G.dma_start(out=wqb[dkc][:, :],
                                          in_=wq_d[r0:r0 + 128, :]), "qWQ", 16)
                    C.inc(G.dma_start(
                        out=ktb[(ph + 1) % 2].ap().rearrange("p (c k) -> p c k", c=4),
                        in_=kt_d[ph + 1:ph + 2, :, :].rearrange(
                            "s (c p) k -> p (s c) k", p=128)), "qWQ", 16)
            gd["slot_sa"][s] = ("sA", gd["sa_keff"])

        s_bc2 = next_slot()
        ln_rows_bcast(None, s_bc2)
        sv_ln2 = ln_apply(r1sb)
        ln2_dc = gd["sv_apply_dc"]
        gd["sv_ln2_apply"] = sv_ln2
        for j in range(4):
            ocA = 2 * j
            C.wait(SY, "S", "sV", sv_ln2)
            C.inc(SY.dma_start(
                out=x2_d[ocA:ocA + 2, :, t0:t0 + TILE].rearrange("c p t -> p c t"),
                in_=x2.ap()[:, ocA * 512:(ocA + 2) * 512].rearrange(
                    "p (c t) -> p c t", c=2)), "qSPX", 16)
        gd["qs_x2dma"] = C.cnt["qSPX"]

        for gi in range(4):
            g = t * 4 + gi
            go = gi * 128
            st_sim = {}
            for p in range(2):
                dstp = psS if p == 0 else psL
                if not (gi == 0 and p == 0):
                    C.wait(T, "T", "sV", sv_ln2)
                if p == 0:
                    C.wait(T, "T", "sV", gd["sv_argS"])
                    C.wait(T, "T", "sA", gd["sa_vv_last"])
                else:
                    C.wait(T, "T", "sV", gd["sv_argL"])
                last = None
                for half in range(2):
                    for dc in range(NDC):
                        if gi == 0 and p == 0 and half == 0:
                            C.wait(T, "T", "sV", ln2_dc[dc])
                        last = T.matmul(dstp[:, half * 512:(half + 1) * 512],
                                        x2[:, dc * 512 + go:dc * 512 + go + 128],
                                        keff[dc][:, p * 1024 + half * 512:p * 1024 + (half + 1) * 512],
                                        start=(dc == 0), stop=(dc == NDC - 1))
                st_sim[p] = C.inc(last, "sT")
            gd["st_x2_read"] = st_sim[1]
            for p in range(2):
                dstp = psS if p == 0 else psL
                ma = m0a if p == 0 else m1a
                ia = i0a if p == 0 else i1a
                C.wait(V, "V", "sT", st_sim[p])
                vwait_self()
                for hh in range(H):
                    vop(V.max(ma[:, hh * 8:hh * 8 + 8], dstp[:, hh * 128:(hh + 1) * 128]))
                vwait_self()
                for hh in range(H):
                    vop(V.max_index(ia[:, hh * 8:hh * 8 + 8], ma[:, hh * 8:hh * 8 + 8],
                                    dstp[:, hh * 128:(hh + 1) * 128]))
                if p == 0:
                    gd["sv_argS"] = C.cnt["sV"]
                else:
                    gd["sv_argL"] = C.cnt["sV"]
            vwait_self()
            m0s = m0a.ap().rearrange("p (h e) -> p h e", e=8)[:, :, 0:1]
            m1s = m1a.ap().rearrange("p (h e) -> p h e", e=8)[:, :, 0:1]
            i0s = i0a.ap().rearrange("p (h e) -> p h e", e=8)[:, :, 0:1]
            i1s = i1a.ap().rearrange("p (h e) -> p h e", e=8)[:, :, 0:1]
            vop(V.tensor_add(sc8[:, 16:24], m0s, m1s))
            vwait_self()
            vop(V.tensor_scalar_max(gate_all[:, g * 8:(g + 1) * 8], sc8[:, 16:24], 0.0))
            vop(V.tensor_copy(sc8[:, 0:8], i0s))
            vop(V.tensor_copy(sc8[:, 8:16], i1s))
            vwait_self()
            vop(V.scalar_tensor_tensor(sc8[:, 16:24], sc8[:, 0:8], 128.0,
                                       sc8[:, 8:16], op0=ALU.mult, op1=ALU.add))
            vwait_self()
            vop(V.tensor_copy(idx_all[:, g * 8:(g + 1) * 8], sc8[:, 16:24]))

        if t + 1 < nt:
            C.wait(V, "V", "qU", qs_u[t + 1])
            s_st = next_slot()
            for piece in range(4):
                ln_stats_piece(ut.ap(), piece, s_st, piece == 0, piece == 3)
            s_bc = next_slot()
            ln_rows_bcast(s_st, s_bc)
            sv_ln1_pending[0] = ln_apply(ut, extra_T_wait=gd["st_x2_read"])

        for cc in (2 * t, 2 * t + 1):
            if cc < NCV:
                conv_chunk(cc)

    for cc in range(2 * nt, NCV):
        conv_chunk(cc)

    C.wait(G, "P", "qWC", C.cnt["qWC"])
    with G.If_eq(pid1, 0):
        G.dma_start(out=shf[0:1, :], in_=flone[0:1, :]).then_inc(C.sems["qWC"], 16)
    with G.Else():
        G.dma_start(out=shf[1:2, :], in_=flone[0:1, :]).then_inc(C.sems["qWC"], 16)
    C.cnt["qWC"] += 16
    C.wait(G, "P", "qWC", C.cnt["qWC"])
    with nc.gpsimd.register("pcnt") as pcnt, nc.gpsimd.register("pcond") as pcond:
        G.reg_mov(pcnt, 0)
        G.reg_mov(pcond, 1)
        with G.While(pcond):
            G.reg_add(pcnt, pcnt, 16)
            with G.If_eq(pid1, 0):
                G.dma_start(out=flrd[0:1, 0:1], in_=shf[1:2, 0:1]).then_inc(C.sems["qPF"], 16)
            with G.Else():
                G.dma_start(out=flrd[0:1, 0:1], in_=shf[0:1, 0:1]).then_inc(C.sems["qPF"], 16)
            G.wait_ge(C.sems["qPF"], pcnt)
            G.reg_load(pcond, flrd[0:1, 0:1])
            G.reg_alu(pcond, pcond, 1, ALU.bitwise_xor)

    endT, endV, endA, endP = C.cnt["sT"], C.cnt["sV"], C.cnt["sA"], C.cnt["sP"]

    p1_ctx.close()
    ps_ctx.close()

    b2_ctx = ExitStack()
    b2 = b2_ctx.enter_context
    wif16 = [b2(nc.sbuf_tensor(f"wif16_{k}", [128, DI], BF16)) for k in range(NDC)]
    wof16 = [b2(nc.sbuf_tensor(f"wof16_{k}", [128, D], BF16)) for k in range(NIC)]
    ainb0 = [b2(nc.sbuf_tensor(f"ainb_{h}", [128, 1024], BF16)) for h in range(H)]
    binb0 = [b2(nc.sbuf_tensor(f"binb_{h}", [128, 2048], BF16)) for h in range(H)]
    aoutb0 = [b2(nc.sbuf_tensor(f"aoutb_{h}", [128, 2048], BF16)) for h in range(H)]
    boutb0 = [b2(nc.sbuf_tensor(f"boutb_{h}", [128, 1024], BF16)) for h in range(H)]
    ainb = [ainb0, ainb0]
    binb = [binb0, binb0]
    aoutb = [aoutb0, aoutb0]
    boutb = [boutb0, boutb0]
    x2g2 = [b2(nc.sbuf_tensor(f"x2g_{k}", [128, 1024], BF16)) for k in range(2)]
    r1g = b2(nc.sbuf_tensor("r1g", [128, 1024], F32))
    xtok = b2(nc.sbuf_tensor("xtok", [128, 1024], BF16))
    htok = b2(nc.sbuf_tensor("htok", [128, 2048], BF16))
    hfm = b2(nc.sbuf_tensor("hfm", [128, 2048], BF16))
    pj4 = b2(nc.sbuf_tensor("pj4", [128, 2048], BF16))
    pjin = b2(nc.sbuf_tensor("pjin", [128, 1024], BF16))
    linr = b2(nc.sbuf_tensor("linr", [128, 16], F32))
    ling = b2(nc.sbuf_tensor("ling", [128, 16], F32))
    dgb = [b2(nc.sbuf_tensor(f"dgb_{h}", [128, 128], BF16)) for h in range(2 * H)]
    outg = b2(nc.sbuf_tensor("outg", [128, 1024], F32))

    psH = b2(nc.psum_tensor("psH", [128, 2048], F32))
    psO = b2(nc.psum_tensor("psO", [128, 1024], F32))
    psT2 = b2(nc.psum_tensor("psT2", [128, 512], F32))
    psT3 = b2(nc.psum_tensor("psT3", [128, 512], F32))

    C.wait(G, "P", "sT", endT)
    C.wait(G, "P", "sV", endV)
    C.wait(G, "P", "sA", endA)
    C.wait(G, "P", "sP", endP)
    for k in range(NDC):
        C.inc(G.dma_start(out=wif16[k][:, :], in_=wif_d[k * 128:(k + 1) * 128, :]), "qWF", 16)
    for k in range(NIC):
        C.inc(G.dma_start(out=wof16[k][:, :], in_=wof_d[k * 128:(k + 1) * 128, :]), "qWF", 16)
    qg_ffnw = C.cnt["qWF"]
    with G.register("stag") as stag:
        pv2 = G.partition_id()
        G.reg_alu(stag, pv2, 800, ALU.mult)
        with G.While(stag):
            G.reg_alu(stag, stag, 1, ALU.subtract)
    C.wait(SY, "S", "sT", endT)
    C.wait(SY, "S", "sA", endA)
    C.wait(SY, "S", "sV", endV)
    C.wait(SY, "S", "sP", endP)
    C.wait(T, "T", "sV", endV)
    C.wait(T, "T", "sA", endA)
    C.wait(A, "A", "sT", endT)
    C.wait(A, "A", "sV", endV)
    C.wait(V, "V", "sT", endT)
    C.wait(V, "V", "sA", endA)
    C.wait(P, "P", "sT", endT)

    bg = dict(st_bin=0, st_bout=0, sa_gelu=0, sa_out=0, qs_y=0,
              sa_xtok=0, sa_hfm=[0, 0, 0, 0], sv_amul_in=0, sv_amul_out=0,
              sv_ling_in=0, sv_ling_out=0, sa_red_in=[0] * H, sa_red_out=[0] * H)

    GOFF = {0: (0, 1024, ainb, "qAI"), 1: (1024, 2048, binb, "qBI"),
            2: (3072, 2048, aoutb, "qAO"), 3: (5120, 1024, boutb, "qBO")}
    for qn in ("qAI", "qBI", "qAO", "qBO"):
        C.sem(qn)

    def emit_gathers(g, which, gate_sem, gate_val):
        off, width, bufs, qn = GOFF[which]
        C.wait(G, "P", gate_sem, gate_val)
        for h in range(H):
            C.inc(G.indirect_dma_start(
                bufs[g % 2][h][:, 0:width], None,
                comb16[:, :],
                IndirectOffsetOnAxis(idx_all.ap()[:, g * 8 + h:g * 8 + h + 1], 0),
                element_offset=off), qn, 16)
        return C.cnt[qn]

    qai = {}
    qbi = {}
    qao = {}
    qbo = {}
    qai[0] = qai[1] = emit_gathers(0, 0, "sT", 0)
    qbi[0] = qbi[1] = emit_gathers(0, 1, "sT", 0)
    qao[0] = qao[1] = emit_gathers(0, 2, "sT", 0)
    qbo[0] = qbo[1] = emit_gathers(0, 3, "sT", 0)
    x2g3 = [x2g2[k].ap().rearrange("p (c t) -> p c t", c=NDC) for k in range(2)]
    r1g3 = r1g.ap().rearrange("p (c t) -> p c t", c=NDC)
    C.inc(SY.dma_start(out=x2g3[0], in_=x2_d[:, :, 0:128].rearrange("c p t -> p c t")),
          "qX", 16)
    qx = C.cnt["qX"]
    C.inc(SY.dma_start(out=r1g3, in_=r1_d[:, :, 0:128].rearrange("c p t -> p c t")),
          "qR", 16)
    qr = C.cnt["qR"]

    for g in range(NG):
        par = g % 2
        x2g = x2g2[par]
        C.wait(T, "T", "qX", qx)
        for half in range(2):
            pst = psT2 if half == 0 else psT3
            C.wait(T, "T", "sA", max(bg["sa_xtok"], bg["sa_hfm"][half + 2]))
            last = None
            for c in range(4):
                cc = half * 4 + c
                last = T.matmul(pst[:, c * 128:(c + 1) * 128],
                                x2g[:, cc * 128:(cc + 1) * 128], eye16[:, :],
                                start=True, stop=True)
            st = C.inc(last, "sT")
            C.wait(A, "A", "sT", st)
            C.wait(A, "A", "sV", bg["sv_amul_in"])
            bg["sa_xtok"] = C.inc(
                A.copy(xtok[:, half * 512:(half + 1) * 512], pst[:, :]), "sA")

        C.wait(V, "V", "qAI", qai[par])
        C.wait(V, "V", "sA", bg["sa_xtok"])
        for h in range(H):
            vwait_self()
            vop(V.scalar_tensor_tensor(pjin[:, :], ainb[par][h][:, :], 1.0,
                                       xtok[:, :], op0=ALU.mult, op1=ALU.mult,
                                       accum_out=linr[:, h:h + 1]))
        bg["sv_amul_in"] = C.cnt["sV"]
        if g + 1 < NG:
            qai[0] = qai[1] = emit_gathers(g + 1, 0, "sV", bg["sv_amul_in"])
        vwait_self()
        vop(V.tensor_mul(ling[:, 0:8], linr[:, 0:8], gate_all[:, g * 8:(g + 1) * 8]))
        bg["sv_ling_in"] = C.cnt["sV"]
        vwait_self()
        for h in range(H):
            C.wait(V, "V", "sT", bg["st_bin"])
            vop(V.tensor_scalar(dgb[h][:, :], eyeF[:, :], ling[:, h:h + 1], None,
                                op0=ALU.mult))
        sv_dgin = C.cnt["sV"]

        C.wait(T, "T", "qWF", qg_ffnw)
        C.wait(T, "T", "qBI", qbi[par])
        C.wait(T, "T", "sV", sv_dgin)
        C.wait(T, "T", "sA", bg["sa_gelu"])
        n_contrib = NDC + H
        ci = 0
        last1 = None
        for dc in range(NDC):
            lhs = x2g[:, dc * 128:(dc + 1) * 128]
            for qtr in range(4):
                last1 = T.matmul(psH[:, qtr * 512:(qtr + 1) * 512], lhs,
                                 wif16[dc][:, qtr * 512:(qtr + 1) * 512],
                                 start=(ci == 0), stop=(ci == n_contrib - 1))
            ci += 1
        for h in range(H):
            for qtr in range(4):
                last1 = T.matmul(psH[:, qtr * 512:(qtr + 1) * 512], dgb[h][:, :],
                                 binb[par][h][:, qtr * 512:(qtr + 1) * 512],
                                 start=(ci == 0), stop=(ci == n_contrib - 1))
            ci += 1
        bg["st_bin"] = C.inc(last1, "sT")
        if g + 1 < NG:
            qbi[0] = qbi[1] = emit_gathers(g + 1, 1, "sT", bg["st_bin"])
            gsl = slice((g + 1) * 128, (g + 2) * 128)
            C.wait(SY, "S", "sT", bg["st_bin_prev"] if "st_bin_prev" in bg else 0)
            C.inc(SY.dma_start(out=x2g3[(g + 1) % 2],
                               in_=x2_d[:, :, gsl].rearrange("c p t -> p c t")), "qX", 16)
            qx = C.cnt["qX"]
        bg["st_bin_prev"] = bg["st_bin"]

        C.wait(A, "A", "sT", bg["st_bin"])
        C.wait(A, "A", "sV", bg["sv_amul_out"])
        bg["sa_gelu"] = C.inc(A.activation(htok[:, :], psH[:, :], AF.Gelu), "sA")

        C.wait(T, "T", "sA", bg["sa_gelu"])
        for half in range(4):
            pst = psT2 if half % 2 == 0 else psT3
            C.wait(T, "T", "sA",
                   bg["sa_xtok"] if half < 2 else bg["sa_hfm"][half - 2])
            last = None
            for c in range(4):
                cc = half * 4 + c
                last = T.matmul(pst[:, c * 128:(c + 1) * 128],
                                htok[:, cc * 128:(cc + 1) * 128], eye16[:, :],
                                start=True, stop=True)
            st = C.inc(last, "sT")
            C.wait(A, "A", "sT", st)
            C.wait(A, "A", "sT", bg["st_bout"])
            bg["sa_hfm"][half] = C.inc(
                A.copy(hfm[:, half * 512:(half + 1) * 512], pst[:, :]), "sA")

        C.wait(V, "V", "qAO", qao[par])
        C.wait(V, "V", "sA", bg["sa_gelu"])
        for h in range(H):
            vwait_self()
            vop(V.scalar_tensor_tensor(pj4[:, :], aoutb[par][h][:, :], 1.0,
                                       htok[:, :], op0=ALU.mult, op1=ALU.mult,
                                       accum_out=linr[:, 8 + h:9 + h]))
        bg["sv_amul_out"] = C.cnt["sV"]
        if g + 1 < NG:
            qao[0] = qao[1] = emit_gathers(g + 1, 2, "sV", bg["sv_amul_out"])
        vwait_self()
        vop(V.tensor_mul(ling[:, 8:16], linr[:, 8:16], gate_all[:, g * 8:(g + 1) * 8]))
        bg["sv_ling_out"] = C.cnt["sV"]
        vwait_self()
        for h in range(H):
            C.wait(V, "V", "sT", bg["st_bout"])
            vop(V.tensor_scalar(dgb[8 + h][:, :], eyeF[:, :], ling[:, 8 + h:9 + h], None,
                                op0=ALU.mult))
        sv_dgout = C.cnt["sV"]

        C.wait(T, "T", "sV", sv_dgout)
        C.wait(T, "T", "qBO", qbo[par])
        C.wait(T, "T", "qR", qr)
        C.wait(T, "T", "sA", bg["sa_out"])
        last = None
        for c in range(NIC):
            for hf in range(2):
                last = T.matmul(psO[:, hf * 512:(hf + 1) * 512],
                                hfm[:, c * 128:(c + 1) * 128],
                                wof16[c][:, hf * 512:(hf + 1) * 512],
                                start=(c == 0), stop=False)
        for h in range(H):
            for hf in range(2):
                last = T.matmul(psO[:, hf * 512:(hf + 1) * 512], dgb[8 + h][:, :],
                                boutb[par][h][:, hf * 512:(hf + 1) * 512],
                                start=False, stop=False)
        for dc in range(NDC):
            last = T.matmul(psO[:, dc * 128:(dc + 1) * 128],
                            r1g[:, dc * 128:(dc + 1) * 128], eyeF[:, :],
                            start=False, stop=(dc == 3 or dc == NDC - 1))
        bg["st_bout"] = C.inc(last, "sT")
        if g + 1 < NG:
            qbo[0] = qbo[1] = emit_gathers(g + 1, 3, "sT", bg["st_bout"])
            gsl = slice((g + 1) * 128, (g + 2) * 128)
            C.wait(SY, "S", "sT", bg["st_bout"])
            C.inc(SY.dma_start(out=r1g3,
                               in_=r1_d[:, :, gsl].rearrange("c p t -> p c t")), "qR", 16)
            qr = C.cnt["qR"]

        C.wait(A, "A", "sT", bg["st_bout"])
        C.wait(A, "A", "qY", bg["qs_y"])
        bg["sa_out"] = C.inc(A.copy(outg[:, :], psO[:, :]), "sA")
        C.wait(SY, "S", "sA", bg["sa_out"])
        C.inc(SY.dma_start(out=y_d[g * 128:(g + 1) * 128, :], in_=outg[:, :]), "qY", 16)
        bg["qs_y"] = C.cnt["qY"]

    if debug:
        C.wait(SY, "S", "sV", C.cnt["sV"])
        C.inc(SY.dma_start(out=idx_d[:, :], in_=idx_all[:, :]), "qY", 16)
        C.inc(SY.dma_start(out=gate_d[:, :], in_=gate_all[:, :]), "qY", 16)

    for s in ("qC", "qU", "qSPR", "qSPX", "qX", "qR", "qY"):
        C.wait(SY, "S", s, C.cnt[s])
    for s in ("qWA", "qWQ", "qWF", "qAI", "qBI", "qAO", "qBO", "qWC"):
        C.wait(G, "P", s, C.cnt[s])

    b2_ctx.close()
    ctx.close()
    return nc


def _host_prep(inputs):
    u = inputs["L0_attn_u"]
    comb = np.concatenate([inputs["L0_ffn_in_a"], inputs["L0_ffn_in_b"],
                           inputs["L0_ffn_out_a"], inputs["L0_ffn_out_b"]], axis=1)
    comb = np.ascontiguousarray(comb)
    keys = inputs["L0_ffn_keys"]
    kt = np.ascontiguousarray(keys.transpose(2, 0, 3, 1).reshape(16, 512, 128))
    base = {
        "combt": comb,
        "wi": np.ascontiguousarray(inputs["L0_attn_W_in"]),
        "wo": np.ascontiguousarray(inputs["L0_attn_W_out"]),
        "wq": np.ascontiguousarray(inputs["L0_ffn_Wq"]),
        "kt": kt,
        "wif": np.ascontiguousarray(inputs["L0_ffn_W_in"].T),
        "wof": np.ascontiguousarray(inputs["L0_ffn_W_out"].T),
        "binp": np.ascontiguousarray(inputs["L0_attn_b_in"].reshape(NOC, 128).T),
        "boutp": np.ascontiguousarray(inputs["L0_attn_b_out"].reshape(NDC, 128).T),
        "sbp": np.ascontiguousarray(inputs["L0_attn_short_b"].reshape(NOC, 128).T),
        "ibp": np.ascontiguousarray(inputs["L0_attn_inner_b"].reshape(NDC, 128).T),
        "swp": np.ascontiguousarray(
            inputs["L0_attn_short_w"].reshape(NOC, 128, 3).transpose(1, 0, 2).reshape(128, NOC * 3)),
        "iwp": np.ascontiguousarray(
            inputs["L0_attn_inner_w"].reshape(NDC, 128, 4).transpose(1, 0, 2).reshape(128, NDC * 4)),
    }
    u_fm = np.ascontiguousarray(u.transpose(0, 2, 1))
    return base, u_fm


def kernel(**inputs):
    base, u_fm = _host_prep(inputs)
    nc = build_nc(nt=8, debug=False)
    in_maps = [dict(base, u=u_fm[c]) for c in range(8)]
    res = run_bass_kernel_spmd(nc, in_maps, core_ids=list(range(8)))
    out = np.stack([np.asarray(r["yout"]) for r in res.results], axis=0)
    return out.astype(np.float32, copy=False)


# version 14
# speedup vs baseline: 1.4537x; 1.2257x over previous
import numpy as np
import concourse.bass as bass
import concourse.mybir as mybir
from concourse.bass_utils import run_bass_kernel_spmd
from concourse.bass import IndirectOffsetOnAxis
from contextlib import ExitStack

F32 = mybir.dt.float32
BF16 = mybir.dt.bfloat16
I32 = mybir.dt.int32
U32 = mybir.dt.uint32
AF = mybir.ActivationFunctionType
ALU = mybir.AluOpType

L = 4096
TILE = 512
GRP = 128
D = 1024
NDC = 8
D3 = 3072
NOC = 24
DI = 2048
NIC = 16
H = 8
NE = 16384
EPS = 1e-5


class Ctr:
    def __init__(self, nc):
        self.nc = nc
        self.sems = {}
        self.cnt = {}
        self.marks = {}

    def sem(self, name):
        s = self.nc.semaphore(name).__enter__()
        self.sems[name] = s
        self.cnt[name] = 0

    def inc(self, instr, name, by=1):
        self.cnt[name] += by
        instr.then_inc(self.sems[name], by)
        return self.cnt[name]

    def wait(self, eng, ename, name, val):
        if val <= 0:
            return
        key = (ename, name)
        if self.marks.get(key, -1) >= val:
            return
        eng.wait_ge(self.sems[name], val)
        self.marks[key] = val


def build_nc(nt=8, debug=False):
    nc = bass.Bass("TRN2", target_bir_lowering=False, debug=False,
                   detect_race_conditions=False)
    NG = (nt * TILE) // GRP

    u_d = nc.dram_tensor("u", [D, L], F32, kind="ExternalInput").ap()
    comb_d = nc.dram_tensor("combt", [NE, 6144], F32, kind="ExternalInput").ap()
    wi_d = nc.dram_tensor("wi", [D, D3], F32, kind="ExternalInput").ap()
    wo_d = nc.dram_tensor("wo", [D, D], F32, kind="ExternalInput").ap()
    wq_d = nc.dram_tensor("wq", [8192, D], F32, kind="ExternalInput").ap()
    kt_d = nc.dram_tensor("kt", [16, 512, 128], F32, kind="ExternalInput").ap()
    wif_d = nc.dram_tensor("wif", [D, DI], F32, kind="ExternalInput").ap()
    wof_d = nc.dram_tensor("wof", [DI, D], F32, kind="ExternalInput").ap()
    bin_d = nc.dram_tensor("binp", [128, NOC], F32, kind="ExternalInput").ap()
    bout_d = nc.dram_tensor("boutp", [128, NDC], F32, kind="ExternalInput").ap()
    sb_d = nc.dram_tensor("sbp", [128, NOC], F32, kind="ExternalInput").ap()
    ib_d = nc.dram_tensor("ibp", [128, NDC], F32, kind="ExternalInput").ap()
    sw_d = nc.dram_tensor("swp", [128, NOC * 3], F32, kind="ExternalInput").ap()
    iw_d = nc.dram_tensor("iwp", [128, NDC * 4], F32, kind="ExternalInput").ap()
    eye_d = nc.inline_tensor(np.eye(128, dtype=np.float32), "eye128").ap()
    comb16 = nc.dram_tensor("comb16sh", [NE, 6144], BF16, kind="Internal",
                            addr_space="Shared").ap()
    shf = nc.dram_tensor("shflag", [2, 128], I32, kind="Internal",
                         addr_space="Shared").ap()

    y_d = nc.dram_tensor("yout", [L, D], F32, kind="ExternalOutput").ap()
    ik = "ExternalOutput" if debug else "Internal"
    r1_d = nc.dram_tensor("r1d", [NDC, 128, L], F32, kind=ik).ap()
    x2_d = nc.dram_tensor("x2d", [NDC, 128, L], BF16, kind=ik).ap()
    if debug:
        idx_d = nc.dram_tensor("idxd", [128, 256], I32, kind="ExternalOutput").ap()
        gate_d = nc.dram_tensor("gated", [128, 256], F32, kind="ExternalOutput").ap()

    ctx = ExitStack()
    e = ctx.enter_context
    C = Ctr(nc)
    for s in ("qC", "qU", "qSPR", "qSPX", "qX", "qR", "qY", "qWA", "qWQ",
              "qWF", "qGI", "qGO", "qWC", "qPF", "sV", "sA", "sP", "sT"):
        C.sem(s)
    SY, V, A, P, T, G = nc.sync, nc.vector, nc.scalar, nc.gpsimd, nc.tensor, nc.gpsimd

    def vop(instr):
        return C.inc(instr, "sV")

    def vwait_self():
        C.wait(V, "V", "sV", C.cnt["sV"])

    ones16 = e(nc.sbuf_tensor("ones16", [128, 1], BF16))
    ones32 = e(nc.sbuf_tensor("ones32", [1, 128], F32))
    eyeF = e(nc.sbuf_tensor("eyeF", [128, 128], F32))
    eye16 = e(nc.sbuf_tensor("eye16", [128, 128], BF16))
    swp = e(nc.sbuf_tensor("swp_s", [128, NOC * 3], F32))
    iwp = e(nc.sbuf_tensor("iwp_s", [128, NDC * 4], F32))
    binp = e(nc.sbuf_tensor("binp_s", [128, NOC], F32))
    boutp = e(nc.sbuf_tensor("boutp_s", [128, NDC], F32))
    sbp = e(nc.sbuf_tensor("sbp_s", [128, NOC], F32))
    ibp = e(nc.sbuf_tensor("ibp_s", [128, NDC], F32))
    cst = e(nc.sbuf_tensor("cst_ln", [128, 2], F32))
    flz = e(nc.sbuf_tensor("flz", [1, 128], I32))
    flone = e(nc.sbuf_tensor("flone", [1, 128], I32))
    flrd = e(nc.sbuf_tensor("flrd", [1, 4], I32))
    idx_all = e(nc.sbuf_tensor("idx_all", [128, 256], I32))
    gate_all = e(nc.sbuf_tensor("gate_all", [128, 256], F32))

    ps_ctx = ExitStack()
    pe = ps_ctx.enter_context
    psA = pe(nc.psum_tensor("psA", [128, 1024], F32))
    psB = pe(nc.psum_tensor("psB", [128, 1024], F32))
    psL = pe(nc.psum_tensor("psL", [128, 1024], F32))
    psS = pe(nc.psum_tensor("psS", [128, 1024], F32))
    slots = [psA, psB]

    p1_ctx = ExitStack()
    p1 = p1_ctx.enter_context
    m0a = p1(nc.sbuf_tensor("m0a", [128, 64], F32))
    m1a = p1(nc.sbuf_tensor("m1a", [128, 64], F32))
    i0a = p1(nc.sbuf_tensor("i0a", [128, 64], U32))
    i1a = p1(nc.sbuf_tensor("i1a", [128, 64], U32))
    sc8 = p1(nc.sbuf_tensor("sc8", [128, 24], F32))
    rowb = p1(nc.sbuf_tensor("rowb", [1, 1024], F32))
    rowt = p1(nc.sbuf_tensor("rowt", [1, 512], F32))
    lnb = p1(nc.sbuf_tensor("lnb", [128, 1024], F32))
    wi16 = [p1(nc.sbuf_tensor(f"wi16_{k}", [128, D3], BF16)) for k in range(NDC)]
    wo16 = [p1(nc.sbuf_tensor(f"wo16_{k}", [128, D], BF16)) for k in range(NDC)]
    keff = [p1(nc.sbuf_tensor(f"keff_{k}", [128, 2048], BF16)) for k in range(NDC)]
    wqb = [p1(nc.sbuf_tensor(f"wqb_{k}", [128, D], BF16)) for k in range(4)]
    ktb = [p1(nc.sbuf_tensor(f"ktb_{k}", [128, 512], BF16)) for k in range(2)]
    dgA = [p1(nc.sbuf_tensor(f"dgA_{k}", [128, 128], BF16)) for k in range(12)]
    dgB = [p1(nc.sbuf_tensor(f"dgB_{k}", [128, 128], BF16)) for k in range(8)]
    ut = p1(nc.sbuf_tensor("ut", [128, 8 * 512], F32))
    u16sq = p1(nc.sbuf_tensor("u16sq", [128, 4096], BF16))
    xa = p1(nc.sbuf_tensor("xa", [128, 8 * 512], BF16))
    x2 = xa
    x1pair = p1(nc.sbuf_tensor("x1pair", [128, 1028], BF16))
    x1pairB = p1(nc.sbuf_tensor("x1pairB", [128, 1028], BF16))
    x1h = p1(nc.sbuf_tensor("x1h", [128, 48], BF16))
    convpair = p1(nc.sbuf_tensor("convpair", [128, 1024], F32))
    kvt = p1(nc.sbuf_tensor("kvt", [128, 515], BF16))
    kvtB = p1(nc.sbuf_tensor("kvtB", [128, 515], BF16))
    kvh = p1(nc.sbuf_tensor("kvh", [128, NDC * 3], BF16))
    vvt = p1(nc.sbuf_tensor("vvt", [128, 8 * 512], BF16))
    y16 = p1(nc.sbuf_tensor("y16", [128, 8 * 512], BF16))
    r1sb = p1(nc.sbuf_tensor("r1sb", [128, 8 * 512], F32))

    for dst, srcd in ((swp, sw_d), (iwp, iw_d), (eyeF, eye_d), (binp, bin_d),
                      (boutp, bout_d), (sbp, sb_d), (ibp, ib_d)):
        C.inc(SY.dma_start(out=dst[:, :], in_=srcd[:, :]), "qC", 16)
    qs_consts = C.cnt["qC"]

    for k in range(NDC):
        C.inc(G.dma_start(out=wi16[k][:, :], in_=wi_d[k * 128:(k + 1) * 128, :]), "qWA", 16)
    for k in range(NDC):
        C.inc(G.dma_start(out=wo16[k][:, :], in_=wo_d[k * 128:(k + 1) * 128, :]), "qWA", 16)
    qg_attnw = C.cnt["qWA"]
    C.inc(G.dma_start(out=ktb[0].ap().rearrange("p (c k) -> p c k", c=4),
                      in_=kt_d[0:1, :, :].rearrange("s (c p) k -> p (s c) k", p=128)), "qWQ", 16)
    for dkc in range(4):
        C.inc(G.dma_start(out=wqb[dkc][:, :],
                          in_=wq_d[dkc * 128:dkc * 128 + 128, :]), "qWQ", 16)

    C.wait(V, "V", "qC", qs_consts)
    vop(V.memset(cst[:, 0:1], EPS))
    vop(V.memset(cst[:, 1:2], -0.5))
    nc.const_aps.register(F32, EPS, cst.ap()[:, 0:1])
    nc.const_aps.register(F32, -0.5, cst.ap()[:, 1:2])
    vop(V.memset(ones16[:, :], 1.0))
    vop(V.memset(ones32[:, :], 1.0))
    vop(V.tensor_copy(eye16[:, :], eyeF[:, :]))
    vop(V.memset(idx_all[:, :], 0))
    vop(V.memset(gate_all[:, :], 0.0))
    vop(V.memset(x1h[:, :], 0.0))
    vop(V.memset(kvt[:, :], 0.0))
    vop(V.memset(kvtB[:, :], 0.0))
    vop(V.memset(kvh[:, :], 0.0))
    vop(V.memset(flz[:, :], 0))
    vop(V.memset(flone[:, :], 1))
    sv_p0 = C.cnt["sV"]

    pid1 = nc.gpsimd.alloc_register("pid1")
    pv = G.partition_id()
    G.reg_alu(pid1, pv, 1, ALU.bitwise_and)
    C.wait(G, "P", "sV", sv_p0)
    with G.If_eq(pid1, 0):
        G.dma_start(out=shf[0:1, :], in_=flz[0:1, :]).then_inc(C.sems["qWC"], 16)
    with G.Else():
        G.dma_start(out=shf[1:2, :], in_=flz[0:1, :]).then_inc(C.sems["qWC"], 16)
    C.cnt["qWC"] += 16
    qwc_zero = C.cnt["qWC"]

    NCV = 16

    def conv_chunk(cc):
        r0e = cc * 512
        r0o = 8192 + cc * 512
        with G.If_eq(pid1, 0):
            G.dma_start(out=comb16[r0e:r0e + 512, :],
                        in_=comb_d[r0e:r0e + 512, :]).then_inc(C.sems["qWC"], 16)
        with G.Else():
            G.dma_start(out=comb16[r0o:r0o + 512, :],
                        in_=comb_d[r0o:r0o + 512, :]).then_inc(C.sems["qWC"], 16)
        C.cnt["qWC"] += 16

    u_rearr = u_d.rearrange("(c p) t -> p c t", p=128)
    ut3 = ut.ap().rearrange("p (c t) -> p c t", c=NDC)
    C.inc(SY.dma_start(out=ut3, in_=u_rearr[:, :, 0:TILE]), "qU", 16)
    qs_u = {0: C.cnt["qU"]}

    gd = dict(
        st_stats_prev=0, sv_rows=0, sa_lnb=0, sv_apply=0,
        sv_argS=0, sv_argL=0,
        st_x2_read=0, st_xa_read=0, qs_x2dma=0, qs_r1dma=0,
        sv_haloout=0, sv_kvh=0,
        st_conv3=0, st_conv4=0, sp_y_last=0,
        sa_vv_last=0, sp_convpair=0, sp_kv=0,
        st_wout_last=0, sv_ln2_apply=0,
        sa_keff=0,
        si=0,
    )
    gd["dgA_mark"] = [0, 0]
    gd["dgA_par"] = 0
    gd["dgB_mark"] = [0, 0]
    gd["dgB_par"] = 0
    gd["slot_sa"] = [("sA", 0), ("sA", 0)]
    gd["st_conv3p"] = [0, 0]
    gd["st_conv4p"] = [0, 0]
    gd["st_stats_par"] = [0, 0]
    gd["kv_aux"] = {}
    keff_done = [False]

    def next_slot():
        s = gd["si"] % 2
        gd["si"] += 1
        return s

    xa3 = xa.ap().rearrange("p (c t) -> p c t", c=NDC)
    tmp3 = u16sq.ap().rearrange("p (c t) -> p c t", c=NDC)
    x1v = x1pair.ap().rearrange("p (w c) -> p w c", w=2)
    x1vB = x1pairB.ap().rearrange("p (w c) -> p w c", w=2)
    x1vs = [x1v, x1vB]
    x1pairs = [x1pair, x1pairB]
    kvts = [kvt, kvtB]

    def ln_stats_piece(src, piece, s_st, first, last_):
        lo = piece * 1024
        par = piece % 2
        base = par * 2048
        u16 = u16sq.ap()[:, base:base + 1024]
        sq16 = u16sq.ap()[:, base + 1024:base + 2048]
        C.wait(V, "V", "sT", gd["st_stats_par"][par])
        vwait_self()
        vop(V.tensor_copy(u16, src[:, lo:lo + 1024]))
        vwait_self()
        vop(V.tensor_mul(sq16, u16, u16))
        sv_c = C.cnt["sV"]
        ps = psL if s_st is None else slots[s_st]
        C.wait(T, "T", "sV", sv_c)
        if first and s_st is not None:
            C.wait(T, "T", gd["slot_sa"][s_st][0], gd["slot_sa"][s_st][1])
        last = None
        for q2 in range(2):
            T.matmul(ps[0:1, 0:512], ones16[:, :], u16[:, q2 * 512:(q2 + 1) * 512],
                     start=(first and q2 == 0), stop=(last_ and q2 == 1))
            last = T.matmul(ps[0:1, 512:1024], ones16[:, :],
                            sq16[:, q2 * 512:(q2 + 1) * 512],
                            start=(first and q2 == 0), stop=(last_ and q2 == 1))
        st = C.inc(last, "sT")
        gd["st_stats_prev"] = st
        gd["st_stats_par"][par] = st
        return st

    def ln_rows_bcast(s_st, s_bc):
        ps = psL if s_st is None else slots[s_st]
        C.wait(V, "V", "sT", gd["st_stats_prev"])
        vwait_self()
        vop(V.tensor_scalar(rowb[0:1, 0:1024], ps[0:1, 0:1024], 1.0 / D, None, op0=ALU.mult))
        vwait_self()
        vop(V.tensor_mul(rowt[0:1, :], rowb[0:1, 0:512], rowb[0:1, 0:512]))
        vwait_self()
        vop(V.tensor_sub(rowt[0:1, :], rowb[0:1, 512:1024], rowt[0:1, :]))
        sv_rows = C.cnt["sV"]
        if s_st is not None:
            gd["slot_sa"][s_st] = ("sV", sv_rows)
        C.wait(A, "A", "sV", sv_rows)
        C.inc(A.activation(rowt[0:1, :], rowt[0:1, :], AF.Ln, bias=EPS), "sA")
        C.wait(A, "A", "sA", C.cnt["sA"])
        C.inc(A.activation(rowt[0:1, :], rowt[0:1, :], AF.Exp, scale=-0.5), "sA")
        sa_rstd = C.cnt["sA"]
        C.wait(V, "V", "sA", sa_rstd)
        vwait_self()
        vop(V.scalar_tensor_tensor(rowb[0:1, 512:1024], rowb[0:1, 0:512], -1.0,
                                   rowt[0:1, :], op0=ALU.mult, op1=ALU.mult))
        sv_rw = C.cnt["sV"]
        ps2 = slots[s_bc]
        C.wait(T, "T", "sV", sv_rw)
        C.wait(T, "T", gd["slot_sa"][s_bc][0], gd["slot_sa"][s_bc][1])
        T.matmul(ps2[:, 0:512], ones32[:, :], rowt[0:1, :], start=True, stop=True)
        st_b = C.inc(T.matmul(ps2[:, 512:1024], ones32[:, :], rowb[0:1, 512:1024],
                              start=True, stop=True), "sT")
        C.wait(A, "A", "sT", st_b)
        C.wait(A, "A", "sV", gd["sv_apply"])
        gd["sa_lnb"] = C.inc(A.copy(lnb[:, :], ps2[:, :]), "sA")
        gd["slot_sa"][s_bc] = ("sA", gd["sa_lnb"])
        return gd["sa_lnb"]

    def ln_apply(src_sb, extra_T_wait=0):
        src3 = src_sb.ap().rearrange("p (c t) -> p c t", c=NDC)
        rstd_b = lnb.ap()[:, 0:512].rearrange("p (o t) -> p o t", o=1).broadcast_to([128, 1, 512])
        mrstd_b = lnb.ap()[:, 512:1024].rearrange("p (o t) -> p o t", o=1).broadcast_to([128, 1, 512])
        C.wait(V, "V", "sA", gd["sa_lnb"])
        C.wait(V, "V", "sT", gd["st_stats_prev"])
        C.wait(V, "V", "sT", max(gd["st_x2_read"], gd["st_xa_read"], extra_T_wait))
        C.wait(V, "V", "qSPX", gd["qs_x2dma"])
        marks = []
        for dc in range(NDC):
            vwait_self()
            vop(V.tensor_mul(tmp3[:, dc:dc + 1, :], src3[:, dc:dc + 1, :], rstd_b))
            vwait_self()
            vop(V.tensor_tensor(xa3[:, dc:dc + 1, :], tmp3[:, dc:dc + 1, :],
                                mrstd_b, op=ALU.add))
            marks.append(C.cnt["sV"])
        gd["sv_apply"] = C.cnt["sV"]
        gd["sv_apply_dc"] = marks
        return gd["sv_apply"]

    def win_pair(ocA, ocB, sv_src, wtab, src_sb):
        s = next_slot()
        ps = slots[s]
        C.wait(T, "T", "qWA", qg_attnw)
        chase = isinstance(sv_src, list)
        if not chase:
            C.wait(T, "T", "sV", sv_src)
        C.wait(T, "T", gd["slot_sa"][s][0], gd["slot_sa"][s][1])
        last = None
        for k in range(NDC):
            if chase:
                C.wait(T, "T", "sV", sv_src[k])
            T.matmul(ps[:, 0:512], wtab[k][:, ocA * 128:(ocA + 1) * 128],
                     src_sb[:, k * 512:(k + 1) * 512],
                     start=(k == 0), stop=(k == NDC - 1))
            last = T.matmul(ps[:, 512:1024], wtab[k][:, ocB * 128:(ocB + 1) * 128],
                            src_sb[:, k * 512:(k + 1) * 512],
                            start=(k == 0), stop=(k == NDC - 1))
        return s, C.inc(last, "sT")

    def evac_to_x1(s, st_win, hslot, ebA, ebB, jpar):
        xv = x1vs[jpar]
        C.wait(A, "A", "sT", st_win)
        C.wait(A, "A", "sT", gd["st_conv3p"][jpar])
        C.wait(A, "A", "sV", gd["sv_haloout"])
        A.activation(xv[:, 0:1, 2:514], slots[s][:, 0:512], AF.Identity,
                     bias=binp[:, ebA:ebA + 1])
        sa = C.inc(A.activation(xv[:, 1:2, 2:514], slots[s][:, 512:1024], AF.Identity,
                                bias=binp[:, ebB:ebB + 1]), "sA")
        C.wait(V, "V", "sA", sa)
        C.wait(V, "V", "sT", gd["st_conv3p"][jpar])
        vwait_self()
        vop(V.tensor_copy(xv[:, :, 0:2],
                          x1h.ap()[:, hslot * 4:hslot * 4 + 4].rearrange(
                              "p (w c) -> p w c", w=2)))
        sv_hin = C.cnt["sV"]
        vop(V.tensor_copy(x1h.ap()[:, hslot * 4:hslot * 4 + 4].rearrange(
            "p (w c) -> p w c", w=2), xv[:, :, 512:514]))
        gd["sv_haloout"] = C.cnt["sV"]
        return sa, sv_hin

    def conv3_pair(s, ocA, ocB, sa_x1, sv_hin, jpar):
        par = gd["dgA_par"] = 1 - gd["dgA_par"]
        C.wait(V, "V", "sT", gd["dgA_mark"][par])
        vwait_self()
        for half, oc in ((0, ocA), (1, ocB)):
            for k in range(3):
                vop(V.tensor_scalar(dgA[par * 6 + half * 3 + k][:, :], eyeF[:, :],
                                    swp[:, oc * 3 + k:oc * 3 + k + 1], None, op0=ALU.mult))
        sv_dg = C.cnt["sV"]
        ps = slots[s]
        C.wait(T, "T", "sA", sa_x1)
        C.wait(T, "T", "sV", max(sv_dg, gd["sv_haloout"], sv_hin, sv_p0))
        last = None
        for half, oc in ((0, ocA), (1, ocB)):
            for k in range(3):
                last = T.matmul(ps[:, half * 512:(half + 1) * 512],
                                dgA[par * 6 + half * 3 + k][:, :],
                                x1pairs[jpar][:, half * 514 + k:half * 514 + k + 512],
                                start=(k == 0), stop=(k == 2))
        st = C.inc(last, "sT")
        gd["st_conv3p"][jpar] = st
        gd["dgA_mark"][par] = st
        return st

    def evac_to_convpair(s, st_c3, btab=None, ebA=0, ebB=0):
        C.wait(A, "A", "sT", st_c3)
        C.wait(A, "A", "sP", gd["sp_convpair"])
        if btab is None:
            sa = C.inc(A.copy(convpair[:, :], slots[s][:, :]), "sA")
        else:
            A.activation(convpair[:, 0:512], slots[s][:, 0:512], AF.Identity,
                         bias=btab[:, ebA:ebA + 1])
            sa = C.inc(A.activation(convpair[:, 512:1024], slots[s][:, 512:1024],
                                    AF.Identity, bias=btab[:, ebB:ebB + 1]), "sA")
        gd["slot_sa"][s] = ("sA", sa)
        return sa

    sv_ln1_pending = [None]

    for t in range(nt):
        t0 = t * TILE
        if t == 0:
            C.wait(V, "V", "qU", qs_u[0])
            s_st = next_slot()
            for piece in range(4):
                ln_stats_piece(ut.ap(), piece, s_st, piece == 0, piece == 3)
            s_bc = next_slot()
            ln_rows_bcast(s_st, s_bc)
            sv_ln1 = ln_apply(ut)
        else:
            sv_ln1 = sv_ln1_pending[0]

        ln1_dc = gd["sv_apply_dc"]
        jobs = [("kv", i) for i in range(NDC)] + [("q", j) for j in range(4)]

        def job_win(idx, job):
            kind, i = job
            if kind == "kv":
                ocA, ocB, hslot = 8 + i, 16 + i, i
            else:
                ocA, ocB, hslot = 2 * i, 2 * i + 1, 8 + i
            s, st_w = win_pair(ocA, ocB, ln1_dc if idx == 0 else sv_ln1, wi16, xa)
            gd["st_xa_read"] = st_w
            return [s, ocA, ocB, st_w, hslot, idx % 2]

        def job_evac(hd):
            s, ocA, ocB, st_w, hslot, jpar = hd
            sa_x1, sv_hin = evac_to_x1(s, st_w, hslot, ocA, ocB, jpar)
            hd[3] = (sa_x1, sv_hin)

        def job_tail1(job, hd):
            kind, i = job
            s, ocA, ocB, (sa_x1, sv_hin), hslot, jpar = hd
            st_c3 = conv3_pair(s, ocA, ocB, sa_x1, sv_hin, jpar)
            sa_cp = evac_to_convpair(s, st_c3, sbp, ocA, ocB)
            if kind == "kv":
                par = i % 2
                kv = kvts[par]
                C.wait(P, "P", "sA", sa_cp)
                C.wait(P, "P", "sT", gd["st_conv4p"][par])
                C.wait(P, "P", "sV", gd["sv_kvh"])
                sp_kv = C.inc(P.tensor_mul(kv[:, 3:515], convpair[:, 0:512],
                                           convpair[:, 512:1024]), "sP")
                gd["sp_convpair"] = sp_kv
                C.wait(V, "V", "sP", sp_kv)
                C.wait(V, "V", "sT", gd["st_conv4p"][par])
                vwait_self()
                vop(V.tensor_copy(kv[:, 0:3], kvh[:, i * 3:i * 3 + 3]))
                vop(V.tensor_copy(kvh[:, i * 3:i * 3 + 3], kv[:, 512:515]))
                gd["sv_kvh"] = C.cnt["sV"]
                dpar = gd["dgB_par"] = 1 - gd["dgB_par"]
                C.wait(V, "V", "sT", gd["dgB_mark"][dpar])
                vwait_self()
                for k in range(4):
                    vop(V.tensor_scalar(dgB[dpar * 4 + k][:, :], eyeF[:, :],
                                        iwp[:, i * 4 + k:i * 4 + k + 1], None,
                                        op0=ALU.mult))
                gd["kv_aux"][i] = (par, dpar, C.cnt["sV"], sp_kv)
            else:
                C.wait(P, "P", "sA", sa_cp)
                C.wait(P, "P", "sA", gd["sa_vv_last"])
                C.wait(P, "P", "sT", gd["st_wout_last"])
                P.tensor_mul(y16[:, ocA * 512:(ocA + 1) * 512], convpair[:, 0:512],
                             vvt[:, ocA * 512:(ocA + 1) * 512])
                spy = C.inc(P.tensor_mul(y16[:, ocB * 512:(ocB + 1) * 512],
                                         convpair[:, 512:1024],
                                         vvt[:, ocB * 512:(ocB + 1) * 512]), "sP")
                gd["sp_convpair"] = spy
                gd["sp_y_last"] = spy

        def job_tail2(job):
            kind, i = job
            par, dpar, sv_dg4, sp_kv = gd["kv_aux"][i]
            kv = kvts[par]
            C.wait(T, "T", "sP", sp_kv)
            C.wait(T, "T", "sV", max(sv_dg4, gd["sv_kvh"]))
            C.wait(T, "T", "sA", gd["sa_vv_last"])
            C.wait(T, "T", "sV", gd["sv_argS"])
            last = None
            for k in range(4):
                last = T.matmul(psS[:, 0:512], dgB[dpar * 4 + k][:, :], kv[:, k:k + 512],
                                start=(k == 0), stop=(k == 3))
            st_c4 = C.inc(last, "sT")
            gd["st_conv4p"][par] = st_c4
            gd["dgB_mark"][dpar] = st_c4
            C.wait(A, "A", "sT", st_c4)
            gd["sa_vv_last"] = C.inc(
                A.activation(vvt[:, i * 512:(i + 1) * 512], psS[:, 0:512],
                             AF.Identity, bias=ibp[:, i:i + 1]), "sA")

        heads = {}
        for idx, job in enumerate(jobs):
            heads[idx] = job_win(idx, job)
            if idx >= 1:
                job_tail1(jobs[idx - 1], heads[idx - 1])
            if idx >= 2 and jobs[idx - 2][0] == "kv":
                job_tail2(jobs[idx - 2])
            job_evac(heads[idx])
        job_tail1(jobs[-1], heads[len(jobs) - 1])
        sp_y = gd["sp_y_last"]

        sp_r1 = 0
        st_wo = 0
        for j in range(4):
            ocA, ocB = 2 * j, 2 * j + 1
            s = next_slot()
            ps = slots[s]
            C.wait(T, "T", "sP", sp_y)
            C.wait(T, "T", gd["slot_sa"][s][0], gd["slot_sa"][s][1])
            last = None
            for k in range(NDC):
                T.matmul(ps[:, 0:512], wo16[k][:, ocA * 128:(ocA + 1) * 128],
                         y16[:, k * 512:(k + 1) * 512], start=(k == 0), stop=(k == NDC - 1))
                last = T.matmul(ps[:, 512:1024], wo16[k][:, ocB * 128:(ocB + 1) * 128],
                                y16[:, k * 512:(k + 1) * 512], start=(k == 0), stop=(k == NDC - 1))
            st_wo = C.inc(last, "sT")
            C.wait(A, "A", "sT", st_wo)
            C.wait(A, "A", "sP", gd["sp_convpair"])
            A.activation(convpair[:, 0:512], ps[:, 0:512], AF.Identity,
                         bias=boutp[:, ocA:ocA + 1])
            sa_o = C.inc(A.activation(convpair[:, 512:1024], ps[:, 512:1024],
                                      AF.Identity, bias=boutp[:, ocB:ocB + 1]), "sA")
            gd["slot_sa"][s] = ("sA", sa_o)
            C.wait(P, "P", "sA", sa_o)
            C.wait(P, "P", "qSPR", gd["qs_r1dma"])
            C.wait(P, "P", "sV", gd["sv_ln2_apply"])
            sp_r1 = C.inc(P.tensor_add(r1sb[:, ocA * 512:(ocB + 1) * 512],
                                       convpair[:, :],
                                       ut[:, ocA * 512:(ocB + 1) * 512]), "sP")
            gd["sp_convpair"] = sp_r1
            C.wait(SY, "S", "sP", sp_r1)
            C.inc(SY.dma_start(
                out=r1_d[ocA:ocB + 1, :, t0:t0 + TILE].rearrange("c p t -> p c t"),
                in_=r1sb.ap()[:, ocA * 512:(ocB + 1) * 512].rearrange(
                    "p (c t) -> p c t", c=2)), "qSPR", 16)
            C.wait(V, "V", "sP", sp_r1)
            ln_stats_piece(r1sb.ap(), j, None, j == 0, j == 3)
        gd["st_wout_last"] = st_wo
        gd["qs_r1dma"] = C.cnt["qSPR"]
        sp_r1_done = sp_r1

        if t + 1 < nt:
            C.wait(SY, "S", "sP", sp_r1_done)
            C.inc(SY.dma_start(out=ut3, in_=u_rearr[:, :, (t + 1) * TILE:(t + 2) * TILE]),
                  "qU", 16)
            qs_u[t + 1] = C.cnt["qU"]

        if not keff_done[0]:
            keff_done[0] = True
            keff_mark = 0
            s = next_slot()
            ps = slots[s]
            first = True
            for ph in range(16):
                C.wait(T, "T", "qWQ", C.cnt["qWQ"])
                for dc in range(NDC):
                    if first:
                        C.wait(T, "T", gd["slot_sa"][s][0], gd["slot_sa"][s][1])
                    else:
                        C.wait(T, "T", "sA", gd["sa_keff"])
                    first = False
                    last = None
                    for dkc in range(4):
                        last = T.matmul(ps[:, 0:128],
                                        wqb[dkc][:, dc * 128:(dc + 1) * 128],
                                        ktb[ph % 2][:, dkc * 128:(dkc + 1) * 128],
                                        start=(dkc == 0), stop=(dkc == 3))
                    stk = C.inc(last, "sT")
                    if dc == NDC - 1:
                        keff_mark = stk
                    C.wait(A, "A", "sT", stk)
                    gd["sa_keff"] = C.inc(
                        A.copy(keff[dc][:, ph * 128:(ph + 1) * 128], ps[:, 0:128]), "sA")
                if ph + 1 < 16:
                    C.wait(G, "P", "sT", keff_mark)
                    for dkc in range(4):
                        r0 = (ph + 1) * 512 + dkc * 128
                        C.inc(G.dma_start(out=wqb[dkc][:, :],
                                          in_=wq_d[r0:r0 + 128, :]), "qWQ", 16)
                    C.inc(G.dma_start(
                        out=ktb[(ph + 1) % 2].ap().rearrange("p (c k) -> p c k", c=4),
                        in_=kt_d[ph + 1:ph + 2, :, :].rearrange(
                            "s (c p) k -> p (s c) k", p=128)), "qWQ", 16)
            gd["slot_sa"][s] = ("sA", gd["sa_keff"])

        s_bc2 = next_slot()
        ln_rows_bcast(None, s_bc2)
        sv_ln2 = ln_apply(r1sb)
        ln2_dc = gd["sv_apply_dc"]
        gd["sv_ln2_apply"] = sv_ln2
        for j in range(4):
            ocA = 2 * j
            C.wait(SY, "S", "sV", sv_ln2)
            C.inc(SY.dma_start(
                out=x2_d[ocA:ocA + 2, :, t0:t0 + TILE].rearrange("c p t -> p c t"),
                in_=x2.ap()[:, ocA * 512:(ocA + 2) * 512].rearrange(
                    "p (c t) -> p c t", c=2)), "qSPX", 16)
        gd["qs_x2dma"] = C.cnt["qSPX"]

        for gi in range(4):
            g = t * 4 + gi
            go = gi * 128
            st_sim = {}
            for p in range(2):
                dstp = psS if p == 0 else psL
                if not (gi == 0 and p == 0):
                    C.wait(T, "T", "sV", sv_ln2)
                if p == 0:
                    C.wait(T, "T", "sV", gd["sv_argS"])
                    C.wait(T, "T", "sA", gd["sa_vv_last"])
                else:
                    C.wait(T, "T", "sV", gd["sv_argL"])
                last = None
                for half in range(2):
                    for dc in range(NDC):
                        if gi == 0 and p == 0 and half == 0:
                            C.wait(T, "T", "sV", ln2_dc[dc])
                        last = T.matmul(dstp[:, half * 512:(half + 1) * 512],
                                        x2[:, dc * 512 + go:dc * 512 + go + 128],
                                        keff[dc][:, p * 1024 + half * 512:p * 1024 + (half + 1) * 512],
                                        start=(dc == 0), stop=(dc == NDC - 1))
                st_sim[p] = C.inc(last, "sT")
            gd["st_x2_read"] = st_sim[1]
            for p in range(2):
                dstp = psS if p == 0 else psL
                ma = m0a if p == 0 else m1a
                ia = i0a if p == 0 else i1a
                C.wait(V, "V", "sT", st_sim[p])
                vwait_self()
                for hh in range(H):
                    vop(V.max(ma[:, hh * 8:hh * 8 + 8], dstp[:, hh * 128:(hh + 1) * 128]))
                vwait_self()
                for hh in range(H):
                    vop(V.max_index(ia[:, hh * 8:hh * 8 + 8], ma[:, hh * 8:hh * 8 + 8],
                                    dstp[:, hh * 128:(hh + 1) * 128]))
                if p == 0:
                    gd["sv_argS"] = C.cnt["sV"]
                else:
                    gd["sv_argL"] = C.cnt["sV"]
            vwait_self()
            m0s = m0a.ap().rearrange("p (h e) -> p h e", e=8)[:, :, 0:1]
            m1s = m1a.ap().rearrange("p (h e) -> p h e", e=8)[:, :, 0:1]
            i0s = i0a.ap().rearrange("p (h e) -> p h e", e=8)[:, :, 0:1]
            i1s = i1a.ap().rearrange("p (h e) -> p h e", e=8)[:, :, 0:1]
            vop(V.tensor_add(sc8[:, 16:24], m0s, m1s))
            vwait_self()
            vop(V.tensor_scalar_max(gate_all[:, g * 8:(g + 1) * 8], sc8[:, 16:24], 0.0))
            vop(V.tensor_copy(sc8[:, 0:8], i0s))
            vop(V.tensor_copy(sc8[:, 8:16], i1s))
            vwait_self()
            vop(V.scalar_tensor_tensor(sc8[:, 16:24], sc8[:, 0:8], 128.0,
                                       sc8[:, 8:16], op0=ALU.mult, op1=ALU.add))
            vwait_self()
            vop(V.tensor_copy(idx_all[:, g * 8:(g + 1) * 8], sc8[:, 16:24]))

        if t + 1 < nt:
            C.wait(V, "V", "qU", qs_u[t + 1])
            s_st = next_slot()
            for piece in range(4):
                ln_stats_piece(ut.ap(), piece, s_st, piece == 0, piece == 3)
            s_bc = next_slot()
            ln_rows_bcast(s_st, s_bc)
            sv_ln1_pending[0] = ln_apply(ut, extra_T_wait=gd["st_x2_read"])

        for cc in (2 * t, 2 * t + 1):
            if cc < NCV:
                conv_chunk(cc)

    for cc in range(2 * nt, NCV):
        conv_chunk(cc)

    C.wait(G, "P", "qWC", C.cnt["qWC"])
    with G.If_eq(pid1, 0):
        G.dma_start(out=shf[0:1, :], in_=flone[0:1, :]).then_inc(C.sems["qWC"], 16)
    with G.Else():
        G.dma_start(out=shf[1:2, :], in_=flone[0:1, :]).then_inc(C.sems["qWC"], 16)
    C.cnt["qWC"] += 16
    C.wait(G, "P", "qWC", C.cnt["qWC"])
    with nc.gpsimd.register("pcnt") as pcnt, nc.gpsimd.register("pcond") as pcond:
        G.reg_mov(pcnt, 0)
        G.reg_mov(pcond, 1)
        with G.While(pcond):
            G.reg_add(pcnt, pcnt, 16)
            with G.If_eq(pid1, 0):
                G.dma_start(out=flrd[0:1, 0:1], in_=shf[1:2, 0:1]).then_inc(C.sems["qPF"], 16)
            with G.Else():
                G.dma_start(out=flrd[0:1, 0:1], in_=shf[0:1, 0:1]).then_inc(C.sems["qPF"], 16)
            G.wait_ge(C.sems["qPF"], pcnt)
            G.reg_load(pcond, flrd[0:1, 0:1])
            G.reg_alu(pcond, pcond, 1, ALU.bitwise_xor)

    endT, endV, endA, endP = C.cnt["sT"], C.cnt["sV"], C.cnt["sA"], C.cnt["sP"]

    p1_ctx.close()
    ps_ctx.close()

    b2_ctx = ExitStack()
    b2 = b2_ctx.enter_context
    wif16 = [b2(nc.sbuf_tensor(f"wif16_{k}", [128, DI], BF16)) for k in range(NDC)]
    wof16 = [b2(nc.sbuf_tensor(f"wof16_{k}", [128, D], BF16)) for k in range(NIC)]
    ainb0 = [b2(nc.sbuf_tensor(f"ainb_{h}", [128, 1024], BF16)) for h in range(H)]
    binb0 = [b2(nc.sbuf_tensor(f"binb_{h}", [128, 2048], BF16)) for h in range(H)]
    aoutb0 = [b2(nc.sbuf_tensor(f"aoutb_{h}", [128, 2048], BF16)) for h in range(H)]
    boutb0 = [b2(nc.sbuf_tensor(f"boutb_{h}", [128, 1024], BF16)) for h in range(H)]
    ainb = [ainb0, ainb0]
    binb = [binb0, binb0]
    aoutb = [aoutb0, aoutb0]
    boutb = [boutb0, boutb0]
    x2g2 = [b2(nc.sbuf_tensor(f"x2g_{k}", [128, 1024], BF16)) for k in range(2)]
    r1g = b2(nc.sbuf_tensor("r1g", [128, 1024], F32))
    xtok = b2(nc.sbuf_tensor("xtok", [128, 1024], BF16))
    htok = b2(nc.sbuf_tensor("htok", [128, 2048], BF16))
    hfm = b2(nc.sbuf_tensor("hfm", [128, 2048], BF16))
    pj4 = b2(nc.sbuf_tensor("pj4", [128, 2048], BF16))
    pjin = b2(nc.sbuf_tensor("pjin", [128, 1024], BF16))
    linr = b2(nc.sbuf_tensor("linr", [128, 16], F32))
    ling = b2(nc.sbuf_tensor("ling", [128, 16], F32))
    dgb = [b2(nc.sbuf_tensor(f"dgb_{h}", [128, 128], BF16)) for h in range(2 * H)]
    outg = b2(nc.sbuf_tensor("outg", [128, 1024], F32))

    psH = b2(nc.psum_tensor("psH", [128, 2048], F32))
    psO = b2(nc.psum_tensor("psO", [128, 1024], F32))
    psT2 = b2(nc.psum_tensor("psT2", [128, 512], F32))
    psT3 = b2(nc.psum_tensor("psT3", [128, 512], F32))

    C.wait(G, "P", "sT", endT)
    C.wait(G, "P", "sV", endV)
    C.wait(G, "P", "sA", endA)
    C.wait(G, "P", "sP", endP)
    for k in range(NDC):
        C.inc(G.dma_start(out=wif16[k][:, :], in_=wif_d[k * 128:(k + 1) * 128, :]), "qWF", 16)
    for k in range(NIC):
        C.inc(G.dma_start(out=wof16[k][:, :], in_=wof_d[k * 128:(k + 1) * 128, :]), "qWF", 16)
    qg_ffnw = C.cnt["qWF"]
    C.wait(SY, "S", "sT", endT)
    C.wait(SY, "S", "sA", endA)
    C.wait(SY, "S", "sV", endV)
    C.wait(SY, "S", "sP", endP)
    C.wait(T, "T", "sV", endV)
    C.wait(T, "T", "sA", endA)
    C.wait(A, "A", "sT", endT)
    C.wait(A, "A", "sV", endV)
    C.wait(V, "V", "sT", endT)
    C.wait(V, "V", "sA", endA)
    C.wait(P, "P", "sT", endT)

    bg = dict(st_bin=0, st_bout=0, sa_gelu=0, sa_out=0, qs_y=0,
              sa_xtok=0, sa_hfm=[0, 0, 0, 0], sv_amul_in=0, sv_amul_out=0,
              sv_ling_in=0, sv_ling_out=0, sa_red_in=[0] * H, sa_red_out=[0] * H)

    GOFF = {0: (0, 1024, ainb, "qAI"), 1: (1024, 2048, binb, "qBI"),
            2: (3072, 2048, aoutb, "qAO"), 3: (5120, 1024, boutb, "qBO")}
    for qn in ("qAI", "qBI", "qAO", "qBO"):
        C.sem(qn)

    def emit_gathers(g, which, gate_sem, gate_val):
        off, width, bufs, qn = GOFF[which]
        C.wait(G, "P", gate_sem, gate_val)
        for h in range(H):
            C.inc(G.indirect_dma_start(
                bufs[g % 2][h][:, 0:width], None,
                comb16[:, :],
                IndirectOffsetOnAxis(idx_all.ap()[:, g * 8 + h:g * 8 + h + 1], 0),
                element_offset=off), qn, 16)
        return C.cnt[qn]

    qai = {}
    qbi = {}
    qao = {}
    qbo = {}
    qai[0] = qai[1] = emit_gathers(0, 0, "sT", 0)
    qbi[0] = qbi[1] = emit_gathers(0, 1, "sT", 0)
    qao[0] = qao[1] = emit_gathers(0, 2, "sT", 0)
    qbo[0] = qbo[1] = emit_gathers(0, 3, "sT", 0)
    x2g3 = [x2g2[k].ap().rearrange("p (c t) -> p c t", c=NDC) for k in range(2)]
    r1g3 = r1g.ap().rearrange("p (c t) -> p c t", c=NDC)
    C.inc(SY.dma_start(out=x2g3[0], in_=x2_d[:, :, 0:128].rearrange("c p t -> p c t")),
          "qX", 16)
    qx = C.cnt["qX"]
    C.inc(SY.dma_start(out=r1g3, in_=r1_d[:, :, 0:128].rearrange("c p t -> p c t")),
          "qR", 16)
    qr = C.cnt["qR"]

    for g in range(NG):
        par = g % 2
        x2g = x2g2[par]
        C.wait(T, "T", "qX", qx)
        for half in range(2):
            pst = psT2 if half == 0 else psT3
            C.wait(T, "T", "sA", max(bg["sa_xtok"], bg["sa_hfm"][half + 2]))
            last = None
            for c in range(4):
                cc = half * 4 + c
                last = T.matmul(pst[:, c * 128:(c + 1) * 128],
                                x2g[:, cc * 128:(cc + 1) * 128], eye16[:, :],
                                start=True, stop=True)
            st = C.inc(last, "sT")
            C.wait(A, "A", "sT", st)
            C.wait(A, "A", "sV", bg["sv_amul_in"])
            bg["sa_xtok"] = C.inc(
                A.copy(xtok[:, half * 512:(half + 1) * 512], pst[:, :]), "sA")

        C.wait(V, "V", "qAI", qai[par])
        C.wait(V, "V", "sA", bg["sa_xtok"])
        for h in range(H):
            vwait_self()
            vop(V.scalar_tensor_tensor(pjin[:, :], ainb[par][h][:, :], 1.0,
                                       xtok[:, :], op0=ALU.mult, op1=ALU.mult,
                                       accum_out=linr[:, h:h + 1]))
        bg["sv_amul_in"] = C.cnt["sV"]
        if g + 1 < NG:
            qai[0] = qai[1] = emit_gathers(g + 1, 0, "sV", bg["sv_amul_in"])
        vwait_self()
        vop(V.tensor_mul(ling[:, 0:8], linr[:, 0:8], gate_all[:, g * 8:(g + 1) * 8]))
        bg["sv_ling_in"] = C.cnt["sV"]
        vwait_self()
        for h in range(H):
            C.wait(V, "V", "sT", bg["st_bin"])
            vop(V.tensor_scalar(dgb[h][:, :], eyeF[:, :], ling[:, h:h + 1], None,
                                op0=ALU.mult))
        sv_dgin = C.cnt["sV"]

        C.wait(T, "T", "qWF", qg_ffnw)
        C.wait(T, "T", "qBI", qbi[par])
        C.wait(T, "T", "sV", sv_dgin)
        C.wait(T, "T", "sA", bg["sa_gelu"])
        n_contrib = NDC + H
        ci = 0
        last1 = None
        for dc in range(NDC):
            lhs = x2g[:, dc * 128:(dc + 1) * 128]
            for qtr in range(4):
                last1 = T.matmul(psH[:, qtr * 512:(qtr + 1) * 512], lhs,
                                 wif16[dc][:, qtr * 512:(qtr + 1) * 512],
                                 start=(ci == 0), stop=(ci == n_contrib - 1))
            ci += 1
        for h in range(H):
            for qtr in range(4):
                last1 = T.matmul(psH[:, qtr * 512:(qtr + 1) * 512], dgb[h][:, :],
                                 binb[par][h][:, qtr * 512:(qtr + 1) * 512],
                                 start=(ci == 0), stop=(ci == n_contrib - 1))
            ci += 1
        bg["st_bin"] = C.inc(last1, "sT")
        if g + 1 < NG:
            qbi[0] = qbi[1] = emit_gathers(g + 1, 1, "sT", bg["st_bin"])
            gsl = slice((g + 1) * 128, (g + 2) * 128)
            C.wait(SY, "S", "sT", bg["st_bin_prev"] if "st_bin_prev" in bg else 0)
            C.inc(SY.dma_start(out=x2g3[(g + 1) % 2],
                               in_=x2_d[:, :, gsl].rearrange("c p t -> p c t")), "qX", 16)
            qx = C.cnt["qX"]
        bg["st_bin_prev"] = bg["st_bin"]

        C.wait(A, "A", "sT", bg["st_bin"])
        C.wait(A, "A", "sV", bg["sv_amul_out"])
        bg["sa_gelu"] = C.inc(A.activation(htok[:, :], psH[:, :], AF.Gelu), "sA")

        C.wait(T, "T", "sA", bg["sa_gelu"])
        for half in range(4):
            pst = psT2 if half % 2 == 0 else psT3
            C.wait(T, "T", "sA",
                   bg["sa_xtok"] if half < 2 else bg["sa_hfm"][half - 2])
            last = None
            for c in range(4):
                cc = half * 4 + c
                last = T.matmul(pst[:, c * 128:(c + 1) * 128],
                                htok[:, cc * 128:(cc + 1) * 128], eye16[:, :],
                                start=True, stop=True)
            st = C.inc(last, "sT")
            C.wait(A, "A", "sT", st)
            C.wait(A, "A", "sT", bg["st_bout"])
            bg["sa_hfm"][half] = C.inc(
                A.copy(hfm[:, half * 512:(half + 1) * 512], pst[:, :]), "sA")

        C.wait(V, "V", "qAO", qao[par])
        C.wait(V, "V", "sA", bg["sa_gelu"])
        for h in range(H):
            vwait_self()
            vop(V.scalar_tensor_tensor(pj4[:, :], aoutb[par][h][:, :], 1.0,
                                       htok[:, :], op0=ALU.mult, op1=ALU.mult,
                                       accum_out=linr[:, 8 + h:9 + h]))
        bg["sv_amul_out"] = C.cnt["sV"]
        if g + 1 < NG:
            qao[0] = qao[1] = emit_gathers(g + 1, 2, "sV", bg["sv_amul_out"])
        vwait_self()
        vop(V.tensor_mul(ling[:, 8:16], linr[:, 8:16], gate_all[:, g * 8:(g + 1) * 8]))
        bg["sv_ling_out"] = C.cnt["sV"]
        vwait_self()
        for h in range(H):
            C.wait(V, "V", "sT", bg["st_bout"])
            vop(V.tensor_scalar(dgb[8 + h][:, :], eyeF[:, :], ling[:, 8 + h:9 + h], None,
                                op0=ALU.mult))
        sv_dgout = C.cnt["sV"]

        C.wait(T, "T", "sV", sv_dgout)
        C.wait(T, "T", "qBO", qbo[par])
        C.wait(T, "T", "qR", qr)
        C.wait(T, "T", "sA", bg["sa_out"])
        last = None
        for c in range(NIC):
            for hf in range(2):
                last = T.matmul(psO[:, hf * 512:(hf + 1) * 512],
                                hfm[:, c * 128:(c + 1) * 128],
                                wof16[c][:, hf * 512:(hf + 1) * 512],
                                start=(c == 0), stop=False)
        for h in range(H):
            for hf in range(2):
                last = T.matmul(psO[:, hf * 512:(hf + 1) * 512], dgb[8 + h][:, :],
                                boutb[par][h][:, hf * 512:(hf + 1) * 512],
                                start=False, stop=False)
        for dc in range(NDC):
            last = T.matmul(psO[:, dc * 128:(dc + 1) * 128],
                            r1g[:, dc * 128:(dc + 1) * 128], eyeF[:, :],
                            start=False, stop=(dc == 3 or dc == NDC - 1))
        bg["st_bout"] = C.inc(last, "sT")
        if g + 1 < NG:
            qbo[0] = qbo[1] = emit_gathers(g + 1, 3, "sT", bg["st_bout"])
            gsl = slice((g + 1) * 128, (g + 2) * 128)
            C.wait(SY, "S", "sT", bg["st_bout"])
            C.inc(SY.dma_start(out=r1g3,
                               in_=r1_d[:, :, gsl].rearrange("c p t -> p c t")), "qR", 16)
            qr = C.cnt["qR"]

        C.wait(A, "A", "sT", bg["st_bout"])
        C.wait(A, "A", "qY", bg["qs_y"])
        bg["sa_out"] = C.inc(A.copy(outg[:, :], psO[:, :]), "sA")
        C.wait(SY, "S", "sA", bg["sa_out"])
        C.inc(SY.dma_start(out=y_d[g * 128:(g + 1) * 128, :], in_=outg[:, :]), "qY", 16)
        bg["qs_y"] = C.cnt["qY"]

    if debug:
        C.wait(SY, "S", "sV", C.cnt["sV"])
        C.inc(SY.dma_start(out=idx_d[:, :], in_=idx_all[:, :]), "qY", 16)
        C.inc(SY.dma_start(out=gate_d[:, :], in_=gate_all[:, :]), "qY", 16)

    for s in ("qC", "qU", "qSPR", "qSPX", "qX", "qR", "qY"):
        C.wait(SY, "S", s, C.cnt[s])
    for s in ("qWA", "qWQ", "qWF", "qAI", "qBI", "qAO", "qBO", "qWC"):
        C.wait(G, "P", s, C.cnt[s])

    b2_ctx.close()
    ctx.close()
    return nc


def _host_prep(inputs):
    u = inputs["L0_attn_u"]
    comb = np.concatenate([inputs["L0_ffn_in_a"], inputs["L0_ffn_in_b"],
                           inputs["L0_ffn_out_a"], inputs["L0_ffn_out_b"]], axis=1)
    comb = np.ascontiguousarray(comb)
    keys = inputs["L0_ffn_keys"]
    kt = np.ascontiguousarray(keys.transpose(2, 0, 3, 1).reshape(16, 512, 128))
    base = {
        "combt": comb,
        "wi": np.ascontiguousarray(inputs["L0_attn_W_in"]),
        "wo": np.ascontiguousarray(inputs["L0_attn_W_out"]),
        "wq": np.ascontiguousarray(inputs["L0_ffn_Wq"]),
        "kt": kt,
        "wif": np.ascontiguousarray(inputs["L0_ffn_W_in"].T),
        "wof": np.ascontiguousarray(inputs["L0_ffn_W_out"].T),
        "binp": np.ascontiguousarray(inputs["L0_attn_b_in"].reshape(NOC, 128).T),
        "boutp": np.ascontiguousarray(inputs["L0_attn_b_out"].reshape(NDC, 128).T),
        "sbp": np.ascontiguousarray(inputs["L0_attn_short_b"].reshape(NOC, 128).T),
        "ibp": np.ascontiguousarray(inputs["L0_attn_inner_b"].reshape(NDC, 128).T),
        "swp": np.ascontiguousarray(
            inputs["L0_attn_short_w"].reshape(NOC, 128, 3).transpose(1, 0, 2).reshape(128, NOC * 3)),
        "iwp": np.ascontiguousarray(
            inputs["L0_attn_inner_w"].reshape(NDC, 128, 4).transpose(1, 0, 2).reshape(128, NDC * 4)),
    }
    u_fm = np.ascontiguousarray(u.transpose(0, 2, 1))
    return base, u_fm


def kernel(**inputs):
    base, u_fm = _host_prep(inputs)
    nc = build_nc(nt=8, debug=False)
    in_maps = [dict(base, u=u_fm[c]) for c in range(8)]
    res = run_bass_kernel_spmd(nc, in_maps, core_ids=list(range(8)))
    out = np.stack([np.asarray(r["yout"]) for r in res.results], axis=0)
    return out.astype(np.float32, copy=False)


# version 15
# speedup vs baseline: 1.8017x; 1.2394x over previous
import numpy as np
import concourse.bass as bass
import concourse.mybir as mybir
from concourse.bass_utils import run_bass_kernel_spmd
from concourse.bass import IndirectOffsetOnAxis
from contextlib import ExitStack

F32 = mybir.dt.float32
BF16 = mybir.dt.bfloat16
I32 = mybir.dt.int32
U32 = mybir.dt.uint32
AF = mybir.ActivationFunctionType
ALU = mybir.AluOpType

L = 4096
TILE = 512
GRP = 128
D = 1024
NDC = 8
D3 = 3072
NOC = 24
DI = 2048
NIC = 16
H = 8
NE = 16384
EPS = 1e-5


class Ctr:
    def __init__(self, nc):
        self.nc = nc
        self.sems = {}
        self.cnt = {}
        self.marks = {}

    def sem(self, name):
        s = self.nc.semaphore(name).__enter__()
        self.sems[name] = s
        self.cnt[name] = 0

    def inc(self, instr, name, by=1):
        self.cnt[name] += by
        instr.then_inc(self.sems[name], by)
        return self.cnt[name]

    def wait(self, eng, ename, name, val):
        if val <= 0:
            return
        key = (ename, name)
        if self.marks.get(key, -1) >= val:
            return
        eng.wait_ge(self.sems[name], val)
        self.marks[key] = val


def build_nc(nt=8, debug=False):
    nc = bass.Bass("TRN2", target_bir_lowering=False, debug=False,
                   detect_race_conditions=False)
    NG = (nt * TILE) // GRP

    u_d = nc.dram_tensor("u", [D, L], F32, kind="ExternalInput").ap()
    comb_d = nc.dram_tensor("combt", [NE, 6144], F32, kind="ExternalInput").ap()
    wi_d = nc.dram_tensor("wi", [D, D3], F32, kind="ExternalInput").ap()
    wo_d = nc.dram_tensor("wo", [D, D], F32, kind="ExternalInput").ap()
    wq_d = nc.dram_tensor("wq", [8192, D], F32, kind="ExternalInput").ap()
    kt_d = nc.dram_tensor("kt", [16, 512, 128], F32, kind="ExternalInput").ap()
    wif_d = nc.dram_tensor("wif", [D, DI], F32, kind="ExternalInput").ap()
    wof_d = nc.dram_tensor("wof", [DI, D], F32, kind="ExternalInput").ap()
    bin_d = nc.dram_tensor("binp", [128, NOC], F32, kind="ExternalInput").ap()
    bout_d = nc.dram_tensor("boutp", [128, NDC], F32, kind="ExternalInput").ap()
    sb_d = nc.dram_tensor("sbp", [128, NOC], F32, kind="ExternalInput").ap()
    ib_d = nc.dram_tensor("ibp", [128, NDC], F32, kind="ExternalInput").ap()
    sw_d = nc.dram_tensor("swp", [128, NOC * 3], F32, kind="ExternalInput").ap()
    iw_d = nc.dram_tensor("iwp", [128, NDC * 4], F32, kind="ExternalInput").ap()
    eye_d = nc.inline_tensor(np.eye(128, dtype=np.float32), "eye128").ap()
    comb16 = nc.dram_tensor("comb16sh", [NE, 6144], BF16, kind="Internal",
                            addr_space="Shared").ap()
    shf = nc.dram_tensor("shflag", [2, 128], I32, kind="Internal",
                         addr_space="Shared").ap()

    y_d = nc.dram_tensor("yout", [L, D], F32, kind="ExternalOutput").ap()
    ik = "ExternalOutput" if debug else "Internal"
    r1_d = nc.dram_tensor("r1d", [NDC, 128, L], F32, kind=ik).ap()
    x2_d = nc.dram_tensor("x2d", [NDC, 128, L], BF16, kind=ik).ap()
    if debug:
        idx_d = nc.dram_tensor("idxd", [128, 256], I32, kind="ExternalOutput").ap()
        gate_d = nc.dram_tensor("gated", [128, 256], F32, kind="ExternalOutput").ap()

    ctx = ExitStack()
    e = ctx.enter_context
    C = Ctr(nc)
    for s in ("qC", "qU", "qSPR", "qSPX", "qX", "qR", "qY", "qWA", "qWQ",
              "qWF", "qGI", "qGO", "qWC", "qPF", "sV", "sA", "sP", "sT"):
        C.sem(s)
    SY, V, A, P, T, G = nc.sync, nc.vector, nc.scalar, nc.gpsimd, nc.tensor, nc.gpsimd

    def vop(instr):
        return C.inc(instr, "sV")

    def vwait_self():
        C.wait(V, "V", "sV", C.cnt["sV"])

    ones16 = e(nc.sbuf_tensor("ones16", [128, 1], BF16))
    ones32 = e(nc.sbuf_tensor("ones32", [1, 128], F32))
    eyeF = e(nc.sbuf_tensor("eyeF", [128, 128], F32))
    eye16 = e(nc.sbuf_tensor("eye16", [128, 128], BF16))
    swp = e(nc.sbuf_tensor("swp_s", [128, NOC * 3], F32))
    iwp = e(nc.sbuf_tensor("iwp_s", [128, NDC * 4], F32))
    binp = e(nc.sbuf_tensor("binp_s", [128, NOC], F32))
    boutp = e(nc.sbuf_tensor("boutp_s", [128, NDC], F32))
    sbp = e(nc.sbuf_tensor("sbp_s", [128, NOC], F32))
    ibp = e(nc.sbuf_tensor("ibp_s", [128, NDC], F32))
    cst = e(nc.sbuf_tensor("cst_ln", [128, 2], F32))
    flz = e(nc.sbuf_tensor("flz", [1, 128], I32))
    flone = e(nc.sbuf_tensor("flone", [1, 128], I32))
    flrd = e(nc.sbuf_tensor("flrd", [1, 4], I32))
    idx_all = e(nc.sbuf_tensor("idx_all", [128, 256], I32))
    gate_all = e(nc.sbuf_tensor("gate_all", [128, 256], F32))

    ps_ctx = ExitStack()
    pe = ps_ctx.enter_context
    psA = pe(nc.psum_tensor("psA", [128, 1024], F32))
    psB = pe(nc.psum_tensor("psB", [128, 1024], F32))
    psL = pe(nc.psum_tensor("psL", [128, 1024], F32))
    psS = pe(nc.psum_tensor("psS", [128, 1024], F32))
    slots = [psA, psB]

    p1_ctx = ExitStack()
    p1 = p1_ctx.enter_context
    m0a = p1(nc.sbuf_tensor("m0a", [128, 64], F32))
    m1a = p1(nc.sbuf_tensor("m1a", [128, 64], F32))
    i0a = p1(nc.sbuf_tensor("i0a", [128, 64], U32))
    i1a = p1(nc.sbuf_tensor("i1a", [128, 64], U32))
    sc8 = p1(nc.sbuf_tensor("sc8", [128, 24], F32))
    rowb = p1(nc.sbuf_tensor("rowb", [1, 1024], F32))
    rowt = p1(nc.sbuf_tensor("rowt", [1, 512], F32))
    lnb = p1(nc.sbuf_tensor("lnb", [128, 1024], F32))
    wi16 = [p1(nc.sbuf_tensor(f"wi16_{k}", [128, D3], BF16)) for k in range(NDC)]
    wo16 = [p1(nc.sbuf_tensor(f"wo16_{k}", [128, D], BF16)) for k in range(NDC)]
    keff = [p1(nc.sbuf_tensor(f"keff_{k}", [128, 2048], BF16)) for k in range(NDC)]
    wqb = [p1(nc.sbuf_tensor(f"wqb_{k}", [128, D], BF16)) for k in range(4)]
    ktb = [p1(nc.sbuf_tensor(f"ktb_{k}", [128, 512], BF16)) for k in range(2)]
    dgA = [p1(nc.sbuf_tensor(f"dgA_{k}", [128, 128], BF16)) for k in range(12)]
    dgB = [p1(nc.sbuf_tensor(f"dgB_{k}", [128, 128], BF16)) for k in range(8)]
    ut = p1(nc.sbuf_tensor("ut", [128, 8 * 512], F32))
    u16sq = p1(nc.sbuf_tensor("u16sq", [128, 4096], BF16))
    xa = p1(nc.sbuf_tensor("xa", [128, 8 * 512], BF16))
    x2 = xa
    x1pair = p1(nc.sbuf_tensor("x1pair", [128, 1028], BF16))
    x1pairB = p1(nc.sbuf_tensor("x1pairB", [128, 1028], BF16))
    x1h = p1(nc.sbuf_tensor("x1h", [128, 48], BF16))
    convpair = p1(nc.sbuf_tensor("convpair", [128, 1024], F32))
    kvt = p1(nc.sbuf_tensor("kvt", [128, 515], BF16))
    kvtB = p1(nc.sbuf_tensor("kvtB", [128, 515], BF16))
    kvh = p1(nc.sbuf_tensor("kvh", [128, NDC * 3], BF16))
    vvt = p1(nc.sbuf_tensor("vvt", [128, 8 * 512], BF16))
    y16 = p1(nc.sbuf_tensor("y16", [128, 8 * 512], BF16))
    r1sb = p1(nc.sbuf_tensor("r1sb", [128, 8 * 512], F32))

    for dst, srcd in ((swp, sw_d), (iwp, iw_d), (eyeF, eye_d), (binp, bin_d),
                      (boutp, bout_d), (sbp, sb_d), (ibp, ib_d)):
        C.inc(SY.dma_start(out=dst[:, :], in_=srcd[:, :]), "qC", 16)
    qs_consts = C.cnt["qC"]

    for k in range(NDC):
        C.inc(G.dma_start(out=wi16[k][:, :], in_=wi_d[k * 128:(k + 1) * 128, :]), "qWA", 16)
    for k in range(NDC):
        C.inc(G.dma_start(out=wo16[k][:, :], in_=wo_d[k * 128:(k + 1) * 128, :]), "qWA", 16)
    qg_attnw = C.cnt["qWA"]
    C.inc(G.dma_start(out=ktb[0].ap().rearrange("p (c k) -> p c k", c=4),
                      in_=kt_d[0:1, :, :].rearrange("s (c p) k -> p (s c) k", p=128)), "qWQ", 16)
    for dkc in range(4):
        C.inc(G.dma_start(out=wqb[dkc][:, :],
                          in_=wq_d[dkc * 128:dkc * 128 + 128, :]), "qWQ", 16)

    C.wait(V, "V", "qC", qs_consts)
    vop(V.memset(cst[:, 0:1], EPS))
    vop(V.memset(cst[:, 1:2], -0.5))
    nc.const_aps.register(F32, EPS, cst.ap()[:, 0:1])
    nc.const_aps.register(F32, -0.5, cst.ap()[:, 1:2])
    vop(V.memset(ones16[:, :], 1.0))
    vop(V.memset(ones32[:, :], 1.0))
    vop(V.tensor_copy(eye16[:, :], eyeF[:, :]))
    vop(V.memset(idx_all[:, :], 0))
    vop(V.memset(gate_all[:, :], 0.0))
    vop(V.memset(x1h[:, :], 0.0))
    vop(V.memset(kvt[:, :], 0.0))
    vop(V.memset(kvtB[:, :], 0.0))
    vop(V.memset(kvh[:, :], 0.0))
    vop(V.memset(flz[:, :], 0))
    vop(V.memset(flone[:, :], 1))
    sv_p0 = C.cnt["sV"]

    pid1 = nc.gpsimd.alloc_register("pid1")
    pv = G.partition_id()
    G.reg_alu(pid1, pv, 1, ALU.bitwise_and)
    C.wait(G, "P", "sV", sv_p0)
    with G.If_eq(pid1, 0):
        G.dma_start(out=shf[0:1, :], in_=flz[0:1, :]).then_inc(C.sems["qWC"], 16)
    with G.Else():
        G.dma_start(out=shf[1:2, :], in_=flz[0:1, :]).then_inc(C.sems["qWC"], 16)
    C.cnt["qWC"] += 16
    qwc_zero = C.cnt["qWC"]

    NCV = 16

    def conv_chunk(cc):
        r0e = cc * 512
        r0o = 8192 + cc * 512
        with G.If_eq(pid1, 0):
            G.dma_start(out=comb16[r0e:r0e + 512, :],
                        in_=comb_d[r0e:r0e + 512, :]).then_inc(C.sems["qWC"], 16)
        with G.Else():
            G.dma_start(out=comb16[r0o:r0o + 512, :],
                        in_=comb_d[r0o:r0o + 512, :]).then_inc(C.sems["qWC"], 16)
        C.cnt["qWC"] += 16

    u_rearr = u_d.rearrange("(c p) t -> p c t", p=128)
    ut3 = ut.ap().rearrange("p (c t) -> p c t", c=NDC)
    C.inc(SY.dma_start(out=ut3, in_=u_rearr[:, :, 0:TILE]), "qU", 16)
    qs_u = {0: C.cnt["qU"]}

    gd = dict(
        st_stats_prev=0, sv_rows=0, sa_lnb=0, sv_apply=0,
        sv_argS=0, sv_argL=0,
        st_x2_read=0, st_xa_read=0, qs_x2dma=0, qs_r1dma=0,
        sv_haloout=0, sv_kvh=0,
        st_conv3=0, st_conv4=0, sp_y_last=0,
        sa_vv_last=0, sp_convpair=0, sp_kv=0,
        st_wout_last=0, sv_ln2_apply=0,
        sa_keff=0,
        si=0,
    )
    gd["dgA_mark"] = [0, 0]
    gd["dgA_par"] = 0
    gd["dgB_mark"] = [0, 0]
    gd["dgB_par"] = 0
    gd["slot_sa"] = [("sA", 0), ("sA", 0)]
    gd["st_conv3p"] = [0, 0]
    gd["st_conv4p"] = [0, 0]
    gd["st_stats_par"] = [0, 0]
    gd["kv_aux"] = {}
    keff_done = [False]

    def next_slot():
        s = gd["si"] % 2
        gd["si"] += 1
        return s

    xa3 = xa.ap().rearrange("p (c t) -> p c t", c=NDC)
    tmp3 = u16sq.ap().rearrange("p (c t) -> p c t", c=NDC)
    x1v = x1pair.ap().rearrange("p (w c) -> p w c", w=2)
    x1vB = x1pairB.ap().rearrange("p (w c) -> p w c", w=2)
    x1vs = [x1v, x1vB]
    x1pairs = [x1pair, x1pairB]
    kvts = [kvt, kvtB]

    def ln_stats_piece(src, piece, s_st, first, last_):
        lo = piece * 1024
        par = piece % 2
        base = par * 2048
        u16 = u16sq.ap()[:, base:base + 1024]
        sq16 = u16sq.ap()[:, base + 1024:base + 2048]
        C.wait(V, "V", "sT", gd["st_stats_par"][par])
        vwait_self()
        vop(V.tensor_copy(u16, src[:, lo:lo + 1024]))
        vwait_self()
        vop(V.tensor_mul(sq16, u16, u16))
        sv_c = C.cnt["sV"]
        ps = psL if s_st is None else slots[s_st]
        C.wait(T, "T", "sV", sv_c)
        if first and s_st is not None:
            C.wait(T, "T", gd["slot_sa"][s_st][0], gd["slot_sa"][s_st][1])
        last = None
        for q2 in range(2):
            T.matmul(ps[0:1, 0:512], ones16[:, :], u16[:, q2 * 512:(q2 + 1) * 512],
                     start=(first and q2 == 0), stop=(last_ and q2 == 1))
            last = T.matmul(ps[0:1, 512:1024], ones16[:, :],
                            sq16[:, q2 * 512:(q2 + 1) * 512],
                            start=(first and q2 == 0), stop=(last_ and q2 == 1))
        st = C.inc(last, "sT")
        gd["st_stats_prev"] = st
        gd["st_stats_par"][par] = st
        return st

    def ln_rows_bcast(s_st, s_bc):
        ps = psL if s_st is None else slots[s_st]
        C.wait(V, "V", "sT", gd["st_stats_prev"])
        vwait_self()
        vop(V.tensor_scalar(rowb[0:1, 0:1024], ps[0:1, 0:1024], 1.0 / D, None, op0=ALU.mult))
        vwait_self()
        vop(V.tensor_mul(rowt[0:1, :], rowb[0:1, 0:512], rowb[0:1, 0:512]))
        vwait_self()
        vop(V.tensor_sub(rowt[0:1, :], rowb[0:1, 512:1024], rowt[0:1, :]))
        sv_rows = C.cnt["sV"]
        if s_st is not None:
            gd["slot_sa"][s_st] = ("sV", sv_rows)
        C.wait(A, "A", "sV", sv_rows)
        C.inc(A.activation(rowt[0:1, :], rowt[0:1, :], AF.Ln, bias=EPS), "sA")
        C.wait(A, "A", "sA", C.cnt["sA"])
        C.inc(A.activation(rowt[0:1, :], rowt[0:1, :], AF.Exp, scale=-0.5), "sA")
        sa_rstd = C.cnt["sA"]
        C.wait(V, "V", "sA", sa_rstd)
        vwait_self()
        vop(V.scalar_tensor_tensor(rowb[0:1, 512:1024], rowb[0:1, 0:512], -1.0,
                                   rowt[0:1, :], op0=ALU.mult, op1=ALU.mult))
        sv_rw = C.cnt["sV"]
        ps2 = slots[s_bc]
        C.wait(T, "T", "sV", sv_rw)
        C.wait(T, "T", gd["slot_sa"][s_bc][0], gd["slot_sa"][s_bc][1])
        T.matmul(ps2[:, 0:512], ones32[:, :], rowt[0:1, :], start=True, stop=True)
        st_b = C.inc(T.matmul(ps2[:, 512:1024], ones32[:, :], rowb[0:1, 512:1024],
                              start=True, stop=True), "sT")
        C.wait(A, "A", "sT", st_b)
        C.wait(A, "A", "sV", gd["sv_apply"])
        gd["sa_lnb"] = C.inc(A.copy(lnb[:, :], ps2[:, :]), "sA")
        gd["slot_sa"][s_bc] = ("sA", gd["sa_lnb"])
        return gd["sa_lnb"]

    def ln_apply(src_sb, extra_T_wait=0):
        src3 = src_sb.ap().rearrange("p (c t) -> p c t", c=NDC)
        rstd_b = lnb.ap()[:, 0:512].rearrange("p (o t) -> p o t", o=1).broadcast_to([128, 1, 512])
        mrstd_b = lnb.ap()[:, 512:1024].rearrange("p (o t) -> p o t", o=1).broadcast_to([128, 1, 512])
        C.wait(V, "V", "sA", gd["sa_lnb"])
        C.wait(V, "V", "sT", gd["st_stats_prev"])
        C.wait(V, "V", "sT", max(gd["st_x2_read"], gd["st_xa_read"], extra_T_wait))
        C.wait(V, "V", "qSPX", gd["qs_x2dma"])
        marks = []
        for dc in range(0, NDC, 2):
            vwait_self()
            vop(V.tensor_mul(tmp3[:, dc:dc + 2, :], src3[:, dc:dc + 2, :],
                             rstd_b.broadcast_to([128, 2, 512])))
            vwait_self()
            vop(V.tensor_tensor(xa3[:, dc:dc + 2, :], tmp3[:, dc:dc + 2, :],
                                mrstd_b.broadcast_to([128, 2, 512]), op=ALU.add))
            marks.extend([C.cnt["sV"]] * 2)
        gd["sv_apply"] = C.cnt["sV"]
        gd["sv_apply_dc"] = marks
        return gd["sv_apply"]

    def win_pair(ocA, ocB, sv_src, wtab, src_sb):
        s = next_slot()
        ps = slots[s]
        C.wait(T, "T", "qWA", qg_attnw)
        chase = isinstance(sv_src, list)
        if not chase:
            C.wait(T, "T", "sV", sv_src)
        C.wait(T, "T", gd["slot_sa"][s][0], gd["slot_sa"][s][1])
        last = None
        for k in range(NDC):
            if chase:
                C.wait(T, "T", "sV", sv_src[k])
            T.matmul(ps[:, 0:512], wtab[k][:, ocA * 128:(ocA + 1) * 128],
                     src_sb[:, k * 512:(k + 1) * 512],
                     start=(k == 0), stop=(k == NDC - 1))
            last = T.matmul(ps[:, 512:1024], wtab[k][:, ocB * 128:(ocB + 1) * 128],
                            src_sb[:, k * 512:(k + 1) * 512],
                            start=(k == 0), stop=(k == NDC - 1))
        return s, C.inc(last, "sT")

    def evac_to_x1(s, st_win, hslot, ebA, ebB, jpar):
        xv = x1vs[jpar]
        C.wait(A, "A", "sT", st_win)
        C.wait(A, "A", "sT", gd["st_conv3p"][jpar])
        C.wait(A, "A", "sV", gd["sv_haloout"])
        A.activation(xv[:, 0:1, 2:514], slots[s][:, 0:512], AF.Identity,
                     bias=binp[:, ebA:ebA + 1])
        sa = C.inc(A.activation(xv[:, 1:2, 2:514], slots[s][:, 512:1024], AF.Identity,
                                bias=binp[:, ebB:ebB + 1]), "sA")
        C.wait(V, "V", "sA", sa)
        C.wait(V, "V", "sT", gd["st_conv3p"][jpar])
        vwait_self()
        vop(V.tensor_copy(xv[:, :, 0:2],
                          x1h.ap()[:, hslot * 4:hslot * 4 + 4].rearrange(
                              "p (w c) -> p w c", w=2)))
        sv_hin = C.cnt["sV"]
        vop(V.tensor_copy(x1h.ap()[:, hslot * 4:hslot * 4 + 4].rearrange(
            "p (w c) -> p w c", w=2), xv[:, :, 512:514]))
        gd["sv_haloout"] = C.cnt["sV"]
        return sa, sv_hin

    def conv3_pair(s, ocA, ocB, sa_x1, sv_hin, jpar):
        par = gd["dgA_par"] = 1 - gd["dgA_par"]
        C.wait(V, "V", "sT", gd["dgA_mark"][par])
        vwait_self()
        for half, oc in ((0, ocA), (1, ocB)):
            for k in range(3):
                vop(V.tensor_scalar(dgA[par * 6 + half * 3 + k][:, :], eyeF[:, :],
                                    swp[:, oc * 3 + k:oc * 3 + k + 1], None, op0=ALU.mult))
        sv_dg = C.cnt["sV"]
        ps = slots[s]
        C.wait(T, "T", "sA", sa_x1)
        C.wait(T, "T", "sV", max(sv_dg, gd["sv_haloout"], sv_hin, sv_p0))
        last = None
        for half, oc in ((0, ocA), (1, ocB)):
            for k in range(3):
                last = T.matmul(ps[:, half * 512:(half + 1) * 512],
                                dgA[par * 6 + half * 3 + k][:, :],
                                x1pairs[jpar][:, half * 514 + k:half * 514 + k + 512],
                                start=(k == 0), stop=(k == 2))
        st = C.inc(last, "sT")
        gd["st_conv3p"][jpar] = st
        gd["dgA_mark"][par] = st
        return st

    def evac_to_convpair(s, st_c3, btab=None, ebA=0, ebB=0):
        C.wait(A, "A", "sT", st_c3)
        C.wait(A, "A", "sP", gd["sp_convpair"])
        if btab is None:
            sa = C.inc(A.copy(convpair[:, :], slots[s][:, :]), "sA")
        else:
            A.activation(convpair[:, 0:512], slots[s][:, 0:512], AF.Identity,
                         bias=btab[:, ebA:ebA + 1])
            sa = C.inc(A.activation(convpair[:, 512:1024], slots[s][:, 512:1024],
                                    AF.Identity, bias=btab[:, ebB:ebB + 1]), "sA")
        gd["slot_sa"][s] = ("sA", sa)
        return sa

    sv_ln1_pending = [None]

    for t in range(nt):
        t0 = t * TILE
        if t == 0:
            C.wait(V, "V", "qU", qs_u[0])
            s_st = next_slot()
            for piece in range(4):
                ln_stats_piece(ut.ap(), piece, s_st, piece == 0, piece == 3)
            s_bc = next_slot()
            ln_rows_bcast(s_st, s_bc)
            sv_ln1 = ln_apply(ut)
        else:
            sv_ln1 = sv_ln1_pending[0]

        ln1_dc = gd["sv_apply_dc"]
        jobs = [("kv", i) for i in range(NDC)] + [("q", j) for j in range(4)]

        def job_win(idx, job):
            kind, i = job
            if kind == "kv":
                ocA, ocB, hslot = 8 + i, 16 + i, i
            else:
                ocA, ocB, hslot = 2 * i, 2 * i + 1, 8 + i
            s, st_w = win_pair(ocA, ocB, ln1_dc if idx == 0 else sv_ln1, wi16, xa)
            gd["st_xa_read"] = st_w
            return [s, ocA, ocB, st_w, hslot, idx % 2]

        def job_evac(hd):
            s, ocA, ocB, st_w, hslot, jpar = hd
            sa_x1, sv_hin = evac_to_x1(s, st_w, hslot, ocA, ocB, jpar)
            hd[3] = (sa_x1, sv_hin)

        def job_tail1(job, hd):
            kind, i = job
            s, ocA, ocB, (sa_x1, sv_hin), hslot, jpar = hd
            st_c3 = conv3_pair(s, ocA, ocB, sa_x1, sv_hin, jpar)
            sa_cp = evac_to_convpair(s, st_c3, sbp, ocA, ocB)
            if kind == "kv":
                par = i % 2
                kv = kvts[par]
                C.wait(P, "P", "sA", sa_cp)
                C.wait(P, "P", "sT", gd["st_conv4p"][par])
                C.wait(P, "P", "sV", gd["sv_kvh"])
                sp_kv = C.inc(P.tensor_mul(kv[:, 3:515], convpair[:, 0:512],
                                           convpair[:, 512:1024]), "sP")
                gd["sp_convpair"] = sp_kv
                C.wait(V, "V", "sP", sp_kv)
                C.wait(V, "V", "sT", gd["st_conv4p"][par])
                vwait_self()
                vop(V.tensor_copy(kv[:, 0:3], kvh[:, i * 3:i * 3 + 3]))
                vop(V.tensor_copy(kvh[:, i * 3:i * 3 + 3], kv[:, 512:515]))
                gd["sv_kvh"] = C.cnt["sV"]
                dpar = gd["dgB_par"] = 1 - gd["dgB_par"]
                C.wait(V, "V", "sT", gd["dgB_mark"][dpar])
                vwait_self()
                for k in range(4):
                    vop(V.tensor_scalar(dgB[dpar * 4 + k][:, :], eyeF[:, :],
                                        iwp[:, i * 4 + k:i * 4 + k + 1], None,
                                        op0=ALU.mult))
                gd["kv_aux"][i] = (par, dpar, C.cnt["sV"], sp_kv)
            else:
                C.wait(P, "P", "sA", sa_cp)
                C.wait(P, "P", "sA", gd["sa_vv_last"])
                C.wait(P, "P", "sT", gd["st_wout_last"])
                P.tensor_mul(y16[:, ocA * 512:(ocA + 1) * 512], convpair[:, 0:512],
                             vvt[:, ocA * 512:(ocA + 1) * 512])
                spy = C.inc(P.tensor_mul(y16[:, ocB * 512:(ocB + 1) * 512],
                                         convpair[:, 512:1024],
                                         vvt[:, ocB * 512:(ocB + 1) * 512]), "sP")
                gd["sp_convpair"] = spy
                gd["sp_y_last"] = spy

        def job_tail2(job):
            kind, i = job
            par, dpar, sv_dg4, sp_kv = gd["kv_aux"][i]
            kv = kvts[par]
            C.wait(T, "T", "sP", sp_kv)
            C.wait(T, "T", "sV", max(sv_dg4, gd["sv_kvh"]))
            C.wait(T, "T", "sA", gd["sa_vv_last"])
            C.wait(T, "T", "sV", gd["sv_argS"])
            last = None
            for k in range(4):
                last = T.matmul(psS[:, 0:512], dgB[dpar * 4 + k][:, :], kv[:, k:k + 512],
                                start=(k == 0), stop=(k == 3))
            st_c4 = C.inc(last, "sT")
            gd["st_conv4p"][par] = st_c4
            gd["dgB_mark"][dpar] = st_c4
            C.wait(A, "A", "sT", st_c4)
            gd["sa_vv_last"] = C.inc(
                A.activation(vvt[:, i * 512:(i + 1) * 512], psS[:, 0:512],
                             AF.Identity, bias=ibp[:, i:i + 1]), "sA")

        heads = {}
        for idx, job in enumerate(jobs):
            heads[idx] = job_win(idx, job)
            if idx >= 1:
                job_tail1(jobs[idx - 1], heads[idx - 1])
            if idx >= 2 and jobs[idx - 2][0] == "kv":
                job_tail2(jobs[idx - 2])
            job_evac(heads[idx])
        job_tail1(jobs[-1], heads[len(jobs) - 1])
        sp_y = gd["sp_y_last"]

        sp_r1 = 0
        st_wo = 0
        for j in range(4):
            ocA, ocB = 2 * j, 2 * j + 1
            s = next_slot()
            ps = slots[s]
            C.wait(T, "T", "sP", sp_y)
            C.wait(T, "T", gd["slot_sa"][s][0], gd["slot_sa"][s][1])
            last = None
            for k in range(NDC):
                T.matmul(ps[:, 0:512], wo16[k][:, ocA * 128:(ocA + 1) * 128],
                         y16[:, k * 512:(k + 1) * 512], start=(k == 0), stop=(k == NDC - 1))
                last = T.matmul(ps[:, 512:1024], wo16[k][:, ocB * 128:(ocB + 1) * 128],
                                y16[:, k * 512:(k + 1) * 512], start=(k == 0), stop=(k == NDC - 1))
            st_wo = C.inc(last, "sT")
            C.wait(A, "A", "sT", st_wo)
            C.wait(A, "A", "sP", gd["sp_convpair"])
            A.activation(convpair[:, 0:512], ps[:, 0:512], AF.Identity,
                         bias=boutp[:, ocA:ocA + 1])
            sa_o = C.inc(A.activation(convpair[:, 512:1024], ps[:, 512:1024],
                                      AF.Identity, bias=boutp[:, ocB:ocB + 1]), "sA")
            gd["slot_sa"][s] = ("sA", sa_o)
            C.wait(P, "P", "sA", sa_o)
            C.wait(P, "P", "qSPR", gd["qs_r1dma"])
            C.wait(P, "P", "sV", gd["sv_ln2_apply"])
            sp_r1 = C.inc(P.tensor_add(r1sb[:, ocA * 512:(ocB + 1) * 512],
                                       convpair[:, :],
                                       ut[:, ocA * 512:(ocB + 1) * 512]), "sP")
            gd["sp_convpair"] = sp_r1
            C.wait(SY, "S", "sP", sp_r1)
            C.inc(SY.dma_start(
                out=r1_d[ocA:ocB + 1, :, t0:t0 + TILE].rearrange("c p t -> p c t"),
                in_=r1sb.ap()[:, ocA * 512:(ocB + 1) * 512].rearrange(
                    "p (c t) -> p c t", c=2)), "qSPR", 16)
            if j > 0:
                C.wait(V, "V", "sP", sp_r1_prev)
                ln_stats_piece(r1sb.ap(), j - 1, None, j == 1, False)
            sp_r1_prev = sp_r1
        gd["st_wout_last"] = st_wo
        gd["qs_r1dma"] = C.cnt["qSPR"]
        sp_r1_done = sp_r1
        C.wait(V, "V", "sP", sp_r1_done)
        ln_stats_piece(r1sb.ap(), 3, None, False, True)

        if t + 1 < nt:
            C.wait(SY, "S", "sP", sp_r1_done)
            C.inc(SY.dma_start(out=ut3, in_=u_rearr[:, :, (t + 1) * TILE:(t + 2) * TILE]),
                  "qU", 16)
            qs_u[t + 1] = C.cnt["qU"]

        if not keff_done[0]:
            keff_done[0] = True
            keff_mark = 0
            s = next_slot()
            ps = slots[s]
            first = True
            for ph in range(16):
                C.wait(T, "T", "qWQ", C.cnt["qWQ"])
                for dc in range(NDC):
                    if first:
                        C.wait(T, "T", gd["slot_sa"][s][0], gd["slot_sa"][s][1])
                    else:
                        C.wait(T, "T", "sA", gd["sa_keff"])
                    first = False
                    last = None
                    for dkc in range(4):
                        last = T.matmul(ps[:, 0:128],
                                        wqb[dkc][:, dc * 128:(dc + 1) * 128],
                                        ktb[ph % 2][:, dkc * 128:(dkc + 1) * 128],
                                        start=(dkc == 0), stop=(dkc == 3))
                    stk = C.inc(last, "sT")
                    if dc == NDC - 1:
                        keff_mark = stk
                    C.wait(A, "A", "sT", stk)
                    gd["sa_keff"] = C.inc(
                        A.copy(keff[dc][:, ph * 128:(ph + 1) * 128], ps[:, 0:128]), "sA")
                if ph + 1 < 16:
                    C.wait(G, "P", "sT", keff_mark)
                    for dkc in range(4):
                        r0 = (ph + 1) * 512 + dkc * 128
                        C.inc(G.dma_start(out=wqb[dkc][:, :],
                                          in_=wq_d[r0:r0 + 128, :]), "qWQ", 16)
                    C.inc(G.dma_start(
                        out=ktb[(ph + 1) % 2].ap().rearrange("p (c k) -> p c k", c=4),
                        in_=kt_d[ph + 1:ph + 2, :, :].rearrange(
                            "s (c p) k -> p (s c) k", p=128)), "qWQ", 16)
            gd["slot_sa"][s] = ("sA", gd["sa_keff"])

        s_bc2 = next_slot()
        ln_rows_bcast(None, s_bc2)
        sv_ln2 = ln_apply(r1sb)
        ln2_dc = gd["sv_apply_dc"]
        gd["sv_ln2_apply"] = sv_ln2
        for j in range(4):
            ocA = 2 * j
            C.wait(SY, "S", "sV", sv_ln2)
            C.inc(SY.dma_start(
                out=x2_d[ocA:ocA + 2, :, t0:t0 + TILE].rearrange("c p t -> p c t"),
                in_=x2.ap()[:, ocA * 512:(ocA + 2) * 512].rearrange(
                    "p (c t) -> p c t", c=2)), "qSPX", 16)
        gd["qs_x2dma"] = C.cnt["qSPX"]

        for gi in range(4):
            g = t * 4 + gi
            go = gi * 128
            st_sim = {}
            for p in range(2):
                dstp = psS if p == 0 else psL
                if not (gi == 0 and p == 0):
                    C.wait(T, "T", "sV", sv_ln2)
                if p == 0:
                    C.wait(T, "T", "sV", gd["sv_argS"])
                    C.wait(T, "T", "sA", gd["sa_vv_last"])
                else:
                    C.wait(T, "T", "sV", gd["sv_argL"])
                last = None
                for half in range(2):
                    for dc in range(NDC):
                        if gi == 0 and p == 0 and half == 0:
                            C.wait(T, "T", "sV", ln2_dc[dc])
                        last = T.matmul(dstp[:, half * 512:(half + 1) * 512],
                                        x2[:, dc * 512 + go:dc * 512 + go + 128],
                                        keff[dc][:, p * 1024 + half * 512:p * 1024 + (half + 1) * 512],
                                        start=(dc == 0), stop=(dc == NDC - 1))
                st_sim[p] = C.inc(last, "sT")
            gd["st_x2_read"] = st_sim[1]
            for p in range(2):
                dstp = psS if p == 0 else psL
                ma = m0a if p == 0 else m1a
                ia = i0a if p == 0 else i1a
                C.wait(V, "V", "sT", st_sim[p])
                vwait_self()
                for hh in range(H):
                    vop(V.max(ma[:, hh * 8:hh * 8 + 8], dstp[:, hh * 128:(hh + 1) * 128]))
                vwait_self()
                for hh in range(H):
                    vop(V.max_index(ia[:, hh * 8:hh * 8 + 8], ma[:, hh * 8:hh * 8 + 8],
                                    dstp[:, hh * 128:(hh + 1) * 128]))
                if p == 0:
                    gd["sv_argS"] = C.cnt["sV"]
                else:
                    gd["sv_argL"] = C.cnt["sV"]
            vwait_self()
            m0s = m0a.ap().rearrange("p (h e) -> p h e", e=8)[:, :, 0:1]
            m1s = m1a.ap().rearrange("p (h e) -> p h e", e=8)[:, :, 0:1]
            i0s = i0a.ap().rearrange("p (h e) -> p h e", e=8)[:, :, 0:1]
            i1s = i1a.ap().rearrange("p (h e) -> p h e", e=8)[:, :, 0:1]
            vop(V.tensor_add(sc8[:, 16:24], m0s, m1s))
            vwait_self()
            vop(V.tensor_scalar_max(gate_all[:, g * 8:(g + 1) * 8], sc8[:, 16:24], 0.0))
            vop(V.tensor_copy(sc8[:, 0:8], i0s))
            vop(V.tensor_copy(sc8[:, 8:16], i1s))
            vwait_self()
            vop(V.scalar_tensor_tensor(sc8[:, 16:24], sc8[:, 0:8], 128.0,
                                       sc8[:, 8:16], op0=ALU.mult, op1=ALU.add))
            vwait_self()
            vop(V.tensor_copy(idx_all[:, g * 8:(g + 1) * 8], sc8[:, 16:24]))

        if t + 1 < nt:
            C.wait(V, "V", "qU", qs_u[t + 1])
            s_st = next_slot()
            for piece in range(4):
                ln_stats_piece(ut.ap(), piece, s_st, piece == 0, piece == 3)
            s_bc = next_slot()
            ln_rows_bcast(s_st, s_bc)
            sv_ln1_pending[0] = ln_apply(ut, extra_T_wait=gd["st_x2_read"])

        for cc in (2 * t, 2 * t + 1):
            if cc < NCV:
                conv_chunk(cc)

    for cc in range(2 * nt, NCV):
        conv_chunk(cc)

    C.wait(G, "P", "qWC", C.cnt["qWC"])
    with G.If_eq(pid1, 0):
        G.dma_start(out=shf[0:1, :], in_=flone[0:1, :]).then_inc(C.sems["qWC"], 16)
    with G.Else():
        G.dma_start(out=shf[1:2, :], in_=flone[0:1, :]).then_inc(C.sems["qWC"], 16)
    C.cnt["qWC"] += 16
    C.wait(G, "P", "qWC", C.cnt["qWC"])
    with nc.gpsimd.register("pcnt") as pcnt, nc.gpsimd.register("pcond") as pcond:
        G.reg_mov(pcnt, 0)
        G.reg_mov(pcond, 1)
        with G.While(pcond):
            G.reg_add(pcnt, pcnt, 16)
            with G.If_eq(pid1, 0):
                G.dma_start(out=flrd[0:1, 0:1], in_=shf[1:2, 0:1]).then_inc(C.sems["qPF"], 16)
            with G.Else():
                G.dma_start(out=flrd[0:1, 0:1], in_=shf[0:1, 0:1]).then_inc(C.sems["qPF"], 16)
            G.wait_ge(C.sems["qPF"], pcnt)
            G.reg_load(pcond, flrd[0:1, 0:1])
            G.reg_alu(pcond, pcond, 1, ALU.bitwise_xor)

    endT, endV, endA, endP = C.cnt["sT"], C.cnt["sV"], C.cnt["sA"], C.cnt["sP"]

    p1_ctx.close()
    ps_ctx.close()

    b2_ctx = ExitStack()
    b2 = b2_ctx.enter_context
    wif16 = [b2(nc.sbuf_tensor(f"wif16_{k}", [128, DI], BF16)) for k in range(NDC)]
    wof16 = [b2(nc.sbuf_tensor(f"wof16_{k}", [128, D], BF16)) for k in range(NIC)]
    ainb0 = [b2(nc.sbuf_tensor(f"ainb_{h}", [128, 1024], BF16)) for h in range(H)]
    binb0 = [b2(nc.sbuf_tensor(f"binb_{h}", [128, 2048], BF16)) for h in range(H)]
    aoutb0 = [b2(nc.sbuf_tensor(f"aoutb_{h}", [128, 2048], BF16)) for h in range(H)]
    boutb0 = [b2(nc.sbuf_tensor(f"boutb_{h}", [128, 1024], BF16)) for h in range(H)]
    ainb = [ainb0, ainb0]
    binb = [binb0, binb0]
    aoutb = [aoutb0, aoutb0]
    boutb = [boutb0, boutb0]
    x2g2 = [b2(nc.sbuf_tensor(f"x2g_{k}", [128, 1024], BF16)) for k in range(2)]
    r1g = b2(nc.sbuf_tensor("r1g", [128, 1024], F32))
    xtok = b2(nc.sbuf_tensor("xtok", [128, 1024], BF16))
    htok = b2(nc.sbuf_tensor("htok", [128, 2048], BF16))
    hfm = b2(nc.sbuf_tensor("hfm", [128, 2048], BF16))
    pj4 = b2(nc.sbuf_tensor("pj4", [128, 4096], BF16))
    pjin = b2(nc.sbuf_tensor("pjin", [128, 1024], BF16))
    pjout2 = b2(nc.sbuf_tensor("pjout2", [128, 2048], BF16))
    linr = b2(nc.sbuf_tensor("linr", [128, 16], F32))
    ling = b2(nc.sbuf_tensor("ling", [128, 16], F32))
    dgb = [b2(nc.sbuf_tensor(f"dgb_{h}", [128, 128], BF16)) for h in range(2 * H)]
    outg = b2(nc.sbuf_tensor("outg", [128, 1024], F32))

    psH = b2(nc.psum_tensor("psH", [128, 2048], F32))
    psO = b2(nc.psum_tensor("psO", [128, 1024], F32))
    psT2 = b2(nc.psum_tensor("psT2", [128, 512], F32))
    psT3 = b2(nc.psum_tensor("psT3", [128, 512], F32))

    C.wait(G, "P", "sT", endT)
    C.wait(G, "P", "sV", endV)
    C.wait(G, "P", "sA", endA)
    C.wait(G, "P", "sP", endP)
    for k in range(NDC):
        C.inc(G.dma_start(out=wif16[k][:, :], in_=wif_d[k * 128:(k + 1) * 128, :]), "qWF", 16)
    for k in range(NIC):
        C.inc(G.dma_start(out=wof16[k][:, :], in_=wof_d[k * 128:(k + 1) * 128, :]), "qWF", 16)
    qg_ffnw = C.cnt["qWF"]
    C.wait(SY, "S", "sT", endT)
    C.wait(SY, "S", "sA", endA)
    C.wait(SY, "S", "sV", endV)
    C.wait(SY, "S", "sP", endP)
    C.wait(T, "T", "sV", endV)
    C.wait(T, "T", "sA", endA)
    C.wait(A, "A", "sT", endT)
    C.wait(A, "A", "sV", endV)
    C.wait(V, "V", "sT", endT)
    C.wait(V, "V", "sA", endA)
    C.wait(P, "P", "sT", endT)

    bg = dict(st_bin=0, st_bout=0, sa_gelu=0, sa_out=0, qs_y=0,
              sa_xtok=0, sa_hfm=[0, 0, 0, 0], sv_amul_in=0, sv_amul_out=0,
              sv_ling_in=0, sv_ling_out=0, sa_red_in=[0] * H, sa_red_out=[0] * H)

    GOFF = {0: (0, 1024, ainb, "qAI"), 1: (1024, 2048, binb, "qBI"),
            2: (3072, 2048, aoutb, "qAO"), 3: (5120, 1024, boutb, "qBO")}
    for qn in ("qAI", "qBI", "qAO", "qBO"):
        C.sem(qn)

    def emit_gathers(g, which, gate_sem, gate_val):
        off, width, bufs, qn = GOFF[which]
        C.wait(G, "P", gate_sem, gate_val)
        for h in range(H):
            C.inc(G.indirect_dma_start(
                bufs[g % 2][h][:, 0:width], None,
                comb16[:, :],
                IndirectOffsetOnAxis(idx_all.ap()[:, g * 8 + h:g * 8 + h + 1], 0),
                element_offset=off), qn, 16)
        return C.cnt[qn]

    qai = {}
    qbi = {}
    qao = {}
    qbo = {}
    qai[0] = qai[1] = emit_gathers(0, 0, "sT", 0)
    qbi[0] = qbi[1] = emit_gathers(0, 1, "sT", 0)
    qao[0] = qao[1] = emit_gathers(0, 2, "sT", 0)
    qbo[0] = qbo[1] = emit_gathers(0, 3, "sT", 0)
    x2g3 = [x2g2[k].ap().rearrange("p (c t) -> p c t", c=NDC) for k in range(2)]
    r1g3 = r1g.ap().rearrange("p (c t) -> p c t", c=NDC)
    C.inc(SY.dma_start(out=x2g3[0], in_=x2_d[:, :, 0:128].rearrange("c p t -> p c t")),
          "qX", 16)
    qx = C.cnt["qX"]
    C.inc(SY.dma_start(out=r1g3, in_=r1_d[:, :, 0:128].rearrange("c p t -> p c t")),
          "qR", 16)
    qr = C.cnt["qR"]

    for g in range(NG):
        par = g % 2
        x2g = x2g2[par]
        C.wait(T, "T", "qX", qx)
        for half in range(2):
            pst = psT2 if half == 0 else psT3
            C.wait(T, "T", "sA", max(bg["sa_xtok"], bg["sa_hfm"][half + 2]))
            last = None
            for c in range(4):
                cc = half * 4 + c
                last = T.matmul(pst[:, c * 128:(c + 1) * 128],
                                x2g[:, cc * 128:(cc + 1) * 128], eye16[:, :],
                                start=True, stop=True)
            st = C.inc(last, "sT")
            C.wait(A, "A", "sT", st)
            C.wait(A, "A", "sV", bg["sv_amul_in"])
            bg["sa_xtok"] = C.inc(
                A.copy(xtok[:, half * 512:(half + 1) * 512], pst[:, :]), "sA")

        C.wait(V, "V", "qAI", qai[par])
        C.wait(V, "V", "sA", bg["sa_xtok"])
        for h in range(H):
            vwait_self()
            vop(V.scalar_tensor_tensor(pjin[:, :], ainb[par][h][:, :], 1.0,
                                       xtok[:, :], op0=ALU.mult, op1=ALU.mult,
                                       accum_out=linr[:, h:h + 1]))
        bg["sv_amul_in"] = C.cnt["sV"]
        if g + 1 < NG:
            qai[0] = qai[1] = emit_gathers(g + 1, 0, "sV", bg["sv_amul_in"])
        vwait_self()
        vop(V.tensor_mul(ling[:, 0:8], linr[:, 0:8], gate_all[:, g * 8:(g + 1) * 8]))
        bg["sv_ling_in"] = C.cnt["sV"]
        vwait_self()
        for h in range(H):
            C.wait(V, "V", "sT", bg["st_bin"])
            vop(V.tensor_scalar(dgb[h][:, :], eyeF[:, :], ling[:, h:h + 1], None,
                                op0=ALU.mult))
        sv_dgin = C.cnt["sV"]

        C.wait(T, "T", "qWF", qg_ffnw)
        C.wait(T, "T", "qBI", qbi[par])
        C.wait(T, "T", "sA", bg["sa_gelu"])
        n_contrib = NDC + H
        ci = 0
        last1 = None
        for dc in range(NDC):
            lhs = x2g[:, dc * 128:(dc + 1) * 128]
            for qtr in range(4):
                last1 = T.matmul(psH[:, qtr * 512:(qtr + 1) * 512], lhs,
                                 wif16[dc][:, qtr * 512:(qtr + 1) * 512],
                                 start=(ci == 0), stop=(ci == n_contrib - 1))
            ci += 1
        C.wait(T, "T", "sV", sv_dgin)
        for h in range(H):
            for qtr in range(4):
                last1 = T.matmul(psH[:, qtr * 512:(qtr + 1) * 512], dgb[h][:, :],
                                 binb[par][h][:, qtr * 512:(qtr + 1) * 512],
                                 start=(ci == 0), stop=(ci == n_contrib - 1))
            ci += 1
        bg["st_bin"] = C.inc(last1, "sT")
        if g + 1 < NG:
            qbi[0] = qbi[1] = emit_gathers(g + 1, 1, "sT", bg["st_bin"])
            gsl = slice((g + 1) * 128, (g + 2) * 128)
            C.wait(SY, "S", "sT", bg["st_bin_prev"] if "st_bin_prev" in bg else 0)
            C.inc(SY.dma_start(out=x2g3[(g + 1) % 2],
                               in_=x2_d[:, :, gsl].rearrange("c p t -> p c t")), "qX", 16)
            qx = C.cnt["qX"]
        bg["st_bin_prev"] = bg["st_bin"]

        C.wait(A, "A", "sT", bg["st_bin"])
        C.wait(A, "A", "sV", bg["sv_amul_out"])
        bg["sa_gelu"] = C.inc(A.activation(htok[:, :], psH[:, :], AF.Gelu), "sA")

        C.wait(T, "T", "sA", bg["sa_gelu"])
        for half in range(4):
            pst = psT2 if half % 2 == 0 else psT3
            C.wait(T, "T", "sA",
                   bg["sa_xtok"] if half < 2 else bg["sa_hfm"][half - 2])
            last = None
            for c in range(4):
                cc = half * 4 + c
                last = T.matmul(pst[:, c * 128:(c + 1) * 128],
                                htok[:, cc * 128:(cc + 1) * 128], eye16[:, :],
                                start=True, stop=True)
            st = C.inc(last, "sT")
            C.wait(A, "A", "sT", st)
            C.wait(A, "A", "sT", bg["st_bout"])
            bg["sa_hfm"][half] = C.inc(
                A.copy(hfm[:, half * 512:(half + 1) * 512], pst[:, :]), "sA")

        C.wait(V, "V", "qAO", qao[par])
        C.wait(V, "V", "sA", bg["sa_gelu"])
        for h in range(4):
            hp = (h % 2) * 2048
            C.wait(V, "V", "sA", bg["sa_red_out"][h % 2])
            vwait_self()
            vop(V.tensor_mul(pj4[:, hp:hp + 2048], aoutb[par][4 + h][:, :], htok[:, :]))
            sv_m = C.cnt["sV"]
            C.wait(A, "A", "sV", sv_m)
            if h == 0:
                C.wait(A, "A", "sV", bg["sv_ling_out"])
            bg["sa_red_out"][h % 2] = C.inc(
                A.activation(pj4[:, hp:hp + 2048], pj4[:, hp:hp + 2048], AF.Copy,
                             accum_out=linr[:, 12 + h:13 + h]), "sA")
        for h in range(4):
            vwait_self()
            vop(V.scalar_tensor_tensor(pjout2[:, :], aoutb[par][h][:, :], 1.0,
                                       htok[:, :], op0=ALU.mult, op1=ALU.mult,
                                       accum_out=linr[:, 8 + h:9 + h]))
        C.wait(V, "V", "sA", bg["sa_red_out"][0])
        C.wait(V, "V", "sA", bg["sa_red_out"][1])
        bg["sv_amul_out"] = C.cnt["sV"]
        if g + 1 < NG:
            qao[0] = qao[1] = emit_gathers(g + 1, 2, "sV", bg["sv_amul_out"])
        vwait_self()
        vop(V.tensor_mul(ling[:, 8:16], linr[:, 8:16], gate_all[:, g * 8:(g + 1) * 8]))
        bg["sv_ling_out"] = C.cnt["sV"]
        vwait_self()
        for h in range(H):
            C.wait(V, "V", "sT", bg["st_bout"])
            vop(V.tensor_scalar(dgb[8 + h][:, :], eyeF[:, :], ling[:, 8 + h:9 + h], None,
                                op0=ALU.mult))
        sv_dgout = C.cnt["sV"]

        C.wait(T, "T", "qBO", qbo[par])
        C.wait(T, "T", "qR", qr)
        C.wait(T, "T", "sA", bg["sa_out"])
        last = None
        for c in range(NIC):
            for hf in range(2):
                last = T.matmul(psO[:, hf * 512:(hf + 1) * 512],
                                hfm[:, c * 128:(c + 1) * 128],
                                wof16[c][:, hf * 512:(hf + 1) * 512],
                                start=(c == 0), stop=False)
        C.wait(T, "T", "sV", sv_dgout)
        for h in range(H):
            for hf in range(2):
                last = T.matmul(psO[:, hf * 512:(hf + 1) * 512], dgb[8 + h][:, :],
                                boutb[par][h][:, hf * 512:(hf + 1) * 512],
                                start=False, stop=False)
        for dc in range(NDC):
            last = T.matmul(psO[:, dc * 128:(dc + 1) * 128],
                            r1g[:, dc * 128:(dc + 1) * 128], eyeF[:, :],
                            start=False, stop=(dc == 3 or dc == NDC - 1))
        bg["st_bout"] = C.inc(last, "sT")
        if g + 1 < NG:
            qbo[0] = qbo[1] = emit_gathers(g + 1, 3, "sT", bg["st_bout"])
            gsl = slice((g + 1) * 128, (g + 2) * 128)
            C.wait(SY, "S", "sT", bg["st_bout"])
            C.inc(SY.dma_start(out=r1g3,
                               in_=r1_d[:, :, gsl].rearrange("c p t -> p c t")), "qR", 16)
            qr = C.cnt["qR"]

        C.wait(A, "A", "sT", bg["st_bout"])
        C.wait(A, "A", "qY", bg["qs_y"])
        bg["sa_out"] = C.inc(A.copy(outg[:, :], psO[:, :]), "sA")
        C.wait(SY, "S", "sA", bg["sa_out"])
        C.inc(SY.dma_start(out=y_d[g * 128:(g + 1) * 128, :], in_=outg[:, :]), "qY", 16)
        bg["qs_y"] = C.cnt["qY"]

    if debug:
        C.wait(SY, "S", "sV", C.cnt["sV"])
        C.inc(SY.dma_start(out=idx_d[:, :], in_=idx_all[:, :]), "qY", 16)
        C.inc(SY.dma_start(out=gate_d[:, :], in_=gate_all[:, :]), "qY", 16)

    for s in ("qC", "qU", "qSPR", "qSPX", "qX", "qR", "qY"):
        C.wait(SY, "S", s, C.cnt[s])
    for s in ("qWA", "qWQ", "qWF", "qAI", "qBI", "qAO", "qBO", "qWC"):
        C.wait(G, "P", s, C.cnt[s])

    b2_ctx.close()
    ctx.close()
    return nc


def _host_prep(inputs):
    u = inputs["L0_attn_u"]
    comb = np.concatenate([inputs["L0_ffn_in_a"], inputs["L0_ffn_in_b"],
                           inputs["L0_ffn_out_a"], inputs["L0_ffn_out_b"]], axis=1)
    comb = np.ascontiguousarray(comb)
    keys = inputs["L0_ffn_keys"]
    kt = np.ascontiguousarray(keys.transpose(2, 0, 3, 1).reshape(16, 512, 128))
    base = {
        "combt": comb,
        "wi": np.ascontiguousarray(inputs["L0_attn_W_in"]),
        "wo": np.ascontiguousarray(inputs["L0_attn_W_out"]),
        "wq": np.ascontiguousarray(inputs["L0_ffn_Wq"]),
        "kt": kt,
        "wif": np.ascontiguousarray(inputs["L0_ffn_W_in"].T),
        "wof": np.ascontiguousarray(inputs["L0_ffn_W_out"].T),
        "binp": np.ascontiguousarray(inputs["L0_attn_b_in"].reshape(NOC, 128).T),
        "boutp": np.ascontiguousarray(inputs["L0_attn_b_out"].reshape(NDC, 128).T),
        "sbp": np.ascontiguousarray(inputs["L0_attn_short_b"].reshape(NOC, 128).T),
        "ibp": np.ascontiguousarray(inputs["L0_attn_inner_b"].reshape(NDC, 128).T),
        "swp": np.ascontiguousarray(
            inputs["L0_attn_short_w"].reshape(NOC, 128, 3).transpose(1, 0, 2).reshape(128, NOC * 3)),
        "iwp": np.ascontiguousarray(
            inputs["L0_attn_inner_w"].reshape(NDC, 128, 4).transpose(1, 0, 2).reshape(128, NDC * 4)),
    }
    u_fm = np.ascontiguousarray(u.transpose(0, 2, 1))
    return base, u_fm


def kernel(**inputs):
    base, u_fm = _host_prep(inputs)
    nc = build_nc(nt=8, debug=False)
    in_maps = [dict(base, u=u_fm[c]) for c in range(8)]
    res = run_bass_kernel_spmd(nc, in_maps, core_ids=list(range(8)))
    out = np.stack([np.asarray(r["yout"]) for r in res.results], axis=0)
    return out.astype(np.float32, copy=False)


# version 16
# speedup vs baseline: 1.8037x; 1.0011x over previous
import numpy as np
import concourse.bass as bass
import concourse.mybir as mybir
from concourse.bass_utils import run_bass_kernel_spmd
from concourse.bass import IndirectOffsetOnAxis
from contextlib import ExitStack

F32 = mybir.dt.float32
BF16 = mybir.dt.bfloat16
I32 = mybir.dt.int32
U32 = mybir.dt.uint32
AF = mybir.ActivationFunctionType
ALU = mybir.AluOpType

L = 4096
TILE = 512
GRP = 128
D = 1024
NDC = 8
D3 = 3072
NOC = 24
DI = 2048
NIC = 16
H = 8
NE = 16384
EPS = 1e-5


class Ctr:
    def __init__(self, nc):
        self.nc = nc
        self.sems = {}
        self.cnt = {}
        self.marks = {}

    def sem(self, name):
        s = self.nc.semaphore(name).__enter__()
        self.sems[name] = s
        self.cnt[name] = 0

    def inc(self, instr, name, by=1):
        self.cnt[name] += by
        instr.then_inc(self.sems[name], by)
        return self.cnt[name]

    def wait(self, eng, ename, name, val):
        if val <= 0:
            return
        key = (ename, name)
        if self.marks.get(key, -1) >= val:
            return
        eng.wait_ge(self.sems[name], val)
        self.marks[key] = val


def build_nc(nt=8, debug=False):
    nc = bass.Bass("TRN2", target_bir_lowering=False, debug=False,
                   detect_race_conditions=False)
    NG = (nt * TILE) // GRP

    u_d = nc.dram_tensor("u", [D, L], F32, kind="ExternalInput").ap()
    comb_d = nc.dram_tensor("combt", [NE, 6144], F32, kind="ExternalInput").ap()
    wi_d = nc.dram_tensor("wi", [D, D3], F32, kind="ExternalInput").ap()
    wo_d = nc.dram_tensor("wo", [D, D], F32, kind="ExternalInput").ap()
    wq_d = nc.dram_tensor("wq", [8192, D], F32, kind="ExternalInput").ap()
    kt_d = nc.dram_tensor("kt", [16, 512, 128], F32, kind="ExternalInput").ap()
    wif_d = nc.dram_tensor("wif", [D, DI], F32, kind="ExternalInput").ap()
    wof_d = nc.dram_tensor("wof", [DI, D], F32, kind="ExternalInput").ap()
    bin_d = nc.dram_tensor("binp", [128, NOC], F32, kind="ExternalInput").ap()
    bout_d = nc.dram_tensor("boutp", [128, NDC], F32, kind="ExternalInput").ap()
    sb_d = nc.dram_tensor("sbp", [128, NOC], F32, kind="ExternalInput").ap()
    ib_d = nc.dram_tensor("ibp", [128, NDC], F32, kind="ExternalInput").ap()
    sw_d = nc.dram_tensor("swp", [128, NOC * 3], F32, kind="ExternalInput").ap()
    iw_d = nc.dram_tensor("iwp", [128, NDC * 4], F32, kind="ExternalInput").ap()
    eye_d = nc.inline_tensor(np.eye(128, dtype=np.float32), "eye128").ap()
    comb16 = nc.dram_tensor("comb16sh", [NE, 6144], BF16, kind="Internal",
                            addr_space="Shared").ap()
    shf = nc.dram_tensor("shflag", [2, 128], I32, kind="Internal",
                         addr_space="Shared").ap()

    y_d = nc.dram_tensor("yout", [L, D], F32, kind="ExternalOutput").ap()
    ik = "ExternalOutput" if debug else "Internal"
    r1_d = nc.dram_tensor("r1d", [NDC, 128, L], F32, kind=ik).ap()
    x2_d = nc.dram_tensor("x2d", [NDC, 128, L], BF16, kind=ik).ap()
    if debug:
        idx_d = nc.dram_tensor("idxd", [128, 256], I32, kind="ExternalOutput").ap()
        gate_d = nc.dram_tensor("gated", [128, 256], F32, kind="ExternalOutput").ap()

    ctx = ExitStack()
    e = ctx.enter_context
    C = Ctr(nc)
    for s in ("qC", "qU", "qSPR", "qSPX", "qX", "qR", "qY", "qWA", "qWQ",
              "qWF", "qGI", "qGO", "qWC", "qPF", "sV", "sA", "sP", "sT"):
        C.sem(s)
    SY, V, A, P, T, G = nc.sync, nc.vector, nc.scalar, nc.gpsimd, nc.tensor, nc.gpsimd

    def vop(instr):
        return C.inc(instr, "sV")

    def vwait_self():
        C.wait(V, "V", "sV", C.cnt["sV"])

    ones16 = e(nc.sbuf_tensor("ones16", [128, 1], BF16))
    ones32 = e(nc.sbuf_tensor("ones32", [1, 128], F32))
    eyeF = e(nc.sbuf_tensor("eyeF", [128, 128], F32))
    eye16 = e(nc.sbuf_tensor("eye16", [128, 128], BF16))
    swp = e(nc.sbuf_tensor("swp_s", [128, NOC * 3], F32))
    iwp = e(nc.sbuf_tensor("iwp_s", [128, NDC * 4], F32))
    binp = e(nc.sbuf_tensor("binp_s", [128, NOC], F32))
    boutp = e(nc.sbuf_tensor("boutp_s", [128, NDC], F32))
    sbp = e(nc.sbuf_tensor("sbp_s", [128, NOC], F32))
    ibp = e(nc.sbuf_tensor("ibp_s", [128, NDC], F32))
    cst = e(nc.sbuf_tensor("cst_ln", [128, 2], F32))
    flz = e(nc.sbuf_tensor("flz", [1, 128], I32))
    flone = e(nc.sbuf_tensor("flone", [1, 128], I32))
    flrd = e(nc.sbuf_tensor("flrd", [1, 4], I32))
    idx_all = e(nc.sbuf_tensor("idx_all", [128, 256], I32))
    gate_all = e(nc.sbuf_tensor("gate_all", [128, 256], F32))

    ps_ctx = ExitStack()
    pe = ps_ctx.enter_context
    psA = pe(nc.psum_tensor("psA", [128, 1024], F32))
    psB = pe(nc.psum_tensor("psB", [128, 1024], F32))
    psL = pe(nc.psum_tensor("psL", [128, 1024], F32))
    psS = pe(nc.psum_tensor("psS", [128, 1024], F32))
    slots = [psA, psB]

    p1_ctx = ExitStack()
    p1 = p1_ctx.enter_context
    m0a = p1(nc.sbuf_tensor("m0a", [128, 64], F32))
    m1a = p1(nc.sbuf_tensor("m1a", [128, 64], F32))
    i0a = p1(nc.sbuf_tensor("i0a", [128, 64], U32))
    i1a = p1(nc.sbuf_tensor("i1a", [128, 64], U32))
    sc8 = p1(nc.sbuf_tensor("sc8", [128, 24], F32))
    rowb = p1(nc.sbuf_tensor("rowb", [1, 1024], F32))
    rowt = p1(nc.sbuf_tensor("rowt", [1, 512], F32))
    lnb = p1(nc.sbuf_tensor("lnb", [128, 1024], F32))
    wi16 = [p1(nc.sbuf_tensor(f"wi16_{k}", [128, D3], BF16)) for k in range(NDC)]
    wo16 = [p1(nc.sbuf_tensor(f"wo16_{k}", [128, D], BF16)) for k in range(NDC)]
    keff = [p1(nc.sbuf_tensor(f"keff_{k}", [128, 2048], BF16)) for k in range(NDC)]
    wqb = [p1(nc.sbuf_tensor(f"wqb_{k}", [128, D], BF16)) for k in range(4)]
    ktb = [p1(nc.sbuf_tensor(f"ktb_{k}", [128, 512], BF16)) for k in range(2)]
    dgA = [p1(nc.sbuf_tensor(f"dgA_{k}", [128, 128], BF16)) for k in range(12)]
    dgB = [p1(nc.sbuf_tensor(f"dgB_{k}", [128, 128], BF16)) for k in range(8)]
    ut = p1(nc.sbuf_tensor("ut", [128, 8 * 512], F32))
    u16sq = p1(nc.sbuf_tensor("u16sq", [128, 4096], BF16))
    xa = p1(nc.sbuf_tensor("xa", [128, 8 * 512], BF16))
    x2 = xa
    x1pair = p1(nc.sbuf_tensor("x1pair", [128, 1028], BF16))
    x1pairB = p1(nc.sbuf_tensor("x1pairB", [128, 1028], BF16))
    x1h = p1(nc.sbuf_tensor("x1h", [128, 48], BF16))
    convpair = p1(nc.sbuf_tensor("convpair", [128, 1024], F32))
    kvt = p1(nc.sbuf_tensor("kvt", [128, 515], BF16))
    kvtB = p1(nc.sbuf_tensor("kvtB", [128, 515], BF16))
    kvh = p1(nc.sbuf_tensor("kvh", [128, NDC * 3], BF16))
    vvt = p1(nc.sbuf_tensor("vvt", [128, 8 * 512], BF16))
    y16 = p1(nc.sbuf_tensor("y16", [128, 8 * 512], BF16))
    r1sb = p1(nc.sbuf_tensor("r1sb", [128, 8 * 512], F32))

    for dst, srcd in ((swp, sw_d), (iwp, iw_d), (eyeF, eye_d), (binp, bin_d),
                      (boutp, bout_d), (sbp, sb_d), (ibp, ib_d)):
        C.inc(SY.dma_start(out=dst[:, :], in_=srcd[:, :]), "qC", 16)
    qs_consts = C.cnt["qC"]

    for k in range(NDC):
        C.inc(G.dma_start(out=wi16[k][:, :], in_=wi_d[k * 128:(k + 1) * 128, :]), "qWA", 16)
    for k in range(NDC):
        C.inc(G.dma_start(out=wo16[k][:, :], in_=wo_d[k * 128:(k + 1) * 128, :]), "qWA", 16)
    qg_attnw = C.cnt["qWA"]
    C.inc(G.dma_start(out=ktb[0].ap().rearrange("p (c k) -> p c k", c=4),
                      in_=kt_d[0:1, :, :].rearrange("s (c p) k -> p (s c) k", p=128)), "qWQ", 16)
    for dkc in range(4):
        C.inc(G.dma_start(out=wqb[dkc][:, :],
                          in_=wq_d[dkc * 128:dkc * 128 + 128, :]), "qWQ", 16)

    C.wait(V, "V", "qC", qs_consts)
    vop(V.memset(cst[:, 0:1], EPS))
    vop(V.memset(cst[:, 1:2], -0.5))
    nc.const_aps.register(F32, EPS, cst.ap()[:, 0:1])
    nc.const_aps.register(F32, -0.5, cst.ap()[:, 1:2])
    vop(V.memset(ones16[:, :], 1.0))
    vop(V.memset(ones32[:, :], 1.0))
    vop(V.tensor_copy(eye16[:, :], eyeF[:, :]))
    vop(V.memset(idx_all[:, :], 0))
    vop(V.memset(gate_all[:, :], 0.0))
    vop(V.memset(x1h[:, :], 0.0))
    vop(V.memset(kvt[:, :], 0.0))
    vop(V.memset(kvtB[:, :], 0.0))
    vop(V.memset(kvh[:, :], 0.0))
    vop(V.memset(flz[:, :], 0))
    vop(V.memset(flone[:, :], 1))
    sv_p0 = C.cnt["sV"]

    pid1 = nc.gpsimd.alloc_register("pid1")
    pv = G.partition_id()
    G.reg_alu(pid1, pv, 1, ALU.bitwise_and)
    C.wait(G, "P", "sV", sv_p0)
    with G.If_eq(pid1, 0):
        G.dma_start(out=shf[0:1, :], in_=flz[0:1, :]).then_inc(C.sems["qWC"], 16)
    with G.Else():
        G.dma_start(out=shf[1:2, :], in_=flz[0:1, :]).then_inc(C.sems["qWC"], 16)
    C.cnt["qWC"] += 16
    qwc_zero = C.cnt["qWC"]

    NCV = 16

    def conv_chunk(cc):
        r0e = cc * 512
        r0o = 8192 + cc * 512
        with G.If_eq(pid1, 0):
            G.dma_start(out=comb16[r0e:r0e + 512, :],
                        in_=comb_d[r0e:r0e + 512, :]).then_inc(C.sems["qWC"], 16)
        with G.Else():
            G.dma_start(out=comb16[r0o:r0o + 512, :],
                        in_=comb_d[r0o:r0o + 512, :]).then_inc(C.sems["qWC"], 16)
        C.cnt["qWC"] += 16

    u_rearr = u_d.rearrange("(c p) t -> p c t", p=128)
    ut3 = ut.ap().rearrange("p (c t) -> p c t", c=NDC)
    C.inc(SY.dma_start(out=ut3, in_=u_rearr[:, :, 0:TILE]), "qU", 16)
    qs_u = {0: C.cnt["qU"]}

    gd = dict(
        st_stats_prev=0, sv_rows=0, sa_lnb=0, sv_apply=0,
        sv_argS=0, sv_argL=0,
        st_x2_read=0, st_xa_read=0, qs_x2dma=0, qs_r1dma=0,
        sv_haloout=0, sv_kvh=0,
        st_conv3=0, st_conv4=0, sp_y_last=0,
        sa_vv_last=0, sp_convpair=0, sp_kv=0,
        st_wout_last=0, sv_ln2_apply=0,
        sa_keff=0,
        si=0,
    )
    gd["dgA_mark"] = [0, 0]
    gd["dgA_par"] = 0
    gd["dgB_mark"] = [0, 0]
    gd["dgB_par"] = 0
    gd["slot_sa"] = [("sA", 0), ("sA", 0)]
    gd["st_conv3p"] = [0, 0]
    gd["st_conv4p"] = [0, 0]
    gd["st_stats_par"] = [0, 0]
    gd["kv_aux"] = {}
    gd["qu_piece"] = [qs_u[0]] * 4
    keff_done = [False]

    def next_slot():
        s = gd["si"] % 2
        gd["si"] += 1
        return s

    xa3 = xa.ap().rearrange("p (c t) -> p c t", c=NDC)
    tmp3 = u16sq.ap().rearrange("p (c t) -> p c t", c=NDC)
    x1v = x1pair.ap().rearrange("p (w c) -> p w c", w=2)
    x1vB = x1pairB.ap().rearrange("p (w c) -> p w c", w=2)
    x1vs = [x1v, x1vB]
    x1pairs = [x1pair, x1pairB]
    kvts = [kvt, kvtB]

    def ln_stats_piece(src, piece, s_st, first, last_):
        lo = piece * 1024
        par = piece % 2
        base = par * 2048
        u16 = u16sq.ap()[:, base:base + 1024]
        sq16 = u16sq.ap()[:, base + 1024:base + 2048]
        C.wait(V, "V", "sT", gd["st_stats_par"][par])
        vwait_self()
        vop(V.tensor_copy(u16, src[:, lo:lo + 1024]))
        vwait_self()
        vop(V.tensor_mul(sq16, u16, u16))
        sv_c = C.cnt["sV"]
        ps = psL if s_st is None else slots[s_st]
        C.wait(T, "T", "sV", sv_c)
        if first and s_st is not None:
            C.wait(T, "T", gd["slot_sa"][s_st][0], gd["slot_sa"][s_st][1])
        last = None
        for q2 in range(2):
            T.matmul(ps[0:1, 0:512], ones16[:, :], u16[:, q2 * 512:(q2 + 1) * 512],
                     start=(first and q2 == 0), stop=(last_ and q2 == 1))
            last = T.matmul(ps[0:1, 512:1024], ones16[:, :],
                            sq16[:, q2 * 512:(q2 + 1) * 512],
                            start=(first and q2 == 0), stop=(last_ and q2 == 1))
        st = C.inc(last, "sT")
        gd["st_stats_prev"] = st
        gd["st_stats_par"][par] = st
        return st

    def ln_rows_bcast(s_st, s_bc):
        ps = psL if s_st is None else slots[s_st]
        C.wait(V, "V", "sT", gd["st_stats_prev"])
        vwait_self()
        vop(V.tensor_scalar(rowb[0:1, 0:1024], ps[0:1, 0:1024], 1.0 / D, None, op0=ALU.mult))
        vwait_self()
        vop(V.tensor_mul(rowt[0:1, :], rowb[0:1, 0:512], rowb[0:1, 0:512]))
        vwait_self()
        vop(V.tensor_sub(rowt[0:1, :], rowb[0:1, 512:1024], rowt[0:1, :]))
        sv_rows = C.cnt["sV"]
        if s_st is not None:
            gd["slot_sa"][s_st] = ("sV", sv_rows)
        C.wait(A, "A", "sV", sv_rows)
        C.inc(A.activation(rowt[0:1, :], rowt[0:1, :], AF.Ln, bias=EPS), "sA")
        C.wait(A, "A", "sA", C.cnt["sA"])
        C.inc(A.activation(rowt[0:1, :], rowt[0:1, :], AF.Exp, scale=-0.5), "sA")
        sa_rstd = C.cnt["sA"]
        C.wait(V, "V", "sA", sa_rstd)
        vwait_self()
        vop(V.scalar_tensor_tensor(rowb[0:1, 512:1024], rowb[0:1, 0:512], -1.0,
                                   rowt[0:1, :], op0=ALU.mult, op1=ALU.mult))
        sv_rw = C.cnt["sV"]
        ps2 = slots[s_bc]
        C.wait(T, "T", "sV", sv_rw)
        C.wait(T, "T", gd["slot_sa"][s_bc][0], gd["slot_sa"][s_bc][1])
        T.matmul(ps2[:, 0:512], ones32[:, :], rowt[0:1, :], start=True, stop=True)
        st_b = C.inc(T.matmul(ps2[:, 512:1024], ones32[:, :], rowb[0:1, 512:1024],
                              start=True, stop=True), "sT")
        C.wait(A, "A", "sT", st_b)
        C.wait(A, "A", "sV", gd["sv_apply"])
        gd["sa_lnb"] = C.inc(A.copy(lnb[:, :], ps2[:, :]), "sA")
        gd["slot_sa"][s_bc] = ("sA", gd["sa_lnb"])
        return gd["sa_lnb"]

    def ln_apply(src_sb, extra_T_wait=0):
        src3 = src_sb.ap().rearrange("p (c t) -> p c t", c=NDC)
        rstd_b = lnb.ap()[:, 0:512].rearrange("p (o t) -> p o t", o=1).broadcast_to([128, 1, 512])
        mrstd_b = lnb.ap()[:, 512:1024].rearrange("p (o t) -> p o t", o=1).broadcast_to([128, 1, 512])
        C.wait(V, "V", "sA", gd["sa_lnb"])
        C.wait(V, "V", "sT", gd["st_stats_prev"])
        C.wait(V, "V", "sT", max(gd["st_x2_read"], gd["st_xa_read"], extra_T_wait))
        C.wait(V, "V", "qSPX", gd["qs_x2dma"])
        marks = []
        for dc in range(0, NDC, 2):
            vwait_self()
            vop(V.tensor_mul(tmp3[:, dc:dc + 2, :], src3[:, dc:dc + 2, :],
                             rstd_b.broadcast_to([128, 2, 512])))
            vwait_self()
            vop(V.tensor_tensor(xa3[:, dc:dc + 2, :], tmp3[:, dc:dc + 2, :],
                                mrstd_b.broadcast_to([128, 2, 512]), op=ALU.add))
            marks.extend([C.cnt["sV"]] * 2)
        gd["sv_apply"] = C.cnt["sV"]
        gd["sv_apply_dc"] = marks
        return gd["sv_apply"]

    def win_pair(ocA, ocB, sv_src, wtab, src_sb):
        s = next_slot()
        ps = slots[s]
        C.wait(T, "T", "qWA", qg_attnw)
        chase = isinstance(sv_src, list)
        if not chase:
            C.wait(T, "T", "sV", sv_src)
        C.wait(T, "T", gd["slot_sa"][s][0], gd["slot_sa"][s][1])
        last = None
        for k in range(NDC):
            if chase:
                C.wait(T, "T", "sV", sv_src[k])
            T.matmul(ps[:, 0:512], wtab[k][:, ocA * 128:(ocA + 1) * 128],
                     src_sb[:, k * 512:(k + 1) * 512],
                     start=(k == 0), stop=(k == NDC - 1))
            last = T.matmul(ps[:, 512:1024], wtab[k][:, ocB * 128:(ocB + 1) * 128],
                            src_sb[:, k * 512:(k + 1) * 512],
                            start=(k == 0), stop=(k == NDC - 1))
        return s, C.inc(last, "sT")

    def evac_to_x1(s, st_win, hslot, ebA, ebB, jpar):
        xv = x1vs[jpar]
        C.wait(A, "A", "sT", st_win)
        C.wait(A, "A", "sT", gd["st_conv3p"][jpar])
        C.wait(A, "A", "sV", gd["sv_haloout"])
        A.activation(xv[:, 0:1, 2:514], slots[s][:, 0:512], AF.Identity,
                     bias=binp[:, ebA:ebA + 1])
        sa = C.inc(A.activation(xv[:, 1:2, 2:514], slots[s][:, 512:1024], AF.Identity,
                                bias=binp[:, ebB:ebB + 1]), "sA")
        C.wait(V, "V", "sA", sa)
        C.wait(V, "V", "sT", gd["st_conv3p"][jpar])
        vwait_self()
        vop(V.tensor_copy(xv[:, :, 0:2],
                          x1h.ap()[:, hslot * 4:hslot * 4 + 4].rearrange(
                              "p (w c) -> p w c", w=2)))
        sv_hin = C.cnt["sV"]
        vop(V.tensor_copy(x1h.ap()[:, hslot * 4:hslot * 4 + 4].rearrange(
            "p (w c) -> p w c", w=2), xv[:, :, 512:514]))
        gd["sv_haloout"] = C.cnt["sV"]
        return sa, sv_hin

    def conv3_pair(s, ocA, ocB, sa_x1, sv_hin, jpar):
        par = gd["dgA_par"] = 1 - gd["dgA_par"]
        C.wait(V, "V", "sT", gd["dgA_mark"][par])
        vwait_self()
        for half, oc in ((0, ocA), (1, ocB)):
            for k in range(3):
                vop(V.tensor_scalar(dgA[par * 6 + half * 3 + k][:, :], eyeF[:, :],
                                    swp[:, oc * 3 + k:oc * 3 + k + 1], None, op0=ALU.mult))
        sv_dg = C.cnt["sV"]
        ps = slots[s]
        C.wait(T, "T", "sA", sa_x1)
        C.wait(T, "T", "sV", max(sv_dg, gd["sv_haloout"], sv_hin, sv_p0))
        last = None
        for half, oc in ((0, ocA), (1, ocB)):
            for k in range(3):
                last = T.matmul(ps[:, half * 512:(half + 1) * 512],
                                dgA[par * 6 + half * 3 + k][:, :],
                                x1pairs[jpar][:, half * 514 + k:half * 514 + k + 512],
                                start=(k == 0), stop=(k == 2))
        st = C.inc(last, "sT")
        gd["st_conv3p"][jpar] = st
        gd["dgA_mark"][par] = st
        return st

    def evac_to_convpair(s, st_c3, btab=None, ebA=0, ebB=0):
        C.wait(A, "A", "sT", st_c3)
        C.wait(A, "A", "sP", gd["sp_convpair"])
        if btab is None:
            sa = C.inc(A.copy(convpair[:, :], slots[s][:, :]), "sA")
        else:
            A.activation(convpair[:, 0:512], slots[s][:, 0:512], AF.Identity,
                         bias=btab[:, ebA:ebA + 1])
            sa = C.inc(A.activation(convpair[:, 512:1024], slots[s][:, 512:1024],
                                    AF.Identity, bias=btab[:, ebB:ebB + 1]), "sA")
        gd["slot_sa"][s] = ("sA", sa)
        return sa

    sv_ln1_pending = [None]

    for t in range(nt):
        t0 = t * TILE
        if t == 0:
            C.wait(V, "V", "qU", qs_u[0])
            s_st = next_slot()
            for piece in range(4):
                ln_stats_piece(ut.ap(), piece, s_st, piece == 0, piece == 3)
            s_bc = next_slot()
            ln_rows_bcast(s_st, s_bc)
            sv_ln1 = ln_apply(ut)
        else:
            sv_ln1 = sv_ln1_pending[0]

        ln1_dc = gd["sv_apply_dc"]
        jobs = [("kv", i) for i in range(NDC)] + [("q", j) for j in range(4)]

        def job_win(idx, job):
            kind, i = job
            if kind == "kv":
                ocA, ocB, hslot = 8 + i, 16 + i, i
            else:
                ocA, ocB, hslot = 2 * i, 2 * i + 1, 8 + i
            s, st_w = win_pair(ocA, ocB, ln1_dc if idx == 0 else sv_ln1, wi16, xa)
            gd["st_xa_read"] = st_w
            return [s, ocA, ocB, st_w, hslot, idx % 2]

        def job_evac(hd):
            s, ocA, ocB, st_w, hslot, jpar = hd
            sa_x1, sv_hin = evac_to_x1(s, st_w, hslot, ocA, ocB, jpar)
            hd[3] = (sa_x1, sv_hin)

        def job_tail1(job, hd):
            kind, i = job
            s, ocA, ocB, (sa_x1, sv_hin), hslot, jpar = hd
            st_c3 = conv3_pair(s, ocA, ocB, sa_x1, sv_hin, jpar)
            sa_cp = evac_to_convpair(s, st_c3, sbp, ocA, ocB)
            if kind == "kv":
                par = i % 2
                kv = kvts[par]
                C.wait(P, "P", "sA", sa_cp)
                C.wait(P, "P", "sT", gd["st_conv4p"][par])
                C.wait(P, "P", "sV", gd["sv_kvh"])
                sp_kv = C.inc(P.tensor_mul(kv[:, 3:515], convpair[:, 0:512],
                                           convpair[:, 512:1024]), "sP")
                gd["sp_convpair"] = sp_kv
                C.wait(V, "V", "sP", sp_kv)
                C.wait(V, "V", "sT", gd["st_conv4p"][par])
                vwait_self()
                vop(V.tensor_copy(kv[:, 0:3], kvh[:, i * 3:i * 3 + 3]))
                vop(V.tensor_copy(kvh[:, i * 3:i * 3 + 3], kv[:, 512:515]))
                gd["sv_kvh"] = C.cnt["sV"]
                dpar = gd["dgB_par"] = 1 - gd["dgB_par"]
                C.wait(V, "V", "sT", gd["dgB_mark"][dpar])
                vwait_self()
                for k in range(4):
                    vop(V.tensor_scalar(dgB[dpar * 4 + k][:, :], eyeF[:, :],
                                        iwp[:, i * 4 + k:i * 4 + k + 1], None,
                                        op0=ALU.mult))
                gd["kv_aux"][i] = (par, dpar, C.cnt["sV"], sp_kv)
            else:
                C.wait(P, "P", "sA", sa_cp)
                C.wait(P, "P", "sA", gd["sa_vv_last"])
                C.wait(P, "P", "sT", gd["st_wout_last"])
                P.tensor_mul(y16[:, ocA * 512:(ocA + 1) * 512], convpair[:, 0:512],
                             vvt[:, ocA * 512:(ocA + 1) * 512])
                spy = C.inc(P.tensor_mul(y16[:, ocB * 512:(ocB + 1) * 512],
                                         convpair[:, 512:1024],
                                         vvt[:, ocB * 512:(ocB + 1) * 512]), "sP")
                gd["sp_convpair"] = spy
                gd["sp_y_last"] = spy

        def job_tail2(job):
            kind, i = job
            par, dpar, sv_dg4, sp_kv = gd["kv_aux"][i]
            kv = kvts[par]
            C.wait(T, "T", "sP", sp_kv)
            C.wait(T, "T", "sV", max(sv_dg4, gd["sv_kvh"]))
            C.wait(T, "T", "sA", gd["sa_vv_last"])
            C.wait(T, "T", "sV", gd["sv_argS"])
            last = None
            for k in range(4):
                last = T.matmul(psS[:, 0:512], dgB[dpar * 4 + k][:, :], kv[:, k:k + 512],
                                start=(k == 0), stop=(k == 3))
            st_c4 = C.inc(last, "sT")
            gd["st_conv4p"][par] = st_c4
            gd["dgB_mark"][dpar] = st_c4
            C.wait(A, "A", "sT", st_c4)
            gd["sa_vv_last"] = C.inc(
                A.activation(vvt[:, i * 512:(i + 1) * 512], psS[:, 0:512],
                             AF.Identity, bias=ibp[:, i:i + 1]), "sA")

        heads = {}
        for idx, job in enumerate(jobs):
            heads[idx] = job_win(idx, job)
            if idx >= 1:
                job_tail1(jobs[idx - 1], heads[idx - 1])
            if idx >= 2 and jobs[idx - 2][0] == "kv":
                job_tail2(jobs[idx - 2])
            job_evac(heads[idx])
        job_tail1(jobs[-1], heads[len(jobs) - 1])
        sp_y = gd["sp_y_last"]

        sp_r1 = 0
        st_wo = 0
        for j in range(4):
            ocA, ocB = 2 * j, 2 * j + 1
            s = next_slot()
            ps = slots[s]
            C.wait(T, "T", "sP", sp_y)
            C.wait(T, "T", gd["slot_sa"][s][0], gd["slot_sa"][s][1])
            last = None
            for k in range(NDC):
                T.matmul(ps[:, 0:512], wo16[k][:, ocA * 128:(ocA + 1) * 128],
                         y16[:, k * 512:(k + 1) * 512], start=(k == 0), stop=(k == NDC - 1))
                last = T.matmul(ps[:, 512:1024], wo16[k][:, ocB * 128:(ocB + 1) * 128],
                                y16[:, k * 512:(k + 1) * 512], start=(k == 0), stop=(k == NDC - 1))
            st_wo = C.inc(last, "sT")
            C.wait(A, "A", "sT", st_wo)
            C.wait(A, "A", "sP", gd["sp_convpair"])
            A.activation(convpair[:, 0:512], ps[:, 0:512], AF.Identity,
                         bias=boutp[:, ocA:ocA + 1])
            sa_o = C.inc(A.activation(convpair[:, 512:1024], ps[:, 512:1024],
                                      AF.Identity, bias=boutp[:, ocB:ocB + 1]), "sA")
            gd["slot_sa"][s] = ("sA", sa_o)
            C.wait(P, "P", "sA", sa_o)
            C.wait(P, "P", "qSPR", gd["qs_r1dma"])
            C.wait(P, "P", "sV", gd["sv_ln2_apply"])
            sp_r1 = C.inc(P.tensor_add(r1sb[:, ocA * 512:(ocB + 1) * 512],
                                       convpair[:, :],
                                       ut[:, ocA * 512:(ocB + 1) * 512]), "sP")
            gd["sp_convpair"] = sp_r1
            C.wait(SY, "S", "sP", sp_r1)
            C.inc(SY.dma_start(
                out=r1_d[ocA:ocB + 1, :, t0:t0 + TILE].rearrange("c p t -> p c t"),
                in_=r1sb.ap()[:, ocA * 512:(ocB + 1) * 512].rearrange(
                    "p (c t) -> p c t", c=2)), "qSPR", 16)
            if t + 1 < nt:
                C.wait(SY, "S", "sP", sp_r1)
                C.inc(SY.dma_start(
                    out=ut3[:, 2 * j:2 * j + 2, :],
                    in_=u_rearr[:, 2 * j:2 * j + 2, (t + 1) * TILE:(t + 2) * TILE]),
                    "qU", 16)
                gd["qu_piece"][j] = C.cnt["qU"]
            if j > 0:
                C.wait(V, "V", "sP", sp_r1_prev)
                ln_stats_piece(r1sb.ap(), j - 1, None, j == 1, False)
            sp_r1_prev = sp_r1
        gd["st_wout_last"] = st_wo
        gd["qs_r1dma"] = C.cnt["qSPR"]
        sp_r1_done = sp_r1
        C.wait(V, "V", "sP", sp_r1_done)
        ln_stats_piece(r1sb.ap(), 3, None, False, True)


        if not keff_done[0]:
            keff_done[0] = True
            keff_mark = 0
            s = next_slot()
            ps = slots[s]
            first = True
            for ph in range(16):
                C.wait(T, "T", "qWQ", C.cnt["qWQ"])
                for dc in range(NDC):
                    if first:
                        C.wait(T, "T", gd["slot_sa"][s][0], gd["slot_sa"][s][1])
                    else:
                        C.wait(T, "T", "sA", gd["sa_keff"])
                    first = False
                    last = None
                    for dkc in range(4):
                        last = T.matmul(ps[:, 0:128],
                                        wqb[dkc][:, dc * 128:(dc + 1) * 128],
                                        ktb[ph % 2][:, dkc * 128:(dkc + 1) * 128],
                                        start=(dkc == 0), stop=(dkc == 3))
                    stk = C.inc(last, "sT")
                    if dc == NDC - 1:
                        keff_mark = stk
                    C.wait(A, "A", "sT", stk)
                    gd["sa_keff"] = C.inc(
                        A.copy(keff[dc][:, ph * 128:(ph + 1) * 128], ps[:, 0:128]), "sA")
                if ph + 1 < 16:
                    C.wait(G, "P", "sT", keff_mark)
                    for dkc in range(4):
                        r0 = (ph + 1) * 512 + dkc * 128
                        C.inc(G.dma_start(out=wqb[dkc][:, :],
                                          in_=wq_d[r0:r0 + 128, :]), "qWQ", 16)
                    C.inc(G.dma_start(
                        out=ktb[(ph + 1) % 2].ap().rearrange("p (c k) -> p c k", c=4),
                        in_=kt_d[ph + 1:ph + 2, :, :].rearrange(
                            "s (c p) k -> p (s c) k", p=128)), "qWQ", 16)
            gd["slot_sa"][s] = ("sA", gd["sa_keff"])

        s_bc2 = next_slot()
        ln_rows_bcast(None, s_bc2)
        sv_ln2 = ln_apply(r1sb)
        ln2_dc = gd["sv_apply_dc"]
        gd["sv_ln2_apply"] = sv_ln2
        for j in range(4):
            ocA = 2 * j
            C.wait(SY, "S", "sV", sv_ln2)
            C.inc(SY.dma_start(
                out=x2_d[ocA:ocA + 2, :, t0:t0 + TILE].rearrange("c p t -> p c t"),
                in_=x2.ap()[:, ocA * 512:(ocA + 2) * 512].rearrange(
                    "p (c t) -> p c t", c=2)), "qSPX", 16)
        gd["qs_x2dma"] = C.cnt["qSPX"]

        for gi in range(4):
            g = t * 4 + gi
            go = gi * 128
            st_sim = {}
            for p in range(2):
                dstp = psS if p == 0 else psL
                if not (gi == 0 and p == 0):
                    C.wait(T, "T", "sV", sv_ln2)
                if p == 0:
                    C.wait(T, "T", "sV", gd["sv_argS"])
                    C.wait(T, "T", "sA", gd["sa_vv_last"])
                else:
                    C.wait(T, "T", "sV", gd["sv_argL"])
                last = None
                for half in range(2):
                    for dc in range(NDC):
                        if gi == 0 and p == 0 and half == 0:
                            C.wait(T, "T", "sV", ln2_dc[dc])
                        last = T.matmul(dstp[:, half * 512:(half + 1) * 512],
                                        x2[:, dc * 512 + go:dc * 512 + go + 128],
                                        keff[dc][:, p * 1024 + half * 512:p * 1024 + (half + 1) * 512],
                                        start=(dc == 0), stop=(dc == NDC - 1))
                st_sim[p] = C.inc(last, "sT")
            gd["st_x2_read"] = st_sim[1]
            for p in range(2):
                dstp = psS if p == 0 else psL
                ma = m0a if p == 0 else m1a
                ia = i0a if p == 0 else i1a
                C.wait(V, "V", "sT", st_sim[p])
                vwait_self()
                for hh in range(H):
                    vop(V.max(ma[:, hh * 8:hh * 8 + 8], dstp[:, hh * 128:(hh + 1) * 128]))
                vwait_self()
                for hh in range(H):
                    vop(V.max_index(ia[:, hh * 8:hh * 8 + 8], ma[:, hh * 8:hh * 8 + 8],
                                    dstp[:, hh * 128:(hh + 1) * 128]))
                if p == 0:
                    gd["sv_argS"] = C.cnt["sV"]
                else:
                    gd["sv_argL"] = C.cnt["sV"]
            vwait_self()
            m0s = m0a.ap().rearrange("p (h e) -> p h e", e=8)[:, :, 0:1]
            m1s = m1a.ap().rearrange("p (h e) -> p h e", e=8)[:, :, 0:1]
            i0s = i0a.ap().rearrange("p (h e) -> p h e", e=8)[:, :, 0:1]
            i1s = i1a.ap().rearrange("p (h e) -> p h e", e=8)[:, :, 0:1]
            vop(V.tensor_add(sc8[:, 16:24], m0s, m1s))
            vwait_self()
            vop(V.tensor_scalar_max(gate_all[:, g * 8:(g + 1) * 8], sc8[:, 16:24], 0.0))
            vop(V.tensor_copy(sc8[:, 0:8], i0s))
            vop(V.tensor_copy(sc8[:, 8:16], i1s))
            vwait_self()
            vop(V.scalar_tensor_tensor(sc8[:, 16:24], sc8[:, 0:8], 128.0,
                                       sc8[:, 8:16], op0=ALU.mult, op1=ALU.add))
            vwait_self()
            vop(V.tensor_copy(idx_all[:, g * 8:(g + 1) * 8], sc8[:, 16:24]))

        if t + 1 < nt:
            s_st = next_slot()
            for piece in range(4):
                C.wait(V, "V", "qU", gd["qu_piece"][piece])
                ln_stats_piece(ut.ap(), piece, s_st, piece == 0, piece == 3)
            s_bc = next_slot()
            ln_rows_bcast(s_st, s_bc)
            sv_ln1_pending[0] = ln_apply(ut, extra_T_wait=gd["st_x2_read"])

        C.wait(G, "P", "qU", C.cnt["qU"])
        for cc in (2 * t, 2 * t + 1):
            if cc < NCV:
                conv_chunk(cc)

    for cc in range(2 * nt, NCV):
        conv_chunk(cc)

    C.wait(G, "P", "qWC", C.cnt["qWC"])
    with G.If_eq(pid1, 0):
        G.dma_start(out=shf[0:1, :], in_=flone[0:1, :]).then_inc(C.sems["qWC"], 16)
    with G.Else():
        G.dma_start(out=shf[1:2, :], in_=flone[0:1, :]).then_inc(C.sems["qWC"], 16)
    C.cnt["qWC"] += 16
    C.wait(G, "P", "qWC", C.cnt["qWC"])
    with nc.gpsimd.register("pcnt") as pcnt, nc.gpsimd.register("pcond") as pcond:
        G.reg_mov(pcnt, 0)
        G.reg_mov(pcond, 1)
        with G.While(pcond):
            G.reg_add(pcnt, pcnt, 16)
            with G.If_eq(pid1, 0):
                G.dma_start(out=flrd[0:1, 0:1], in_=shf[1:2, 0:1]).then_inc(C.sems["qPF"], 16)
            with G.Else():
                G.dma_start(out=flrd[0:1, 0:1], in_=shf[0:1, 0:1]).then_inc(C.sems["qPF"], 16)
            G.wait_ge(C.sems["qPF"], pcnt)
            G.reg_load(pcond, flrd[0:1, 0:1])
            G.reg_alu(pcond, pcond, 1, ALU.bitwise_xor)

    endT, endV, endA, endP = C.cnt["sT"], C.cnt["sV"], C.cnt["sA"], C.cnt["sP"]

    p1_ctx.close()
    ps_ctx.close()

    b2_ctx = ExitStack()
    b2 = b2_ctx.enter_context
    wif16 = [b2(nc.sbuf_tensor(f"wif16_{k}", [128, DI], BF16)) for k in range(NDC)]
    wof16 = [b2(nc.sbuf_tensor(f"wof16_{k}", [128, D], BF16)) for k in range(NIC)]
    ainb0 = [b2(nc.sbuf_tensor(f"ainb_{h}", [128, 1024], BF16)) for h in range(H)]
    binb0 = [b2(nc.sbuf_tensor(f"binb_{h}", [128, 2048], BF16)) for h in range(H)]
    aoutb0 = [b2(nc.sbuf_tensor(f"aoutb_{h}", [128, 2048], BF16)) for h in range(H)]
    boutb0 = [b2(nc.sbuf_tensor(f"boutb_{h}", [128, 1024], BF16)) for h in range(H)]
    ainb = [ainb0, ainb0]
    binb = [binb0, binb0]
    aoutb = [aoutb0, aoutb0]
    boutb = [boutb0, boutb0]
    x2g2 = [b2(nc.sbuf_tensor(f"x2g_{k}", [128, 1024], BF16)) for k in range(2)]
    r1g = b2(nc.sbuf_tensor("r1g", [128, 1024], F32))
    xtok = b2(nc.sbuf_tensor("xtok", [128, 1024], BF16))
    htok = b2(nc.sbuf_tensor("htok", [128, 2048], BF16))
    hfm = b2(nc.sbuf_tensor("hfm", [128, 2048], BF16))
    pj4 = b2(nc.sbuf_tensor("pj4", [128, 4096], BF16))
    pjin = b2(nc.sbuf_tensor("pjin", [128, 1024], BF16))
    pjout2 = b2(nc.sbuf_tensor("pjout2", [128, 2048], BF16))
    linr = b2(nc.sbuf_tensor("linr", [128, 16], F32))
    ling = b2(nc.sbuf_tensor("ling", [128, 16], F32))
    dgb = [b2(nc.sbuf_tensor(f"dgb_{h}", [128, 128], BF16)) for h in range(2 * H)]
    outg = b2(nc.sbuf_tensor("outg", [128, 1024], F32))

    psH = b2(nc.psum_tensor("psH", [128, 2048], F32))
    psO = b2(nc.psum_tensor("psO", [128, 1024], F32))
    psT2 = b2(nc.psum_tensor("psT2", [128, 512], F32))
    psT3 = b2(nc.psum_tensor("psT3", [128, 512], F32))

    C.wait(G, "P", "sT", endT)
    C.wait(G, "P", "sV", endV)
    C.wait(G, "P", "sA", endA)
    C.wait(G, "P", "sP", endP)
    for k in range(NDC):
        C.inc(G.dma_start(out=wif16[k][:, :], in_=wif_d[k * 128:(k + 1) * 128, :]), "qWF", 16)
    for k in range(NIC):
        C.inc(G.dma_start(out=wof16[k][:, :], in_=wof_d[k * 128:(k + 1) * 128, :]), "qWF", 16)
    qg_ffnw = C.cnt["qWF"]
    C.wait(SY, "S", "sT", endT)
    C.wait(SY, "S", "sA", endA)
    C.wait(SY, "S", "sV", endV)
    C.wait(SY, "S", "sP", endP)
    C.wait(T, "T", "sV", endV)
    C.wait(T, "T", "sA", endA)
    C.wait(A, "A", "sT", endT)
    C.wait(A, "A", "sV", endV)
    C.wait(V, "V", "sT", endT)
    C.wait(V, "V", "sA", endA)
    C.wait(P, "P", "sT", endT)

    bg = dict(st_bin=0, st_bout=0, sa_gelu=0, sa_out=0, qs_y=0,
              sa_xtok=0, sa_hfm=[0, 0, 0, 0], sv_amul_in=0, sv_amul_out=0,
              sv_ling_in=0, sv_ling_out=0, sa_red_in=[0] * H, sa_red_out=[0] * H)

    GOFF = {0: (0, 1024, ainb, "qAI"), 1: (1024, 2048, binb, "qBI"),
            2: (3072, 2048, aoutb, "qAO"), 3: (5120, 1024, boutb, "qBO")}
    for qn in ("qAI", "qBI", "qAO", "qBO"):
        C.sem(qn)

    def emit_gathers(g, which, gate_sem, gate_val):
        off, width, bufs, qn = GOFF[which]
        C.wait(G, "P", gate_sem, gate_val)
        for h in range(H):
            C.inc(G.indirect_dma_start(
                bufs[g % 2][h][:, 0:width], None,
                comb16[:, :],
                IndirectOffsetOnAxis(idx_all.ap()[:, g * 8 + h:g * 8 + h + 1], 0),
                element_offset=off), qn, 16)
        return C.cnt[qn]

    qai = {}
    qbi = {}
    qao = {}
    qbo = {}
    qai[0] = qai[1] = emit_gathers(0, 0, "sT", 0)
    qbi[0] = qbi[1] = emit_gathers(0, 1, "sT", 0)
    qao[0] = qao[1] = emit_gathers(0, 2, "sT", 0)
    qbo[0] = qbo[1] = emit_gathers(0, 3, "sT", 0)
    x2g3 = [x2g2[k].ap().rearrange("p (c t) -> p c t", c=NDC) for k in range(2)]
    r1g3 = r1g.ap().rearrange("p (c t) -> p c t", c=NDC)
    C.inc(SY.dma_start(out=x2g3[0], in_=x2_d[:, :, 0:128].rearrange("c p t -> p c t")),
          "qX", 16)
    qx = C.cnt["qX"]
    C.inc(SY.dma_start(out=r1g3, in_=r1_d[:, :, 0:128].rearrange("c p t -> p c t")),
          "qR", 16)
    qr = C.cnt["qR"]

    for g in range(NG):
        par = g % 2
        x2g = x2g2[par]
        C.wait(T, "T", "qX", qx)
        for half in range(2):
            pst = psT2 if half == 0 else psT3
            C.wait(T, "T", "sA", max(bg["sa_xtok"], bg["sa_hfm"][half + 2]))
            last = None
            for c in range(4):
                cc = half * 4 + c
                last = T.matmul(pst[:, c * 128:(c + 1) * 128],
                                x2g[:, cc * 128:(cc + 1) * 128], eye16[:, :],
                                start=True, stop=True)
            st = C.inc(last, "sT")
            C.wait(A, "A", "sT", st)
            C.wait(A, "A", "sV", bg["sv_amul_in"])
            bg["sa_xtok"] = C.inc(
                A.copy(xtok[:, half * 512:(half + 1) * 512], pst[:, :]), "sA")

        C.wait(V, "V", "qAI", qai[par])
        C.wait(V, "V", "sA", bg["sa_xtok"])
        for h in range(H):
            vwait_self()
            vop(V.scalar_tensor_tensor(pjin[:, :], ainb[par][h][:, :], 1.0,
                                       xtok[:, :], op0=ALU.mult, op1=ALU.mult,
                                       accum_out=linr[:, h:h + 1]))
        bg["sv_amul_in"] = C.cnt["sV"]
        if g + 1 < NG:
            qai[0] = qai[1] = emit_gathers(g + 1, 0, "sV", bg["sv_amul_in"])
        vwait_self()
        vop(V.tensor_mul(ling[:, 0:8], linr[:, 0:8], gate_all[:, g * 8:(g + 1) * 8]))
        bg["sv_ling_in"] = C.cnt["sV"]
        vwait_self()
        for h in range(H):
            C.wait(V, "V", "sT", bg["st_bin"])
            vop(V.tensor_scalar(dgb[h][:, :], eyeF[:, :], ling[:, h:h + 1], None,
                                op0=ALU.mult))
        sv_dgin = C.cnt["sV"]

        C.wait(T, "T", "qWF", qg_ffnw)
        C.wait(T, "T", "qBI", qbi[par])
        C.wait(T, "T", "sA", bg["sa_gelu"])
        n_contrib = NDC + H
        ci = 0
        last1 = None
        for dc in range(NDC):
            lhs = x2g[:, dc * 128:(dc + 1) * 128]
            for qtr in range(4):
                last1 = T.matmul(psH[:, qtr * 512:(qtr + 1) * 512], lhs,
                                 wif16[dc][:, qtr * 512:(qtr + 1) * 512],
                                 start=(ci == 0), stop=(ci == n_contrib - 1))
            ci += 1
        C.wait(T, "T", "sV", sv_dgin)
        for h in range(H):
            for qtr in range(4):
                last1 = T.matmul(psH[:, qtr * 512:(qtr + 1) * 512], dgb[h][:, :],
                                 binb[par][h][:, qtr * 512:(qtr + 1) * 512],
                                 start=(ci == 0), stop=(ci == n_contrib - 1))
            ci += 1
        bg["st_bin"] = C.inc(last1, "sT")
        if g + 1 < NG:
            qbi[0] = qbi[1] = emit_gathers(g + 1, 1, "sT", bg["st_bin"])
            gsl = slice((g + 1) * 128, (g + 2) * 128)
            C.wait(SY, "S", "sT", bg["st_bin_prev"] if "st_bin_prev" in bg else 0)
            C.inc(SY.dma_start(out=x2g3[(g + 1) % 2],
                               in_=x2_d[:, :, gsl].rearrange("c p t -> p c t")), "qX", 16)
            qx = C.cnt["qX"]
        bg["st_bin_prev"] = bg["st_bin"]

        C.wait(A, "A", "sT", bg["st_bin"])
        C.wait(A, "A", "sV", bg["sv_amul_out"])
        bg["sa_gelu"] = C.inc(A.activation(htok[:, :], psH[:, :], AF.Gelu), "sA")

        C.wait(T, "T", "sA", bg["sa_gelu"])
        for half in range(4):
            pst = psT2 if half % 2 == 0 else psT3
            C.wait(T, "T", "sA",
                   bg["sa_xtok"] if half < 2 else bg["sa_hfm"][half - 2])
            last = None
            for c in range(4):
                cc = half * 4 + c
                last = T.matmul(pst[:, c * 128:(c + 1) * 128],
                                htok[:, cc * 128:(cc + 1) * 128], eye16[:, :],
                                start=True, stop=True)
            st = C.inc(last, "sT")
            C.wait(A, "A", "sT", st)
            C.wait(A, "A", "sT", bg["st_bout"])
            bg["sa_hfm"][half] = C.inc(
                A.copy(hfm[:, half * 512:(half + 1) * 512], pst[:, :]), "sA")

        C.wait(V, "V", "qAO", qao[par])
        C.wait(V, "V", "sA", bg["sa_gelu"])
        for h in range(4):
            hp = (h % 2) * 2048
            C.wait(V, "V", "sA", bg["sa_red_out"][h % 2])
            vwait_self()
            vop(V.tensor_mul(pj4[:, hp:hp + 2048], aoutb[par][4 + h][:, :], htok[:, :]))
            sv_m = C.cnt["sV"]
            C.wait(A, "A", "sV", sv_m)
            if h == 0:
                C.wait(A, "A", "sV", bg["sv_ling_out"])
            bg["sa_red_out"][h % 2] = C.inc(
                A.activation(pj4[:, hp:hp + 2048], pj4[:, hp:hp + 2048], AF.Copy,
                             accum_out=linr[:, 12 + h:13 + h]), "sA")
        for h in range(4):
            vwait_self()
            vop(V.scalar_tensor_tensor(pjout2[:, :], aoutb[par][h][:, :], 1.0,
                                       htok[:, :], op0=ALU.mult, op1=ALU.mult,
                                       accum_out=linr[:, 8 + h:9 + h]))
        C.wait(V, "V", "sA", bg["sa_red_out"][0])
        C.wait(V, "V", "sA", bg["sa_red_out"][1])
        bg["sv_amul_out"] = C.cnt["sV"]
        if g + 1 < NG:
            qao[0] = qao[1] = emit_gathers(g + 1, 2, "sV", bg["sv_amul_out"])
        vwait_self()
        vop(V.tensor_mul(ling[:, 8:16], linr[:, 8:16], gate_all[:, g * 8:(g + 1) * 8]))
        bg["sv_ling_out"] = C.cnt["sV"]
        vwait_self()
        for h in range(H):
            C.wait(V, "V", "sT", bg["st_bout"])
            vop(V.tensor_scalar(dgb[8 + h][:, :], eyeF[:, :], ling[:, 8 + h:9 + h], None,
                                op0=ALU.mult))
        sv_dgout = C.cnt["sV"]

        C.wait(T, "T", "qBO", qbo[par])
        C.wait(T, "T", "qR", qr)
        C.wait(T, "T", "sA", bg["sa_out"])
        last = None
        for c in range(NIC):
            for hf in range(2):
                last = T.matmul(psO[:, hf * 512:(hf + 1) * 512],
                                hfm[:, c * 128:(c + 1) * 128],
                                wof16[c][:, hf * 512:(hf + 1) * 512],
                                start=(c == 0), stop=False)
        C.wait(T, "T", "sV", sv_dgout)
        for h in range(H):
            for hf in range(2):
                last = T.matmul(psO[:, hf * 512:(hf + 1) * 512], dgb[8 + h][:, :],
                                boutb[par][h][:, hf * 512:(hf + 1) * 512],
                                start=False, stop=False)
        for dc in range(NDC):
            last = T.matmul(psO[:, dc * 128:(dc + 1) * 128],
                            r1g[:, dc * 128:(dc + 1) * 128], eyeF[:, :],
                            start=False, stop=(dc == 3 or dc == NDC - 1))
        bg["st_bout"] = C.inc(last, "sT")
        if g + 1 < NG:
            qbo[0] = qbo[1] = emit_gathers(g + 1, 3, "sT", bg["st_bout"])
            gsl = slice((g + 1) * 128, (g + 2) * 128)
            C.wait(SY, "S", "sT", bg["st_bout"])
            C.inc(SY.dma_start(out=r1g3,
                               in_=r1_d[:, :, gsl].rearrange("c p t -> p c t")), "qR", 16)
            qr = C.cnt["qR"]

        C.wait(A, "A", "sT", bg["st_bout"])
        C.wait(A, "A", "qY", bg["qs_y"])
        bg["sa_out"] = C.inc(A.copy(outg[:, :], psO[:, :]), "sA")
        C.wait(SY, "S", "sA", bg["sa_out"])
        C.inc(SY.dma_start(out=y_d[g * 128:(g + 1) * 128, :], in_=outg[:, :]), "qY", 16)
        bg["qs_y"] = C.cnt["qY"]

    if debug:
        C.wait(SY, "S", "sV", C.cnt["sV"])
        C.inc(SY.dma_start(out=idx_d[:, :], in_=idx_all[:, :]), "qY", 16)
        C.inc(SY.dma_start(out=gate_d[:, :], in_=gate_all[:, :]), "qY", 16)

    for s in ("qC", "qU", "qSPR", "qSPX", "qX", "qR", "qY"):
        C.wait(SY, "S", s, C.cnt[s])
    for s in ("qWA", "qWQ", "qWF", "qAI", "qBI", "qAO", "qBO", "qWC"):
        C.wait(G, "P", s, C.cnt[s])

    b2_ctx.close()
    ctx.close()
    return nc


def _host_prep(inputs):
    u = inputs["L0_attn_u"]
    comb = np.concatenate([inputs["L0_ffn_in_a"], inputs["L0_ffn_in_b"],
                           inputs["L0_ffn_out_a"], inputs["L0_ffn_out_b"]], axis=1)
    comb = np.ascontiguousarray(comb)
    keys = inputs["L0_ffn_keys"]
    kt = np.ascontiguousarray(keys.transpose(2, 0, 3, 1).reshape(16, 512, 128))
    base = {
        "combt": comb,
        "wi": np.ascontiguousarray(inputs["L0_attn_W_in"]),
        "wo": np.ascontiguousarray(inputs["L0_attn_W_out"]),
        "wq": np.ascontiguousarray(inputs["L0_ffn_Wq"]),
        "kt": kt,
        "wif": np.ascontiguousarray(inputs["L0_ffn_W_in"].T),
        "wof": np.ascontiguousarray(inputs["L0_ffn_W_out"].T),
        "binp": np.ascontiguousarray(inputs["L0_attn_b_in"].reshape(NOC, 128).T),
        "boutp": np.ascontiguousarray(inputs["L0_attn_b_out"].reshape(NDC, 128).T),
        "sbp": np.ascontiguousarray(inputs["L0_attn_short_b"].reshape(NOC, 128).T),
        "ibp": np.ascontiguousarray(inputs["L0_attn_inner_b"].reshape(NDC, 128).T),
        "swp": np.ascontiguousarray(
            inputs["L0_attn_short_w"].reshape(NOC, 128, 3).transpose(1, 0, 2).reshape(128, NOC * 3)),
        "iwp": np.ascontiguousarray(
            inputs["L0_attn_inner_w"].reshape(NDC, 128, 4).transpose(1, 0, 2).reshape(128, NDC * 4)),
    }
    u_fm = np.ascontiguousarray(u.transpose(0, 2, 1))
    return base, u_fm


def kernel(**inputs):
    base, u_fm = _host_prep(inputs)
    nc = build_nc(nt=8, debug=False)
    in_maps = [dict(base, u=u_fm[c]) for c in range(8)]
    res = run_bass_kernel_spmd(nc, in_maps, core_ids=list(range(8)))
    out = np.stack([np.asarray(r["yout"]) for r in res.results], axis=0)
    return out.astype(np.float32, copy=False)
